# version 16
# baseline (speedup 1.0000x reference)
import numpy as np
import ml_dtypes
import concourse.bass as bass
import concourse.mybir as mybir
from concourse.bass_utils import run_bass_kernel_spmd

F32 = mybir.dt.float32
BF16 = mybir.dt.bfloat16
I32 = mybir.dt.int32
AF = mybir.ActivationFunctionType
ALU = mybir.AluOpType
AX = mybir.AxisListType

MAGIC = 12582912.0

S = 2048
D = 2048
II = 8192
HD = 128
NQH = 16
NKV = 4
NCH = S // 128
GRP = 2
NGRP = NCH // GRP
GT = GRP * 128
LN_EPS = 1e-5
RMS_EPS = 1e-6
ATT_EPS = 1e-6
Q_EPS = 1e-5
W_EPS = 1e-5

ENGINES = ("sync", "vector", "scalar", "tensor", "gpsimd")

DEBUG_SINK = None


class Step:
    __slots__ = ("eng", "emit", "reads", "writes", "dma", "idx",
                 "needs_inc", "count", "waits", "tag", "slot", "gen")

    def __init__(self, eng, emit, reads, writes, dma, idx, tag):
        self.eng = eng
        self.emit = emit
        self.reads = reads
        self.writes = writes
        self.dma = dma
        self.idx = idx
        self.needs_inc = False
        self.count = None
        self.waits = []
        self.tag = tag


class Sched:

    def __init__(self):
        self.steps = []
        self.last_writer = {}
        self.readers = {}

    def do(self, eng, emit, reads=(), writes=(), dma=None, tag=""):
        st = Step(eng, emit, list(reads), list(writes), dma, len(self.steps), tag)
        deps = []
        for b in st.reads:
            w = self.last_writer.get(id(b))
            if w is not None:
                deps.append(w)
        for b in st.writes:
            w = self.last_writer.get(id(b))
            if w is not None:
                deps.append(w)
            deps.extend(self.readers.get(id(b), ()))
        seen = set()
        for d in deps:
            if d.idx in seen or d is st:
                continue
            seen.add(d.idx)
            if (d.dma is None and st.dma is None and d.eng == st.eng
                    and d.eng == "tensor"):
                continue
            st.waits.append(d)
        for b in st.reads:
            self.readers.setdefault(id(b), []).append(st)
        for b in st.writes:
            self.last_writer[id(b)] = st
            self.readers[id(b)] = []
        self.steps.append(st)
        return st

    DMA_SLOTS = 16

    def _semkey(self, st):
        if st.dma is not None:
            return ("dma", st.dma, st.slot)
        return ("eng", st.eng)

    def resolve_and_emit(self, nc):
        P = self.DMA_SLOTS
        qidx = {}
        for st in self.steps:
            if st.dma is not None:
                i = qidx.get(st.dma, 0)
                qidx[st.dma] = i + 1
                st.slot = i % P
                st.gen = i // P
                st.needs_inc = True
                st.count = 16 * (st.gen + 1)
            for d in st.waits:
                d.needs_inc = True
        cnt = {}
        for st in self.steps:
            if st.needs_inc and st.dma is None:
                key = self._semkey(st)
                cnt[key] = cnt.get(key, 0) + 1
                st.count = cnt[key]
        from contextlib import ExitStack
        self._semstack = ExitStack()
        sems = {}
        for st in self.steps:
            if st.needs_inc:
                key = self._semkey(st)
                if key not in sems:
                    sems[key] = self._semstack.enter_context(
                        nc.semaphore("sem_" + "_".join(str(k) for k in key[1:])))
        hw = {e: {} for e in ENGINES}
        per_eng = {e: [] for e in ENGINES}
        for st in self.steps:
            per_eng[st.eng].append(st)

        with nc.Block() as block:
            def make_body(eng_name):
                def body(eng):
                    marks = hw[eng_name]
                    for st in per_eng[eng_name]:
                        if st.dma is not None and st.gen > 0:
                            key = self._semkey(st)
                            v = 16 * st.gen
                            if marks.get(key, 0) < v:
                                eng.wait_ge(sems[key], v)
                                marks[key] = v
                        for d in st.waits:
                            key = self._semkey(d)
                            v = d.count
                            if marks.get(key, 0) >= v:
                                continue
                            eng.wait_ge(sems[key], v)
                            marks[key] = v
                        inst = st.emit(eng)
                        if st.needs_inc:
                            key = self._semkey(st)
                            inst.then_inc(sems[key], 16 if st.dma else 1)
                return body
            block.sync(make_body("sync"))
            block.vector(make_body("vector"))
            block.scalar(make_body("scalar"))
            block.tensor(make_body("tensor"))
            block.gpsimd(make_body("gpsimd"))
        return sems


def build_program():
    nc = bass.Bass("TRN2", target_bir_lowering=False, debug=False)
    sch = Sched()
    from contextlib import ExitStack
    stack = ExitStack()

    def sb(name, shape, dtype):
        return stack.enter_context(nc.sbuf_tensor(name, shape, dtype))

    def psu(name, shape, dtype=F32):
        return stack.enter_context(nc.psum_tensor(name, shape, dtype))

    x_h = nc.dram_tensor("x", [S, D], F32, kind="ExternalInput").ap()
    wq_h = nc.dram_tensor("wq", [D, D], F32, kind="ExternalInput").ap()
    wk_h = nc.dram_tensor("wk", [512, D], F32, kind="ExternalInput").ap()
    wv_h = nc.dram_tensor("wv", [512, D], F32, kind="ExternalInput").ap()
    wo_h = nc.dram_tensor("wo", [D, D], F32, kind="ExternalInput").ap()
    wg_h = nc.dram_tensor("wg", [II, D], F32, kind="ExternalInput").ap()
    wu_h = nc.dram_tensor("wu", [II, D], F32, kind="ExternalInput").ap()
    wd_h = nc.dram_tensor("wd", [D, II], F32, kind="ExternalInput").ap()
    lnw_h = nc.dram_tensor("lnw", [II], F32, kind="ExternalInput").ap()
    cos_h = nc.dram_tensor("costab", [S, HD], F32, kind="ExternalInput").ap()
    sin_h = nc.dram_tensor("sintab", [S, HD], F32, kind="ExternalInput").ap()
    mask_h = nc.dram_tensor("masktab", [128, 128], BF16, kind="ExternalInput").ap()
    idbf_h = nc.dram_tensor("idbf", [128, 128], BF16, kind="ExternalInput").ap()
    idf32_h = nc.dram_tensor("idf32", [128, 128], F32, kind="ExternalInput").ap()
    out_h = nc.dram_tensor("out", [S, D], F32, kind="ExternalOutput").ap()

    ws_q = nc.dram_tensor("ws_q", [4, 2, 512, D], BF16, kind="Internal").ap()
    ws_k = nc.dram_tensor("ws_k", [1, 2, 512, D], BF16, kind="Internal").ap()
    ws_v = nc.dram_tensor("ws_v", [1, 2, 512, D], BF16, kind="Internal").ap()
    ws_o = nc.dram_tensor("ws_o", [4, 2, 512, D], BF16, kind="Internal").ap()
    ws_gu = nc.dram_tensor("ws_gu", [32, 2, 256, D], BF16, kind="Internal").ap()
    ws_d = nc.dram_tensor("ws_d", [D, II], BF16, kind="Internal").ap()
    pt_h = nc.dram_tensor("pt", [II, S], F32, kind="Internal").ap()
    att_h = nc.dram_tensor("att_sp", [S, D], F32, kind="Internal").ap()
    rowscr_h = nc.dram_tensor("rowscr", [4, S], F32, kind="Internal").ap()

    mask_sb = sb("mask_sb", [128, 128], BF16)
    idbf_sb = sb("idbf_sb", [128, 128], BF16)
    idf32_sb = sb("idf32_sb", [128, 128], F32)
    lnw_tmp = sb("lnw_tmp", [64, 128], F32)
    lnw_sb = sb("lnw_sb", [128, 64], F32)
    alnw_sb = sb("alnw_sb", [128, 64], F32)
    ones_1x128_f = sb("ones_1x128_f", [1, 128], F32)
    nxT = sb("nxT", [128, NCH * S], BF16)
    nhT = nxT
    sxinv_col = sb("sxinv_col", [128, NCH], F32)
    Bsx = sb("Bsx", [128, S], F32)
    sd_col = sb("sd_col", [128, NCH], F32)
    thr = {}
    for nm in ("g", "u", "d"):
        thr[nm] = dict(
            racc=sb(f"racc_{nm}", [128, 1], F32),
            a=sb(f"a_{nm}", [1, 1], F32),
            abc=sb(f"abc_{nm}", [128, 1], F32),
            anbc=sb(f"anbc_{nm}", [128, 1], F32),
        )
    st32 = [sb(f"st32_{g}", [128, 132], F32) for g in range(NKV)]
    sth = [sb(f"sth_{g}", [128, 132], BF16) for g in range(NKV)]
    stl = [sb(f"stl_{g}", [128, 132], BF16) for g in range(NKV)]

    p0 = sb("p0", [128, D], F32)
    p1 = sb("p1", [128, D], F32)
    xb0 = sb("xb0", [128, D], F32)
    fwbuf = [xb0, p0]
    nxbf = sb("nxbf", [128, D], BF16)
    ternbf = nxbf
    hsp_bf = sb("hsp_bf", [128, D], BF16)
    lsp_bf = sb("lsp_bf", [128, D], BF16)
    stat = [sb(f"stat{i}", [128, 1], F32) for i in range(8)]
    row0 = sb("row0", [1, 512], F32)
    ep = [sb(f"ep{i}", [128, 512], F32) for i in range(5)]
    ep32 = sb("epi32", [128, 512], I32)
    qro = sb("qro", [128, 512], F32)
    ssqsl = sb("ssqsl", [128, 512], F32)
    maxsl = sb("maxsl", [128, 512], F32)
    cos_sb = sb("cos_sb", [128, 128], F32)
    pb = [sb(f"pb{i}", [128, 512], F32) for i in range(2)]
    pbg = sb("pbg", [128, 512], F32)
    pbh = sb("pbh", [128, 512], BF16)
    pbl = sb("pbl", [128, 512], BF16)
    pbt = sb("pbt", [128, 512], BF16)
    sin_sb = sb("sin_sb", [128, 128], F32)

    xnT = [sb(f"xnT_{p}", [128, NCH * GT], BF16) for p in "hl"]
    attT = xnT
    qfT = [sb(f"qfT_{p}", [128, NQH * GT], BF16) for p in "hl"]
    kfT = [sb(f"kfT_{p}", [128, NKV * GT], BF16) for p in "hl"]
    kftok = [sb(f"kftok_{p}", [128, GRP * 512], BF16) for p in "hl"]
    vaug = [sb(f"vaug_{p}", [128, GRP * NKV * 132], BF16) for p in "hl"]
    amT = [[sb(f"amT_{p}{i}", [128, 128], BF16) for p in "hl"] for i in range(2)]
    am32 = [sb(f"am32_{i}", [128, 128], F32) for i in range(2)]

    wth = [sb(f"wth{i}", [128, 1024], BF16) for i in range(8)]

    ps_acc = [psu(f"ps_acc{i}", [128, 512], F32) for i in range(4)]
    ps_tr = [psu(f"ps_tr{i}", [128, 128], BF16) for i in range(2)]
    ps_af = psu("ps_af", [128, 128], F32)
    ps_af2 = psu("ps_af2", [128, 128], F32)

    SC = sch.do

    SC("sync", lambda e: e.dma_start(out=mask_sb[:], in_=mask_h), writes=[mask_sb], dma="ld")
    SC("sync", lambda e: e.dma_start(out=idbf_sb[:], in_=idbf_h), writes=[idbf_sb], dma="ld")
    SC("sync", lambda e: e.dma_start(out=idf32_sb[:], in_=idf32_h), writes=[idf32_sb], dma="ld")
    SC("sync", lambda e: e.dma_start(out=lnw_tmp[:], in_=lnw_h.rearrange("(a b) -> a b", b=128)),
       writes=[lnw_tmp], dma="ld")
    SC("gpsimd", lambda e: e.memset(ones_1x128_f[:], 1.0), writes=[ones_1x128_f])
    for g in range(NKV):
        SC("gpsimd", lambda e, g=g: e.memset(st32[g][:], 0.0), writes=[st32[g]])
    SC("tensor", lambda e: e.transpose(ps_af[:, 0:64], lnw_tmp[:], idf32_sb[0:64, 0:64]),
       reads=[lnw_tmp, idf32_sb], writes=[ps_af])
    SC("vector", lambda e: e.tensor_copy(out=lnw_sb[:], in_=ps_af[:, 0:64]),
       reads=[ps_af], writes=[lnw_sb])
    SC("scalar", lambda e: e.activation(out=alnw_sb[:], in_=lnw_sb[:], func=AF.Abs),
       reads=[lnw_sb], writes=[alnw_sb])

    for (w_hbm, w_scr, nrow) in ((wq_h, ws_q, D), (wk_h, ws_k, 512),
                                 (wv_h, ws_v, 512), (wo_h, ws_o, D)):
        for t in range(nrow // 128):
            buf = fwbuf[t % 2]
            r0 = t * 128
            ob, rr = r0 // 512, r0 % 512
            SC("sync", lambda e, b=buf, h=w_hbm, r0=r0:
               e.dma_start(out=b[:], in_=h[r0:r0 + 128, :]), writes=[buf], dma="ld")
            SC("vector", lambda e, b=buf: e.tensor_copy(out=hsp_bf[:], in_=b[:]),
               reads=[buf], writes=[hsp_bf])
            SC("vector", lambda e, b=buf:
               e.tensor_tensor(out=p1[:], in0=b[:], in1=hsp_bf[:], op=ALU.subtract),
               reads=[buf, hsp_bf], writes=[p1])
            SC("vector", lambda e: e.tensor_copy(out=lsp_bf[:], in_=p1[:]),
               reads=[p1], writes=[lsp_bf])
            SC("scalar", lambda e, s=w_scr, ob=ob, rr=rr:
               e.dma_start(out=s[ob, 0, rr:rr + 128, :], in_=hsp_bf[:]), reads=[hsp_bf],
               writes=[w_scr], dma="st")
            SC("scalar", lambda e, s=w_scr, ob=ob, rr=rr:
               e.dma_start(out=s[ob, 1, rr:rr + 128, :], in_=lsp_bf[:]), reads=[lsp_bf],
               writes=[w_scr], dma="st")

    def ffn_subtiles(nrow, ncol):
        for r in range(nrow // 128):
            for cs in range(ncol // D):
                yield r * 128, cs * D

    ffn_mats = (("g", wg_h, 0, II, D), ("u", wu_h, 1, II, D), ("d", wd_h, None, D, II))
    ph1_work = []

    def slice_items(nrow, ncol):
        for r in range(nrow // 128):
            for cs in range(ncol // 512):
                yield r * 128, cs * 512

    pbi = [0]

    for nm, w_hbm, w_scr, nrow, ncol in ffn_mats:
        t = thr[nm]
        SC("gpsimd", lambda e, t=t: e.memset(t["racc"][:], 0.0), writes=[t["racc"]])

        def b_item(r0, c0, w_hbm=w_hbm, t=t):
            buf = pb[pbi[0] % 2]
            pbi[0] += 1
            SC("gpsimd", lambda e, b=buf, h=w_hbm, r0=r0, c0=c0:
               e.dma_start(out=b[:], in_=h[r0:r0 + 128, c0:c0 + 512]),
               writes=[buf], dma="gld")
            SC("scalar", lambda e, b=buf:
               e.activation(out=pbg[:], in_=b[:], func=AF.Abs, accum_out=stat[6][:]),
               reads=[buf], writes=[pbg, stat[6]])
            SC("vector", lambda e, t=t:
               e.tensor_tensor(out=t["racc"][:], in0=t["racc"][:], in1=stat[6][:], op=ALU.add),
               reads=[stat[6], t["racc"]], writes=[t["racc"]])

        for r0, c0 in slice_items(nrow, ncol):
            ph1_work.append(lambda r0=r0, c0=c0, f=b_item: f(r0, c0))

        def b_fin(t=t, nrow=nrow, ncol=ncol):
            SC("tensor", lambda e, t=t:
               e.transpose(ps_acc[3][0:1, 0:128], t["racc"][:], idf32_sb[:]),
               reads=[t["racc"], idf32_sb], writes=[ps_acc[3]], tag="meanred")
            SC("vector", lambda e: e.tensor_copy(out=row0[0:1, 0:128], in_=ps_acc[3][0:1, 0:128]),
               reads=[ps_acc[3]], writes=[row0])
            nelem = float(nrow * ncol)
            SC("vector", lambda e, t=t:
               e.tensor_reduce(out=t["a"][:], in_=row0[0:1, 0:128], op=ALU.add, axis=AX.X),
               reads=[row0], writes=[t["a"]])
            SC("vector", lambda e, t=t, n=nelem:
               e.tensor_scalar(out=t["a"][:], in0=t["a"][:], scalar1=1.0 / n,
                               scalar2=W_EPS, op0=ALU.mult, op1=ALU.max),
               reads=[t["a"]], writes=[t["a"]])
            SC("vector", lambda e, t=t:
               e.tensor_scalar(out=t["a"][:], in0=t["a"][:], scalar1=0.5, scalar2=None,
                               op0=ALU.mult),
               reads=[t["a"]], writes=[t["a"]])
            SC("tensor", lambda e, t=t:
               e.matmul(ps_acc[3][:, 0:1], ones_1x128_f[:], t["a"][:], start=True, stop=True),
               reads=[ones_1x128_f, t["a"]], writes=[ps_acc[3]])
            SC("vector", lambda e, t=t: e.tensor_copy(out=t["abc"][:], in_=ps_acc[3][:, 0:1]),
               reads=[ps_acc[3]], writes=[t["abc"]])
            SC("vector", lambda e, t=t:
               e.tensor_scalar(out=t["anbc"][:], in0=t["abc"][:], scalar1=-1.0, scalar2=None,
                               op0=ALU.mult),
               reads=[t["abc"]], writes=[t["anbc"]])

        ph1_work.append(b_fin)

    for nm, w_hbm, w_scr, nrow, ncol in ffn_mats:
        t = thr[nm]

        def c_item(r0, c0, w_hbm=w_hbm, w_scr=w_scr, t=t):
            buf = pb[pbi[0] % 2]
            pbi[0] += 1
            SC("gpsimd", lambda e, b=buf, h=w_hbm, r0=r0, c0=c0:
               e.dma_start(out=b[:], in_=h[r0:r0 + 128, c0:c0 + 512]),
               writes=[buf], dma="gld")
            SC("scalar", lambda e, b=buf, t=t:
               e.activation(out=pbh[:], in_=b[:], func=AF.Sign, bias=t["anbc"][:]),
               reads=[buf, t["anbc"]], writes=[pbh])
            SC("scalar", lambda e, b=buf, t=t:
               e.activation(out=pbl[:], in_=b[:], func=AF.Sign, bias=t["abc"][:]),
               reads=[buf, t["abc"]], writes=[pbl])
            SC("vector", lambda e:
               e.tensor_tensor(out=pbt[:], in0=pbh[:], in1=pbl[:], op=ALU.add),
               reads=[pbh, pbl], writes=[pbt])
            if w_scr is None:
                SC("scalar", lambda e, r0=r0, c0=c0:
                   e.dma_start(out=ws_d[r0:r0 + 128, c0:c0 + 512], in_=pbt[:]),
                   reads=[pbt], writes=[ws_d], dma="st")
            else:
                SC("scalar", lambda e, gu=w_scr, ic=r0 // 256, rr=r0 % 256, c0=c0:
                   e.dma_start(out=ws_gu[ic, gu, rr:rr + 128, c0:c0 + 512], in_=pbt[:]),
                   reads=[pbt], writes=[ws_gu], dma="st")

        for r0, c0 in slice_items(nrow, ncol):
            ph1_work.append(lambda r0=r0, c0=c0, f=c_item: f(r0, c0))

    for grp in range(NGRP):
        share = (len(ph1_work) + NGRP - 1) // NGRP if grp == 0 else share
        for _ in range(min(share, len(ph1_work)) if grp < NGRP - 1 else len(ph1_work)):
            ph1_work.pop(0)()
        for cl in range(GRP):
            c = grp * GRP + cl
            SC("sync", lambda e, r0=c * 128:
               e.dma_start(out=xb0[:], in_=x_h[r0:r0 + 128, :]), writes=[xb0], dma="ld")
            SC("vector", lambda e: e.tensor_reduce(out=stat[0][:], in_=xb0[:], op=ALU.add,
                                                   axis=AX.X),
               reads=[xb0], writes=[stat[0]])
            SC("scalar", lambda e:
               e.activation(out=p1[:], in_=xb0[:], func=AF.Square, accum_out=stat[1][:]),
               reads=[xb0], writes=[p1, stat[1]])
            SC("vector", lambda e: e.tensor_scalar(out=stat[2][:], in0=stat[0][:],
                                                   scalar1=1.0 / D, scalar2=None, op0=ALU.mult),
               reads=[stat[0]], writes=[stat[2]])
            SC("vector", lambda e: e.tensor_scalar(out=stat[3][:], in0=stat[1][:],
                                                   scalar1=1.0 / D, scalar2=LN_EPS,
                                                   op0=ALU.mult, op1=ALU.add),
               reads=[stat[1]], writes=[stat[3]])
            SC("vector", lambda e: e.tensor_tensor(out=stat[4][:], in0=stat[2][:], in1=stat[2][:],
                                                   op=ALU.mult),
               reads=[stat[2]], writes=[stat[4]])
            SC("vector", lambda e: e.tensor_tensor(out=stat[3][:], in0=stat[3][:], in1=stat[4][:],
                                                   op=ALU.subtract),
               reads=[stat[3], stat[4]], writes=[stat[3]])
            SC("scalar", lambda e: e.activation(out=stat[4][:], in_=stat[3][:], func=AF.Sqrt),
               reads=[stat[3]], writes=[stat[4]])
            SC("vector", lambda e: e.reciprocal(out=stat[5][:], in_=stat[4][:]),
               reads=[stat[4]], writes=[stat[5]])
            SC("vector", lambda e: e.tensor_tensor(out=stat[4][:], in0=stat[5][:], in1=stat[5][:],
                                                   op=ALU.mult),
               reads=[stat[5]], writes=[stat[4]])
            SC("vector", lambda e: e.tensor_tensor(out=stat[4][:], in0=stat[4][:], in1=stat[3][:],
                                                   op=ALU.mult),
               reads=[stat[4], stat[3]], writes=[stat[4]])
            SC("vector", lambda e: e.tensor_scalar(out=stat[4][:], in0=stat[4][:], scalar1=-0.5,
                                                   scalar2=1.5, op0=ALU.mult, op1=ALU.add),
               reads=[stat[4]], writes=[stat[4]])
            SC("vector", lambda e: e.tensor_tensor(out=stat[4][:], in0=stat[4][:], in1=stat[5][:],
                                                   op=ALU.mult),
               reads=[stat[4], stat[5]], writes=[stat[4]])
            SC("vector", lambda e: e.tensor_tensor(out=stat[5][:], in0=stat[2][:], in1=stat[4][:],
                                                   op=ALU.mult),
               reads=[stat[2], stat[4]], writes=[stat[5]])
            SC("vector", lambda e: e.tensor_scalar(out=stat[5][:], in0=stat[5][:], scalar1=-1.0,
                                                   scalar2=None, op0=ALU.mult),
               reads=[stat[5]], writes=[stat[5]])
            xn = p0
            SC("scalar", lambda e:
               e.activation(out=xn[:], in_=xb0[:], func=AF.Identity, bias=stat[5][:],
                            scale=stat[4][:]),
               reads=[xb0, stat[4], stat[5]], writes=[xn])
            SC("vector", lambda e: e.tensor_copy(out=hsp_bf[:], in_=xn[:]),
               reads=[xn], writes=[hsp_bf])
            SC("vector", lambda e: e.tensor_tensor(out=p1[:], in0=xn[:], in1=hsp_bf[:],
                                                   op=ALU.subtract),
               reads=[xn, hsp_bf], writes=[p1])
            SC("vector", lambda e: e.tensor_copy(out=lsp_bf[:], in_=p1[:]),
               reads=[p1], writes=[lsp_bf])
            SC("scalar", lambda e: e.activation(out=p1[:], in_=xn[:], func=AF.Abs),
               reads=[xn], writes=[p1])
            SC("vector", lambda e: e.tensor_reduce(out=stat[0][:], in_=p1[:], op=ALU.max,
                                                   axis=AX.X),
               reads=[p1], writes=[stat[0]])
            SC("vector", lambda e: e.tensor_scalar(out=stat[1][:], in0=stat[0][:], scalar1=Q_EPS,
                                                   scalar2=None, op0=ALU.max),
               reads=[stat[0]], writes=[stat[1]])
            SC("vector", lambda e, c=c: e.tensor_scalar(out=sxinv_col[:, c:c + 1], in0=stat[1][:],
                                                        scalar1=1.0 / 127.0, scalar2=None,
                                                        op0=ALU.mult),
               reads=[stat[1]], writes=[sxinv_col])
            SC("vector", lambda e: e.reciprocal(out=stat[2][:], in_=stat[1][:]),
               reads=[stat[1]], writes=[stat[2]])
            SC("vector", lambda e: e.tensor_scalar(out=stat[2][:], in0=stat[2][:], scalar1=127.0,
                                                   scalar2=None, op0=ALU.mult),
               reads=[stat[2]], writes=[stat[2]])
            for sl4 in range(4):
                cs = slice(sl4 * 512, (sl4 + 1) * 512)
                SC("vector", lambda e, cs=cs: e.tensor_scalar(out=ep[0][:], in0=xn[:, cs],
                                                              scalar1=stat[2][:], scalar2=None,
                                                              op0=ALU.mult),
                   reads=[xn, stat[2]], writes=[ep[0]])
                SC("vector", lambda e: e.tensor_scalar(out=ep32[:], in0=ep[0][:], scalar1=MAGIC,
                                                       scalar2=MAGIC, op0=ALU.add,
                                                       op1=ALU.subtract),
                   reads=[ep[0]], writes=[ep32])
                SC("gpsimd", lambda e, cs=cs: e.tensor_copy(out=nxbf[:, cs], in_=ep32[:]),
                   reads=[ep32], writes=[nxbf])
            for kc in range(16):
                bank = ps_tr[kc % 2]
                SC("tensor", lambda e, b=bank, kc=kc:
                   e.transpose(b[:], nxbf[:, kc * 128:(kc + 1) * 128], idbf_sb[:]),
                   reads=[nxbf, idbf_sb], writes=[bank])
                SC("scalar", lambda e, b=bank, kc=kc, c=c:
                   e.copy(out=nxT[:, kc * S + c * 128: kc * S + c * 128 + 128], in_=b[:]),
                   reads=[bank], writes=[nxT])
            for (src, dstT) in ((hsp_bf, xnT[0]), (lsp_bf, xnT[1])):
                for kc in range(16):
                    bank = ps_tr[kc % 2]
                    SC("tensor", lambda e, b=bank, s=src, kc=kc:
                       e.transpose(b[:], s[:, kc * 128:(kc + 1) * 128], idbf_sb[:]),
                       reads=[src, idbf_sb], writes=[bank])
                    SC("vector", lambda e, b=bank, d=dstT, kc=kc, cl=cl:
                       e.tensor_copy(out=d[:, kc * GT + cl * 128: kc * GT + cl * 128 + 128],
                                     in_=b[:]),
                       reads=[bank], writes=[dstT])

        def rope_elu(bank, c, dst):
            SC("sync", lambda e, c=c: e.dma_start(out=cos_sb[:],
                                                  in_=cos_h[c * 128:(c + 1) * 128, :]),
               writes=[cos_sb], dma="ld")
            SC("sync", lambda e, c=c: e.dma_start(out=sin_sb[:],
                                                  in_=sin_h[c * 128:(c + 1) * 128, :]),
               writes=[sin_sb], dma="ld")
            qc, qs, ro = ep[0], ep[1], qro
            for hh in range(4):
                sl = slice(hh * 128, (hh + 1) * 128)
                SC("vector", lambda e, b=bank, sl=sl:
                   e.tensor_tensor(out=qc[:, sl], in0=b[:, sl], in1=cos_sb[:], op=ALU.mult),
                   reads=[bank, cos_sb], writes=[qc])
                SC("vector", lambda e, b=bank, sl=sl:
                   e.tensor_tensor(out=qs[:, sl], in0=b[:, sl], in1=sin_sb[:], op=ALU.mult),
                   reads=[bank, sin_sb], writes=[qs])
            for hh in range(4):
                b0 = hh * 128
                SC("vector", lambda e, b0=b0:
                   e.tensor_tensor(out=ro[:, b0:b0 + 64], in0=qc[:, b0:b0 + 64],
                                   in1=qs[:, b0 + 64:b0 + 128], op=ALU.subtract),
                   reads=[qc, qs], writes=[ro])
                SC("vector", lambda e, b0=b0:
                   e.tensor_tensor(out=ro[:, b0 + 64:b0 + 128], in0=qc[:, b0 + 64:b0 + 128],
                                   in1=qs[:, b0:b0 + 64], op=ALU.add),
                   reads=[qc, qs], writes=[ro])
            SC("vector", lambda e: e.tensor_scalar(out=ep[2][:], in0=ro[:], scalar1=0.0,
                                                   scalar2=None, op0=ALU.min),
               reads=[ro], writes=[ep[2]])
            SC("scalar", lambda e: e.activation(out=ep[3][:], in_=ep[2][:], func=AF.Exp),
               reads=[ep[2]], writes=[ep[3]])
            SC("vector", lambda e: e.scalar_tensor_tensor(out=dst[:], in0=ro[:], scalar=0.0,
                                                          in1=ep[3][:], op0=ALU.max, op1=ALU.add),
               reads=[ro, ep[3]], writes=[dst])

        def proj(w_scr, nblk, evac, tag):
            for ob in range(nblk):
                for kph in range(2):
                    for j in range(8):
                        kc = kph * 8 + j
                        SC("sync", lambda e, w=wth[j], s=w_scr, ob=ob, kc=kc:
                           e.dma_start_transpose(
                               w[:], s[ob].rearrange("a b c -> (a b) c")[
                                   :, kc * 128:(kc + 1) * 128]),
                           reads=[w_scr], writes=[wth[j]], dma="ld", tag=tag + "_wld")
                    for tb in range(GRP):
                        bank = ps_acc[tb]
                        for j in range(8):
                            kc = kph * 8 + j
                            xh_t = xnT[0][:, kc * GT + tb * 128: kc * GT + tb * 128 + 128]
                            xl_t = xnT[1][:, kc * GT + tb * 128: kc * GT + tb * 128 + 128]
                            SC("tensor", lambda e, b=bank, w=wth[j], xt=xh_t,
                               f=(kph == 0 and j == 0):
                               e.matmul(b[:], xt, w[:, 0:512], start=f, stop=False),
                               reads=[xnT[0], wth[j]], writes=[bank], tag=tag + "_mm")
                            SC("tensor", lambda e, b=bank, w=wth[j], xt=xh_t:
                               e.matmul(b[:], xt, w[:, 512:1024], start=False, stop=False),
                               reads=[xnT[0], wth[j]], writes=[bank])
                            SC("tensor", lambda e, b=bank, w=wth[j], xt=xl_t,
                               l=(kph == 1 and j == 7):
                               e.matmul(b[:], xt, w[:, 0:512], start=False, stop=l),
                               reads=[xnT[1], wth[j]], writes=[bank])
                for tb in range(GRP):
                    evac(tb, ob, ps_acc[tb])

        def q_evac(tb, ob, bank):
            c = grp * GRP + tb
            rope_elu(bank, c, qro)
            qh = nxbf
            SC("vector", lambda e: e.tensor_copy(out=qh[:, 0:512], in_=qro[:]),
               reads=[qro], writes=[qh])
            SC("vector", lambda e: e.tensor_tensor(out=ep[4][:], in0=qro[:], in1=qh[:, 0:512],
                                                   op=ALU.subtract),
               reads=[qro, qh], writes=[ep[4]])
            SC("vector", lambda e: e.tensor_copy(out=qh[:, 512:1024], in_=ep[4][:]),
               reads=[ep[4]], writes=[qh])
            for hh in range(4):
                hg = ob * 4 + hh
                for pi in range(2):
                    bank2 = ps_tr[(hh * 2 + pi) % 2]
                    SC("tensor", lambda e, b=bank2, pi=pi, hh=hh:
                       e.transpose(b[:], qh[:, pi * 512 + hh * 128: pi * 512 + hh * 128 + 128],
                                   idbf_sb[:]),
                       reads=[qh, idbf_sb], writes=[bank2])
                    SC("vector", lambda e, b=bank2, pi=pi, hg=hg, tb=tb:
                       e.tensor_copy(out=qfT[pi][:, hg * GT + tb * 128: hg * GT + tb * 128 + 128],
                                     in_=b[:]),
                       reads=[bank2], writes=[qfT[pi]])

        def k_evac(tb, ob, bank):
            c = grp * GRP + tb
            rope_elu(bank, c, qro)
            SC("vector", lambda e, tb=tb: e.tensor_copy(out=kftok[0][:, tb * 512:(tb + 1) * 512],
                                                        in_=qro[:]),
               reads=[qro], writes=[kftok[0]])
            SC("vector", lambda e, tb=tb:
               e.tensor_tensor(out=ep[4][:], in0=qro[:],
                               in1=kftok[0][:, tb * 512:(tb + 1) * 512], op=ALU.subtract),
               reads=[qro, kftok[0]], writes=[ep[4]])
            SC("vector", lambda e, tb=tb: e.tensor_copy(out=kftok[1][:, tb * 512:(tb + 1) * 512],
                                                        in_=ep[4][:]),
               reads=[ep[4]], writes=[kftok[1]])
            for hh in range(NKV):
                for pi in range(2):
                    bank2 = ps_tr[(hh * 2 + pi) % 2]
                    SC("tensor", lambda e, b=bank2, pi=pi, hh=hh, tb=tb:
                       e.transpose(b[:],
                                   kftok[pi][:, tb * 512 + hh * 128: tb * 512 + hh * 128 + 128],
                                   idbf_sb[:]),
                       reads=[kftok[pi], idbf_sb], writes=[bank2])
                    SC("vector", lambda e, b=bank2, pi=pi, hh=hh, tb=tb:
                       e.tensor_copy(out=kfT[pi][:, hh * GT + tb * 128: hh * GT + tb * 128 + 128],
                                     in_=b[:]),
                       reads=[bank2], writes=[kfT[pi]])

        def v_evac(tb, ob, bank):
            SC("vector", lambda e, b=bank: e.tensor_copy(out=qro[:], in_=b[:]),
               reads=[bank], writes=[qro])
            for g in range(NKV):
                slot = (tb * NKV + g) * 132
                SC("vector", lambda e, g=g, slot=slot:
                   e.tensor_copy(out=vaug[0][:, slot:slot + 128],
                                 in_=qro[:, g * 128:(g + 1) * 128]),
                   reads=[qro], writes=[vaug[0]])
                SC("vector", lambda e, g=g, slot=slot:
                   e.tensor_tensor(out=ep[4][:, 0:128], in0=qro[:, g * 128:(g + 1) * 128],
                                   in1=vaug[0][:, slot:slot + 128], op=ALU.subtract),
                   reads=[qro, vaug[0]], writes=[ep[4]])
                SC("vector", lambda e, slot=slot:
                   e.tensor_copy(out=vaug[1][:, slot:slot + 128], in_=ep[4][:, 0:128]),
                   reads=[ep[4]], writes=[vaug[1]])
                SC("gpsimd", lambda e, slot=slot: e.memset(vaug[0][:, slot + 128:slot + 129], 1.0),
                   writes=[vaug[0]])
                SC("gpsimd", lambda e, slot=slot: e.memset(vaug[1][:, slot + 128:slot + 129], 0.0),
                   writes=[vaug[1]])

        proj(ws_q, 4, q_evac, "qproj")
        proj(ws_k, 1, k_evac, "kproj")
        proj(ws_v, 1, v_evac, "vproj")
        if DEBUG_SINK:
            DEBUG_SINK(f"proj{grp}", nc, SC, dict(qfT=qfT, kfT=kfT, kftok=kftok,
                                                  vaug=vaug, xnT=xnT))

        for cl in range(GRP):
            c = grp * GRP + cl
            att_t = p0
            for h in range(NQH):
                g = h // 4
                pa = ps_af if h % 2 == 0 else ps_af2
                amt = amT[h % 2]
                a32 = am32[h % 2]
                nb = ps_acc[1] if h % 2 == 0 else ps_acc[3]
                ks = [kfT[p][:, g * GT + cl * 128: g * GT + cl * 128 + 128] for p in range(2)]
                qs_ = [qfT[p][:, h * GT + cl * 128: h * GT + cl * 128 + 128] for p in range(2)]
                SC("tensor", lambda e, pa=pa, a=ks[0], b=qs_[0]:
                   e.matmul(pa[:], a, b, start=True, stop=False),
                   reads=[kfT[0], qfT[0]], writes=[pa])
                SC("tensor", lambda e, pa=pa, a=ks[0], b=qs_[1]:
                   e.matmul(pa[:], a, b, start=False, stop=False),
                   reads=[kfT[0], qfT[1]], writes=[pa])
                SC("tensor", lambda e, pa=pa, a=ks[1], b=qs_[0]:
                   e.matmul(pa[:], a, b, start=False, stop=True),
                   reads=[kfT[1], qfT[0]], writes=[pa])
                SC("vector", lambda e, pa=pa, a32=a32:
                   e.tensor_tensor(out=a32[:], in0=pa[:], in1=mask_sb[:], op=ALU.mult),
                   reads=[pa, mask_sb], writes=[a32])
                SC("vector", lambda e, a32=a32, amt=amt: e.tensor_copy(out=amt[0][:], in_=a32[:]),
                   reads=[a32], writes=[amt[0]])
                SC("vector", lambda e, a32=a32, amt=amt:
                   e.tensor_tensor(out=ep[4][:, 0:128], in0=a32[:], in1=amt[0][:],
                                   op=ALU.subtract),
                   reads=[a32, amt[0]], writes=[ep[4]])
                SC("vector", lambda e, amt=amt: e.tensor_copy(out=amt[1][:], in_=ep[4][:, 0:128]),
                   reads=[ep[4]], writes=[amt[1]])
                vslot = (cl * NKV + g) * 132
                if c > 0:
                    SC("tensor", lambda e, nb=nb, q=qs_[0], g=g:
                       e.matmul(nb[:, 0:129], q, sth[g][:, 0:129], start=True, stop=False),
                       reads=[qfT[0], sth[g]], writes=[nb])
                    SC("tensor", lambda e, nb=nb, q=qs_[0], g=g:
                       e.matmul(nb[:, 0:129], q, stl[g][:, 0:129], start=False, stop=False),
                       reads=[qfT[0], stl[g]], writes=[nb])
                    SC("tensor", lambda e, nb=nb, q=qs_[1], g=g:
                       e.matmul(nb[:, 0:129], q, sth[g][:, 0:129], start=False, stop=False),
                       reads=[qfT[1], sth[g]], writes=[nb])
                    fi = False
                else:
                    fi = True
                SC("tensor", lambda e, nb=nb, amt=amt, vs=vslot, f=fi:
                   e.matmul(nb[:, 0:129], amt[0][:], vaug[0][:, vs:vs + 129], start=f, stop=False),
                   reads=[amt[0], vaug[0]], writes=[nb])
                SC("tensor", lambda e, nb=nb, amt=amt, vs=vslot:
                   e.matmul(nb[:, 0:129], amt[0][:], vaug[1][:, vs:vs + 129], start=False,
                            stop=False),
                   reads=[amt[0], vaug[1]], writes=[nb])
                SC("tensor", lambda e, nb=nb, amt=amt, vs=vslot:
                   e.matmul(nb[:, 0:129], amt[1][:], vaug[0][:, vs:vs + 129], start=False,
                            stop=True),
                   reads=[amt[1], vaug[0]], writes=[nb])
                st_d = stat[0] if h % 2 == 0 else stat[2]
                st_r = stat[1] if h % 2 == 0 else stat[3]
                SC("vector", lambda e, nb=nb, sd=st_d:
                   e.tensor_scalar(out=sd[:], in0=nb[:, 128:129],
                                   scalar1=ATT_EPS, scalar2=None, op0=ALU.add),
                   reads=[nb], writes=[st_d])
                SC("vector", lambda e, sd=st_d, sr=st_r: e.reciprocal(out=sr[:], in_=sd[:]),
                   reads=[st_d], writes=[st_r])
                SC("vector", lambda e, nb=nb, sr=st_r, h=h:
                   e.tensor_scalar(out=att_t[:, h * 128:(h + 1) * 128], in0=nb[:, 0:128],
                                   scalar1=sr[:], scalar2=None, op0=ALU.mult),
                   reads=[nb, st_r], writes=[att_t])
            for g in range(NKV):
                vslot = (cl * NKV + g) * 132
                kt = [kftok[p][:, cl * 512 + g * 128: cl * 512 + g * 128 + 128] for p in range(2)]
                db = ps_acc[0]
                SC("tensor", lambda e, k=kt[0], vs=vslot:
                   e.matmul(db[:, 0:129], k, vaug[0][:, vs:vs + 129], start=True, stop=False),
                   reads=[kftok[0], vaug[0]], writes=[db])
                SC("tensor", lambda e, k=kt[0], vs=vslot:
                   e.matmul(db[:, 0:129], k, vaug[1][:, vs:vs + 129], start=False, stop=False),
                   reads=[kftok[0], vaug[1]], writes=[db])
                SC("tensor", lambda e, k=kt[1], vs=vslot:
                   e.matmul(db[:, 0:129], k, vaug[0][:, vs:vs + 129], start=False, stop=True),
                   reads=[kftok[1], vaug[0]], writes=[db])
                SC("vector", lambda e, g=g:
                   e.tensor_tensor(out=st32[g][:, 0:129], in0=st32[g][:, 0:129],
                                   in1=db[:, 0:129], op=ALU.add),
                   reads=[st32[g], db], writes=[st32[g]])
                SC("vector", lambda e, g=g: e.tensor_copy(out=sth[g][:, 0:129],
                                                          in_=st32[g][:, 0:129]),
                   reads=[st32[g]], writes=[sth[g]])
                SC("vector", lambda e, g=g:
                   e.tensor_tensor(out=ep[2][:, 0:129], in0=st32[g][:, 0:129],
                                   in1=sth[g][:, 0:129], op=ALU.subtract),
                   reads=[st32[g], sth[g]], writes=[ep[2]])
                SC("vector", lambda e, g=g: e.tensor_copy(out=stl[g][:, 0:129],
                                                          in_=ep[2][:, 0:129]),
                   reads=[ep[2]], writes=[stl[g]])
            if DEBUG_SINK:
                DEBUG_SINK(f"att{c}", nc, SC, dict(att_t=att_t, st32=st32))
            SC("vector", lambda e: e.tensor_copy(out=hsp_bf[:], in_=att_t[:]),
               reads=[att_t], writes=[hsp_bf])
            SC("vector", lambda e: e.tensor_tensor(out=p1[:], in0=att_t[:], in1=hsp_bf[:],
                                                   op=ALU.subtract),
               reads=[att_t, hsp_bf], writes=[p1])
            SC("vector", lambda e: e.tensor_copy(out=lsp_bf[:], in_=p1[:]),
               reads=[p1], writes=[lsp_bf])
            for (src, dstT) in ((hsp_bf, attT[0]), (lsp_bf, attT[1])):
                for kc in range(16):
                    bank = ps_tr[kc % 2]
                    SC("tensor", lambda e, b=bank, s=src, kc=kc:
                       e.transpose(b[:], s[:, kc * 128:(kc + 1) * 128], idbf_sb[:]),
                       reads=[src, idbf_sb], writes=[bank])
                    SC("scalar", lambda e, b=bank, d=dstT, kc=kc, cl=cl:
                       e.copy(out=d[:, kc * GT + cl * 128: kc * GT + cl * 128 + 128], in_=b[:]),
                       reads=[bank], writes=[dstT])

        for ob in range(4):
            for kph in range(2):
                for j in range(8):
                    kc = kph * 8 + j
                    SC("sync", lambda e, w=wth[j], ob=ob, kc=kc:
                       e.dma_start_transpose(
                           w[:], ws_o[ob].rearrange("a b c -> (a b) c")[
                               :, kc * 128:(kc + 1) * 128]),
                       reads=[ws_o], writes=[wth[j]], dma="ld", tag="oproj_wld")
                for tb in range(GRP):
                    bank = ps_acc[tb]
                    for j in range(8):
                        kc = kph * 8 + j
                        ah_t = attT[0][:, kc * GT + tb * 128: kc * GT + tb * 128 + 128]
                        al_t = attT[1][:, kc * GT + tb * 128: kc * GT + tb * 128 + 128]
                        SC("tensor", lambda e, b=bank, w=wth[j], xt=ah_t,
                           f=(kph == 0 and j == 0):
                           e.matmul(b[:], xt, w[:, 0:512], start=f, stop=False),
                           reads=[attT[0], wth[j]], writes=[bank], tag="oproj_mm")
                        SC("tensor", lambda e, b=bank, w=wth[j], xt=ah_t:
                           e.matmul(b[:], xt, w[:, 512:1024], start=False, stop=False),
                           reads=[attT[0], wth[j]], writes=[bank])
                        SC("tensor", lambda e, b=bank, w=wth[j], xt=al_t,
                           l=(kph == 1 and j == 7):
                           e.matmul(b[:], xt, w[:, 0:512], start=False, stop=l),
                           reads=[attT[1], wth[j]], writes=[bank])
            for tb in range(GRP):
                bank = ps_acc[tb]
                SC("vector", lambda e, b=bank: e.tensor_copy(out=ep[0][:], in_=b[:]),
                   reads=[bank], writes=[ep[0]])
                SC("scalar", lambda e, ob=ob, tb=tb, grp=grp:
                   e.dma_start(out=att_h[grp * GT + tb * 128: grp * GT + tb * 128 + 128,
                                         ob * 512:(ob + 1) * 512], in_=ep[0][:]),
                   reads=[ep[0]], writes=[att_h], dma="st", tag="att_spill")

    for c in range(NCH):
        SC("scalar", lambda e, c=c: e.dma_start(out=rowscr_h[0, c * 128:(c + 1) * 128],
                                                in_=sxinv_col[:, c:c + 1]),
           reads=[sxinv_col], writes=[rowscr_h], dma="st")
    for bb in range(4):
        SC("sync", lambda e, bb=bb: e.dma_start(out=row0[:],
                                                in_=rowscr_h[0:1, bb * 512:(bb + 1) * 512]),
           reads=[rowscr_h], writes=[row0], dma="ld")
        SC("tensor", lambda e, bb=bb:
           e.matmul(ps_acc[3][:], ones_1x128_f[:], row0[0:1, 0:512],
                    start=True, stop=True),
           reads=[ones_1x128_f, row0], writes=[ps_acc[3]])
        SC("vector", lambda e, bb=bb: e.tensor_copy(out=Bsx[:, bb * 512:(bb + 1) * 512],
                                                    in_=ps_acc[3][:]),
           reads=[ps_acc[3]], writes=[Bsx])

    def emit_gu_epilogue(ic, ib, tb, bg, bu):
        ts0 = tb * 512
        ig = ic * 2 + ib
        SC("vector", lambda e, b=bg, ts0=ts0:
           e.scalar_tensor_tensor(out=ep[0][:], in0=b[:], scalar=thr["g"]["abc"][:],
                          in1=Bsx[:, ts0:ts0 + 512], op0=ALU.mult, op1=ALU.mult),
           reads=[bg, thr["g"]["abc"], Bsx], writes=[ep[0]])
        SC("scalar", lambda e: e.activation(out=ep[1][:], in_=ep[0][:], func=AF.Silu),
           reads=[ep[0]], writes=[ep[1]])
        SC("vector", lambda e, b=bu, ts0=ts0:
           e.scalar_tensor_tensor(out=ep[0][:], in0=b[:], scalar=thr["u"]["abc"][:],
                          in1=Bsx[:, ts0:ts0 + 512], op0=ALU.mult, op1=ALU.mult),
           reads=[bu, thr["u"]["abc"], Bsx], writes=[ep[0]])
        SC("vector", lambda e: e.tensor_tensor(out=ep[2][:], in0=ep[1][:], in1=ep[0][:],
                               op=ALU.mult),
           reads=[ep[1], ep[0]], writes=[ep[2]])
        if DEBUG_SINK and ic == 0 and ib == 0 and tb == 0:
            DEBUG_SINK("ffn0", nc, SC, dict(gt_silu=ep[1], ut=ep[0], Bsx=Bsx))
        SC("scalar", lambda e, ig=ig, ts0=ts0:
           e.dma_start(out=pt_h[ig * 128:(ig + 1) * 128, ts0:ts0 + 512], in_=ep[2][:]),
           reads=[ep[2]], writes=[pt_h], dma="st", tag="p_spill")

    def gu_tile(kc):
        return wth[kc // 2][:, (kc % 2) * 512:(kc % 2) * 512 + 512]

    for ic in range(32):
        for kc in range(16):
            SC("sync", lambda e, kc=kc, ic=ic:
               e.dma_start_transpose(
                   gu_tile(kc), ws_gu[ic].rearrange("a b c -> (a b) c")[
                       :, kc * 128:(kc + 1) * 128]),
               reads=[ws_gu], writes=[wth[kc // 2]], dma="ld", tag="wg_ld")
        for ib in range(2):
            ig = ic * 2 + ib
            for tb in range(4):
                par = (ib * 4 + tb) % 2
                bg, bu = ps_acc[0 + 2 * par], ps_acc[1 + 2 * par]
                for kc in range(16):
                    SC("tensor", lambda e, b=bg, kc=kc, ib=ib, tb=tb,
                       f=(kc == 0), l=(kc == 15):
                       e.matmul(b[:], gu_tile(kc)[:, ib * 128:(ib + 1) * 128],
                                nxT[:, kc * S + tb * 512: kc * S + (tb + 1) * 512],
                                start=f, stop=l),
                       reads=[wth[kc // 2], nxT], writes=[bg], tag="gmm")
                for kc in range(16):
                    SC("tensor", lambda e, b=bu, kc=kc, ib=ib, tb=tb,
                       f=(kc == 0), l=(kc == 15):
                       e.matmul(b[:], gu_tile(kc)[:, 256 + ib * 128: 256 + (ib + 1) * 128],
                                nxT[:, kc * S + tb * 512: kc * S + (tb + 1) * 512],
                                start=f, stop=l),
                       reads=[wth[kc // 2], nxT], writes=[bu], tag="umm")
                emit_gu_epilogue(ic, ib, tb, bg, bu)

    def emit_alpha(tsl):
        ts0 = tsl * 512
        SC("gpsimd", lambda e: e.memset(ssqsl[:], 0.0), writes=[ssqsl])
        SC("gpsimd", lambda e: e.memset(maxsl[:], 0.0), writes=[maxsl])
        for ig in range(64):
            SC("sync", lambda e, ig=ig, ts0=ts0:
               e.dma_start(out=ep[4][:], in_=pt_h[ig * 128:(ig + 1) * 128, ts0:ts0 + 512]),
               reads=[pt_h], writes=[ep[4]], dma="ld", tag="p_ld1")
            SC("scalar", lambda e: e.activation(out=ep[0][:], in_=ep[4][:], func=AF.Square),
               reads=[ep[4]], writes=[ep[0]])
            SC("vector", lambda e: e.tensor_tensor(out=ssqsl[:], in0=ssqsl[:], in1=ep[0][:],
                                                   op=ALU.add),
               reads=[ssqsl, ep[0]], writes=[ssqsl])
            SC("scalar", lambda e: e.activation(out=ep[0][:], in_=ep[4][:], func=AF.Abs),
               reads=[ep[4]], writes=[ep[0]])
            SC("vector", lambda e, ig=ig:
               e.scalar_tensor_tensor(out=maxsl[:], in0=ep[0][:], scalar=alnw_sb[:, ig:ig + 1],
                                      in1=maxsl[:], op0=ALU.mult, op1=ALU.max),
               reads=[ep[0], alnw_sb, maxsl], writes=[maxsl])

    def emit_stats(tsl):
        ts0 = tsl * 512
        for j in range(4):
            c = tsl * 4 + j
            SC("tensor", lambda e, j=j: e.transpose(ps_af[:], ssqsl[:, j * 128:(j + 1) * 128],
                                                    idf32_sb[:]),
               reads=[ssqsl, idf32_sb], writes=[ps_af])
            SC("vector", lambda e: e.tensor_copy(out=ep[0][:, 0:128], in_=ps_af[:]),
               reads=[ps_af], writes=[ep[0]])
            SC("vector", lambda e: e.tensor_reduce(out=stat[0][:], in_=ep[0][:, 0:128],
                                                   op=ALU.add, axis=AX.X),
               reads=[ep[0]], writes=[stat[0]])
            SC("tensor", lambda e, j=j: e.transpose(ps_af[:], maxsl[:, j * 128:(j + 1) * 128],
                                                    idf32_sb[:]),
               reads=[maxsl, idf32_sb], writes=[ps_af])
            SC("vector", lambda e: e.tensor_copy(out=ep[0][:, 0:128], in_=ps_af[:]),
               reads=[ps_af], writes=[ep[0]])
            SC("vector", lambda e: e.tensor_reduce(out=stat[1][:], in_=ep[0][:, 0:128],
                                                   op=ALU.max, axis=AX.X),
               reads=[ep[0]], writes=[stat[1]])
            SC("vector", lambda e: e.tensor_scalar(out=stat[2][:], in0=stat[0][:],
                                                   scalar1=1.0 / II, scalar2=RMS_EPS,
                                                   op0=ALU.mult, op1=ALU.add),
               reads=[stat[0]], writes=[stat[2]])
            SC("scalar", lambda e: e.activation(out=stat[3][:], in_=stat[2][:], func=AF.Sqrt),
               reads=[stat[2]], writes=[stat[3]])
            SC("vector", lambda e: e.reciprocal(out=stat[4][:], in_=stat[3][:]),
               reads=[stat[3]], writes=[stat[4]])
            SC("vector", lambda e: e.tensor_tensor(out=stat[3][:], in0=stat[4][:], in1=stat[4][:],
                                                   op=ALU.mult),
               reads=[stat[4]], writes=[stat[3]])
            SC("vector", lambda e: e.tensor_tensor(out=stat[3][:], in0=stat[3][:], in1=stat[2][:],
                                                   op=ALU.mult),
               reads=[stat[3], stat[2]], writes=[stat[3]])
            SC("vector", lambda e: e.tensor_scalar(out=stat[3][:], in0=stat[3][:], scalar1=-0.5,
                                                   scalar2=1.5, op0=ALU.mult, op1=ALU.add),
               reads=[stat[3]], writes=[stat[3]])
            SC("vector", lambda e: e.tensor_tensor(out=stat[3][:], in0=stat[3][:], in1=stat[4][:],
                                                   op=ALU.mult),
               reads=[stat[3], stat[4]], writes=[stat[3]])
            SC("vector", lambda e: e.tensor_tensor(out=stat[1][:], in0=stat[1][:], in1=stat[3][:],
                                                   op=ALU.mult),
               reads=[stat[1], stat[3]], writes=[stat[1]])
            SC("vector", lambda e: e.tensor_scalar(out=stat[1][:], in0=stat[1][:], scalar1=Q_EPS,
                                                   scalar2=None, op0=ALU.max),
               reads=[stat[1]], writes=[stat[1]])
            SC("vector", lambda e: e.reciprocal(out=stat[4][:], in_=stat[1][:]),
               reads=[stat[1]], writes=[stat[4]])
            SC("vector", lambda e: e.tensor_scalar(out=stat[4][:], in0=stat[4][:], scalar1=127.0,
                                                   scalar2=None, op0=ALU.mult),
               reads=[stat[4]], writes=[stat[4]])
            SC("vector", lambda e: e.tensor_tensor(out=stat[5][:], in0=stat[3][:], in1=stat[4][:],
                                                   op=ALU.mult),
               reads=[stat[3], stat[4]], writes=[stat[5]])
            SC("scalar", lambda e, c=c: e.dma_start(out=rowscr_h[3, c * 128:(c + 1) * 128],
                                                    in_=stat[5][:]),
               reads=[stat[5]], writes=[rowscr_h], dma="st")
            SC("vector", lambda e: e.tensor_scalar(out=stat[6][:], in0=stat[1][:],
                                                   scalar1=1.0 / 127.0, scalar2=None,
                                                   op0=ALU.mult),
               reads=[stat[1]], writes=[stat[6]])
            SC("vector", lambda e, c=c:
               e.tensor_tensor(out=sd_col[:, c:c + 1], in0=stat[6][:], in1=thr["d"]["abc"][:],
                               op=ALU.mult),
               reads=[stat[6], thr["d"]["abc"]], writes=[sd_col])
        SC("sync", lambda e, ts0=ts0: e.dma_start(out=row0[0:1, 0:512],
                                                  in_=rowscr_h[3, ts0:ts0 + 512]),
           reads=[rowscr_h], writes=[row0], dma="ld")
        SC("tensor", lambda e:
           e.matmul(ps_acc[3][:], ones_1x128_f[:], row0[0:1, 0:512], start=True, stop=True),
           reads=[ones_1x128_f, row0], writes=[ps_acc[3]])
        chs = ssqsl
        SC("vector", lambda e: e.tensor_copy(out=chs[:], in_=ps_acc[3][:]),
           reads=[ps_acc[3]], writes=[chs])

    def emit_beta(tsl):
        ts0 = tsl * 512
        chs = ssqsl
        for ig in range(64):
            SC("sync", lambda e, ig=ig, ts0=ts0:
               e.dma_start(out=ep[4][:], in_=pt_h[ig * 128:(ig + 1) * 128, ts0:ts0 + 512]),
               reads=[pt_h], writes=[ep[4]], dma="ld", tag="p_ld2")
            SC("vector", lambda e, ig=ig:
               e.scalar_tensor_tensor(out=ep[0][:], in0=ep[4][:], scalar=lnw_sb[:, ig:ig + 1],
                                      in1=chs[:], op0=ALU.mult, op1=ALU.mult),
               reads=[ep[4], lnw_sb, chs], writes=[ep[0]])
            SC("vector", lambda e: e.tensor_scalar(out=ep32[:], in0=ep[0][:], scalar1=MAGIC,
                                                   scalar2=MAGIC, op0=ALU.add, op1=ALU.subtract),
               reads=[ep[0]], writes=[ep32])
            SC("vector", lambda e, ig=ig: e.tensor_copy(out=nhT[:, ig * 512:(ig + 1) * 512],
                                                        in_=ep32[:]),
               reads=[ep32], writes=[nhT], tag="nh_cast")

    def emit_down(tsl):
        ts0 = tsl * 512
        for ob in range(4):
            for kc in range(64):
                wt_ = wth[kc % 8]
                q, qe = "ld", "sync"
                SC(qe, lambda e, w=wt_, ob=ob, kc=kc:
                   e.dma_start_transpose(w[:, 0:512], ws_d[ob * 512:(ob + 1) * 512,
                                                          kc * 128:(kc + 1) * 128]),
                   reads=[ws_d], writes=[wt_], dma=q, tag="wd_ld")
                for tb in range(4):
                    SC("tensor", lambda e, w=wt_, kc=kc, tb=tb, b=ps_acc[tb],
                       f=(kc == 0), l=(kc == 63):
                       e.matmul(b[:], nhT[:, kc * 512 + tb * 128: kc * 512 + tb * 128 + 128],
                                w[:, 0:512], start=f, stop=l),
                       reads=[nhT, wt_], writes=[ps_acc[tb]], tag="dmm")
            for tb in range(4):
                bank = ps_acc[tb]
                c = tsl * 4 + tb
                SC("sync", lambda e, c=c, ob=ob:
                   e.dma_start(out=ep[1][:], in_=x_h[c * 128:(c + 1) * 128,
                                                     ob * 512:(ob + 1) * 512]),
                   writes=[ep[1]], dma="ld", tag="x_reload")
                SC("sync", lambda e, c=c, ob=ob:
                   e.dma_start(out=ep[2][:], in_=att_h[c * 128:(c + 1) * 128,
                                                       ob * 512:(ob + 1) * 512]),
                   reads=[att_h], writes=[ep[2]], dma="ld", tag="att_reload")
                SC("vector", lambda e, b=bank, c=c:
                   e.tensor_scalar(out=ep[3][:], in0=b[:], scalar1=sd_col[:, c:c + 1],
                                   scalar2=None, op0=ALU.mult),
                   reads=[bank, sd_col], writes=[ep[3]])
                SC("vector", lambda e: e.tensor_tensor(out=ep[2][:], in0=ep[1][:], in1=ep[2][:],
                                                       op=ALU.add),
                   reads=[ep[1], ep[2]], writes=[ep[2]])
                SC("vector", lambda e: e.tensor_tensor(out=ep[2][:], in0=ep[3][:], in1=ep[2][:],
                                                       op=ALU.add),
                   reads=[ep[3], ep[2]], writes=[ep[2]])
                SC("scalar", lambda e, c=c, ob=ob:
                   e.dma_start(out=out_h[c * 128:(c + 1) * 128, ob * 512:(ob + 1) * 512],
                               in_=ep[2][:]),
                   reads=[ep[2]], dma="st", tag="out_st")


    for tsl in range(4):
        emit_alpha(tsl)
        emit_stats(tsl)
        if tsl > 0:
            emit_down(tsl - 1)
        emit_beta(tsl)
    emit_down(3)

    _probe = []
    import os as _os
    _pk = int(_os.environ.get("SBUF_PROBE_KB", "0"))
    if _pk:
        _probe.append(sb("probe", [128, _pk * 256], F32))
    last_store = sch.steps[-1]
    term = Step("gpsimd", lambda e: e.engine_nop(), [], [], None, len(sch.steps), "term")
    term.waits.append(last_store)
    sch.steps.append(term)

    sch.resolve_and_emit(nc)
    stack.close()
    return nc


def _host_constants():
    j = np.arange(0, HD, 2, dtype=np.float64)
    inv_freq = 1.0 / (10000.0 ** (j / HD))
    t = np.arange(S, dtype=np.float64)
    freqs = np.outer(t, inv_freq)
    emb = np.concatenate([freqs, freqs], axis=-1)
    cos = np.cos(emb).astype(np.float32)
    sin = np.sin(emb).astype(np.float32)
    ii = np.arange(128)
    mask = (ii[:, None] <= ii[None, :]).astype(ml_dtypes.bfloat16)
    idbf = np.eye(128, dtype=ml_dtypes.bfloat16)
    idf32 = np.eye(128, dtype=np.float32)
    return cos, sin, mask, idbf, idf32


_NC_CACHE = {}


def kernel(L0_attn_x, L0_attn_Wq, L0_attn_Wk, L0_attn_Wv, L0_attn_Wo,
           L0_ffn_w_gate, L0_ffn_w_up, L0_ffn_w_down, L0_ffn_ln_weight):
    x = np.ascontiguousarray(np.asarray(L0_attn_x, dtype=np.float32))
    B = x.shape[0]
    cos, sin, mask, idbf, idf32 = _host_constants()
    common = {
        "wq": np.ascontiguousarray(np.asarray(L0_attn_Wq, np.float32)),
        "wk": np.ascontiguousarray(np.asarray(L0_attn_Wk, np.float32)),
        "wv": np.ascontiguousarray(np.asarray(L0_attn_Wv, np.float32)),
        "wo": np.ascontiguousarray(np.asarray(L0_attn_Wo, np.float32)),
        "wg": np.ascontiguousarray(np.asarray(L0_ffn_w_gate, np.float32)),
        "wu": np.ascontiguousarray(np.asarray(L0_ffn_w_up, np.float32)),
        "wd": np.ascontiguousarray(np.asarray(L0_ffn_w_down, np.float32)),
        "lnw": np.ascontiguousarray(np.asarray(L0_ffn_ln_weight, np.float32)),
        "costab": cos, "sintab": sin, "masktab": mask,
        "idbf": idbf, "idf32": idf32,
    }
    if "nc" not in _NC_CACHE:
        _NC_CACHE["nc"] = build_program()
    nc = _NC_CACHE["nc"]
    in_maps = [dict(common, x=np.ascontiguousarray(x[b])) for b in range(B)]
    res = run_bass_kernel_spmd(nc, in_maps, core_ids=list(range(B)))
    return np.stack([res.results[b]["out"] for b in range(B)], axis=0)


if __name__ == "__main__":
    import time
    t0 = time.time()
    nc = build_program()
    print(f"built ok in {time.time() - t0:.1f}s, "
          f"instructions={sum(1 for _ in nc.all_instructions())}")


# version 18
# speedup vs baseline: 1.0080x; 1.0080x over previous
import numpy as np
import ml_dtypes
import concourse.bass as bass
import concourse.mybir as mybir
from concourse.bass_utils import run_bass_kernel_spmd

F32 = mybir.dt.float32
BF16 = mybir.dt.bfloat16
I32 = mybir.dt.int32
AF = mybir.ActivationFunctionType
ALU = mybir.AluOpType
AX = mybir.AxisListType

MAGIC = 12582912.0

S = 2048
D = 2048
II = 8192
HD = 128
NQH = 16
NKV = 4
NCH = S // 128
GRP = 2
NGRP = NCH // GRP
GT = GRP * 128
LN_EPS = 1e-5
RMS_EPS = 1e-6
ATT_EPS = 1e-6
Q_EPS = 1e-5
W_EPS = 1e-5

ENGINES = ("sync", "vector", "scalar", "tensor", "gpsimd")

DEBUG_SINK = None


class Step:
    __slots__ = ("eng", "emit", "reads", "writes", "dma", "idx",
                 "needs_inc", "count", "waits", "tag", "slot", "gen")

    def __init__(self, eng, emit, reads, writes, dma, idx, tag):
        self.eng = eng
        self.emit = emit
        self.reads = reads
        self.writes = writes
        self.dma = dma
        self.idx = idx
        self.needs_inc = False
        self.count = None
        self.waits = []
        self.tag = tag


class Sched:

    def __init__(self):
        self.steps = []
        self.last_writer = {}
        self.readers = {}

    def do(self, eng, emit, reads=(), writes=(), dma=None, tag=""):
        st = Step(eng, emit, list(reads), list(writes), dma, len(self.steps), tag)
        deps = []
        for b in st.reads:
            w = self.last_writer.get(id(b))
            if w is not None:
                deps.append(w)
        for b in st.writes:
            w = self.last_writer.get(id(b))
            if w is not None:
                deps.append(w)
            deps.extend(self.readers.get(id(b), ()))
        seen = set()
        for d in deps:
            if d.idx in seen or d is st:
                continue
            seen.add(d.idx)
            if (d.dma is None and st.dma is None and d.eng == st.eng
                    and d.eng == "tensor"):
                continue
            st.waits.append(d)
        for b in st.reads:
            self.readers.setdefault(id(b), []).append(st)
        for b in st.writes:
            self.last_writer[id(b)] = st
            self.readers[id(b)] = []
        self.steps.append(st)
        return st

    DMA_SLOTS = 16

    def _semkey(self, st):
        if st.dma is not None:
            return ("dma", st.dma, st.slot)
        return ("eng", st.eng)

    def resolve_and_emit(self, nc):
        P = self.DMA_SLOTS
        qidx = {}
        for st in self.steps:
            if st.dma is not None:
                i = qidx.get(st.dma, 0)
                qidx[st.dma] = i + 1
                st.slot = i % P
                st.gen = i // P
                st.needs_inc = True
                st.count = 16 * (st.gen + 1)
            for d in st.waits:
                d.needs_inc = True
        cnt = {}
        for st in self.steps:
            if st.needs_inc and st.dma is None:
                key = self._semkey(st)
                cnt[key] = cnt.get(key, 0) + 1
                st.count = cnt[key]
        from contextlib import ExitStack
        self._semstack = ExitStack()
        sems = {}
        for st in self.steps:
            if st.needs_inc:
                key = self._semkey(st)
                if key not in sems:
                    sems[key] = self._semstack.enter_context(
                        nc.semaphore("sem_" + "_".join(str(k) for k in key[1:])))
        hw = {e: {} for e in ENGINES}
        per_eng = {e: [] for e in ENGINES}
        for st in self.steps:
            per_eng[st.eng].append(st)

        with nc.Block() as block:
            def make_body(eng_name):
                def body(eng):
                    marks = hw[eng_name]
                    for st in per_eng[eng_name]:
                        if st.dma is not None and st.gen > 0:
                            key = self._semkey(st)
                            v = 16 * st.gen
                            if marks.get(key, 0) < v:
                                eng.wait_ge(sems[key], v)
                                marks[key] = v
                        for d in st.waits:
                            key = self._semkey(d)
                            v = d.count
                            if marks.get(key, 0) >= v:
                                continue
                            eng.wait_ge(sems[key], v)
                            marks[key] = v
                        inst = st.emit(eng)
                        if st.needs_inc:
                            key = self._semkey(st)
                            inst.then_inc(sems[key], 16 if st.dma else 1)
                return body
            block.sync(make_body("sync"))
            block.vector(make_body("vector"))
            block.scalar(make_body("scalar"))
            block.tensor(make_body("tensor"))
            block.gpsimd(make_body("gpsimd"))
        return sems


def build_program():
    nc = bass.Bass("TRN2", target_bir_lowering=False, debug=False)
    sch = Sched()
    from contextlib import ExitStack
    stack = ExitStack()

    def sb(name, shape, dtype):
        return stack.enter_context(nc.sbuf_tensor(name, shape, dtype))

    def psu(name, shape, dtype=F32):
        return stack.enter_context(nc.psum_tensor(name, shape, dtype))

    x_h = nc.dram_tensor("x", [S, D], F32, kind="ExternalInput").ap()
    wq_h = nc.dram_tensor("wq", [D, D], F32, kind="ExternalInput").ap()
    wk_h = nc.dram_tensor("wk", [512, D], F32, kind="ExternalInput").ap()
    wv_h = nc.dram_tensor("wv", [512, D], F32, kind="ExternalInput").ap()
    wo_h = nc.dram_tensor("wo", [D, D], F32, kind="ExternalInput").ap()
    wg_h = nc.dram_tensor("wg", [II, D], F32, kind="ExternalInput").ap()
    wu_h = nc.dram_tensor("wu", [II, D], F32, kind="ExternalInput").ap()
    wd_h = nc.dram_tensor("wd", [D, II], F32, kind="ExternalInput").ap()
    lnw_h = nc.dram_tensor("lnw", [II], F32, kind="ExternalInput").ap()
    cos_h = nc.dram_tensor("costab", [S, HD], F32, kind="ExternalInput").ap()
    sin_h = nc.dram_tensor("sintab", [S, HD], F32, kind="ExternalInput").ap()
    mask_h = nc.dram_tensor("masktab", [128, 128], BF16, kind="ExternalInput").ap()
    idbf_h = nc.dram_tensor("idbf", [128, 128], BF16, kind="ExternalInput").ap()
    idf32_h = nc.dram_tensor("idf32", [128, 128], F32, kind="ExternalInput").ap()
    out_h = nc.dram_tensor("out", [S, D], F32, kind="ExternalOutput").ap()

    ws_q = nc.dram_tensor("ws_q", [4, 2, 512, D], BF16, kind="Internal").ap()
    ws_k = nc.dram_tensor("ws_k", [1, 2, 512, D], BF16, kind="Internal").ap()
    ws_v = nc.dram_tensor("ws_v", [1, 2, 512, D], BF16, kind="Internal").ap()
    ws_o = nc.dram_tensor("ws_o", [4, 2, 512, D], BF16, kind="Internal").ap()
    ws_gu = nc.dram_tensor("ws_gu", [32, 2, 256, D], BF16, kind="Internal").ap()
    ws_d = nc.dram_tensor("ws_d", [D, II], BF16, kind="Internal").ap()
    pt_h = nc.dram_tensor("pt", [II, S], F32, kind="Internal").ap()
    att_h = nc.dram_tensor("att_sp", [S, D], F32, kind="Internal").ap()
    rowscr_h = nc.dram_tensor("rowscr", [4, S], F32, kind="Internal").ap()

    mask_sb = sb("mask_sb", [128, 128], BF16)
    idbf_sb = sb("idbf_sb", [128, 128], BF16)
    idf32_sb = sb("idf32_sb", [128, 128], F32)
    lnw_tmp = sb("lnw_tmp", [64, 128], F32)
    lnw_sb = sb("lnw_sb", [128, 64], F32)
    alnw_sb = sb("alnw_sb", [128, 64], F32)
    ones_1x128_f = sb("ones_1x128_f", [1, 128], F32)
    nxT = sb("nxT", [128, NCH * S], BF16)
    nhT = nxT
    sxinv_col = sb("sxinv_col", [128, NCH], F32)
    Bsx = sb("Bsx", [128, S], F32)
    sd_col = sb("sd_col", [128, NCH], F32)
    thr = {}
    for nm in ("g", "u", "d"):
        thr[nm] = dict(
            racc=sb(f"racc_{nm}", [128, 1], F32),
            a=sb(f"a_{nm}", [1, 1], F32),
            abc=sb(f"abc_{nm}", [128, 1], F32),
            anbc=sb(f"anbc_{nm}", [128, 1], F32),
        )
    st32 = [sb(f"st32_{g}", [128, 132], F32) for g in range(NKV)]
    sth = [sb(f"sth_{g}", [128, 132], BF16) for g in range(NKV)]
    stl = [sb(f"stl_{g}", [128, 132], BF16) for g in range(NKV)]

    p0 = sb("p0", [128, D], F32)
    p1 = sb("p1", [128, D], F32)
    xb0 = sb("xb0", [128, D], F32)
    fwbuf = [xb0, p0]
    nxbf = sb("nxbf", [128, D], BF16)
    ternbf = nxbf
    hsp_bf = sb("hsp_bf", [128, D], BF16)
    lsp_bf = sb("lsp_bf", [128, D], BF16)
    stat = [sb(f"stat{i}", [128, 1], F32) for i in range(8)]
    row0 = sb("row0", [1, 512], F32)
    ep = [sb(f"ep{i}", [128, 512], F32) for i in range(5)]
    ep32 = sb("epi32", [128, 512], I32)
    qro = sb("qro", [128, 512], F32)
    ssqsl = sb("ssqsl", [128, 512], F32)
    maxsl = sb("maxsl", [128, 512], F32)
    cos_sb = sb("cos_sb", [128, 128], F32)
    pb = [sb(f"pb{i}", [128, 512], F32) for i in range(2)]
    pbg = sb("pbg", [128, 512], F32)
    pb2 = sb("pb2", [128, 512], F32)
    pbt = sb("pbt", [128, 512], BF16)
    sin_sb = sb("sin_sb", [128, 128], F32)

    xnT = [sb(f"xnT_{p}", [128, NCH * GT], BF16) for p in "hl"]
    attT = xnT
    qfT = [sb(f"qfT_{p}", [128, NQH * GT], BF16) for p in "hl"]
    kfT = [sb(f"kfT_{p}", [128, NKV * GT], BF16) for p in "hl"]
    kftok = [sb(f"kftok_{p}", [128, GRP * 512], BF16) for p in "hl"]
    vaug = [sb(f"vaug_{p}", [128, GRP * NKV * 132], BF16) for p in "hl"]
    amT = [[sb(f"amT_{p}{i}", [128, 128], BF16) for p in "hl"] for i in range(2)]
    am32 = [sb(f"am32_{i}", [128, 128], F32) for i in range(2)]

    wth = [sb(f"wth{i}", [128, 1024], BF16) for i in range(8)]

    ps_acc = [psu(f"ps_acc{i}", [128, 512], F32) for i in range(4)]
    ps_tr = [psu(f"ps_tr{i}", [128, 128], BF16) for i in range(2)]
    ps_af = psu("ps_af", [128, 128], F32)
    ps_af2 = psu("ps_af2", [128, 128], F32)

    SC = sch.do

    SC("sync", lambda e: e.dma_start(out=mask_sb[:], in_=mask_h), writes=[mask_sb], dma="ld")
    SC("sync", lambda e: e.dma_start(out=idbf_sb[:], in_=idbf_h), writes=[idbf_sb], dma="ld")
    SC("sync", lambda e: e.dma_start(out=idf32_sb[:], in_=idf32_h), writes=[idf32_sb], dma="ld")
    SC("sync", lambda e: e.dma_start(out=lnw_tmp[:], in_=lnw_h.rearrange("(a b) -> a b", b=128)),
       writes=[lnw_tmp], dma="ld")
    SC("gpsimd", lambda e: e.memset(ones_1x128_f[:], 1.0), writes=[ones_1x128_f])
    for g in range(NKV):
        SC("gpsimd", lambda e, g=g: e.memset(st32[g][:], 0.0), writes=[st32[g]])
    SC("tensor", lambda e: e.transpose(ps_af[:, 0:64], lnw_tmp[:], idf32_sb[0:64, 0:64]),
       reads=[lnw_tmp, idf32_sb], writes=[ps_af])
    SC("vector", lambda e: e.tensor_copy(out=lnw_sb[:], in_=ps_af[:, 0:64]),
       reads=[ps_af], writes=[lnw_sb])
    SC("scalar", lambda e: e.activation(out=alnw_sb[:], in_=lnw_sb[:], func=AF.Abs),
       reads=[lnw_sb], writes=[alnw_sb])

    for (w_hbm, w_scr, nrow) in ((wq_h, ws_q, D), (wk_h, ws_k, 512),
                                 (wv_h, ws_v, 512), (wo_h, ws_o, D)):
        for t in range(nrow // 128):
            buf = fwbuf[t % 2]
            r0 = t * 128
            ob, rr = r0 // 512, r0 % 512
            SC("sync", lambda e, b=buf, h=w_hbm, r0=r0:
               e.dma_start(out=b[:], in_=h[r0:r0 + 128, :]), writes=[buf], dma="ld")
            SC("vector", lambda e, b=buf: e.tensor_copy(out=hsp_bf[:], in_=b[:]),
               reads=[buf], writes=[hsp_bf])
            SC("vector", lambda e, b=buf:
               e.tensor_tensor(out=p1[:], in0=b[:], in1=hsp_bf[:], op=ALU.subtract),
               reads=[buf, hsp_bf], writes=[p1])
            SC("vector", lambda e: e.tensor_copy(out=lsp_bf[:], in_=p1[:]),
               reads=[p1], writes=[lsp_bf])
            SC("scalar", lambda e, s=w_scr, ob=ob, rr=rr:
               e.dma_start(out=s[ob, 0, rr:rr + 128, :], in_=hsp_bf[:]), reads=[hsp_bf],
               writes=[w_scr], dma="st")
            SC("scalar", lambda e, s=w_scr, ob=ob, rr=rr:
               e.dma_start(out=s[ob, 1, rr:rr + 128, :], in_=lsp_bf[:]), reads=[lsp_bf],
               writes=[w_scr], dma="st")

    def ffn_subtiles(nrow, ncol):
        for r in range(nrow // 128):
            for cs in range(ncol // D):
                yield r * 128, cs * D

    ffn_mats = (("g", wg_h, 0, II, D), ("u", wu_h, 1, II, D), ("d", wd_h, None, D, II))
    ph1_work = []

    def slice_items(nrow, ncol):
        for r in range(nrow // 128):
            for cs in range(ncol // 512):
                yield r * 128, cs * 512

    pbi = [0]

    for nm, w_hbm, w_scr, nrow, ncol in ffn_mats:
        t = thr[nm]
        SC("gpsimd", lambda e, t=t: e.memset(t["racc"][:], 0.0), writes=[t["racc"]])

        def b_item(r0, c0, w_hbm=w_hbm, t=t):
            buf = pb[pbi[0] % 2]
            pbi[0] += 1
            SC("sync", lambda e, b=buf, h=w_hbm, r0=r0, c0=c0:
               e.dma_start(out=b[:], in_=h[r0:r0 + 128, c0:c0 + 512]),
               writes=[buf], dma="ld")
            SC("scalar", lambda e, b=buf:
               e.activation(out=pbg[:], in_=b[:], func=AF.Abs, accum_out=stat[6][:]),
               reads=[buf], writes=[pbg, stat[6]])
            SC("vector", lambda e, t=t:
               e.tensor_tensor(out=t["racc"][:], in0=t["racc"][:], in1=stat[6][:], op=ALU.add),
               reads=[stat[6], t["racc"]], writes=[t["racc"]])

        for r0, c0 in slice_items(nrow, ncol):
            ph1_work.append(lambda r0=r0, c0=c0, f=b_item: f(r0, c0))

        def b_fin(t=t, nrow=nrow, ncol=ncol):
            SC("tensor", lambda e, t=t:
               e.transpose(ps_acc[3][0:1, 0:128], t["racc"][:], idf32_sb[:]),
               reads=[t["racc"], idf32_sb], writes=[ps_acc[3]], tag="meanred")
            SC("vector", lambda e: e.tensor_copy(out=row0[0:1, 0:128], in_=ps_acc[3][0:1, 0:128]),
               reads=[ps_acc[3]], writes=[row0])
            nelem = float(nrow * ncol)
            SC("vector", lambda e, t=t:
               e.tensor_reduce(out=t["a"][:], in_=row0[0:1, 0:128], op=ALU.add, axis=AX.X),
               reads=[row0], writes=[t["a"]])
            SC("vector", lambda e, t=t, n=nelem:
               e.tensor_scalar(out=t["a"][:], in0=t["a"][:], scalar1=1.0 / n,
                               scalar2=W_EPS, op0=ALU.mult, op1=ALU.max),
               reads=[t["a"]], writes=[t["a"]])
            SC("vector", lambda e, t=t:
               e.tensor_scalar(out=t["a"][:], in0=t["a"][:], scalar1=0.5, scalar2=None,
                               op0=ALU.mult),
               reads=[t["a"]], writes=[t["a"]])
            SC("tensor", lambda e, t=t:
               e.matmul(ps_acc[3][:, 0:1], ones_1x128_f[:], t["a"][:], start=True, stop=True),
               reads=[ones_1x128_f, t["a"]], writes=[ps_acc[3]])
            SC("vector", lambda e, t=t: e.tensor_copy(out=t["abc"][:], in_=ps_acc[3][:, 0:1]),
               reads=[ps_acc[3]], writes=[t["abc"]])
            SC("vector", lambda e, t=t:
               e.tensor_scalar(out=t["anbc"][:], in0=t["abc"][:], scalar1=-1.0, scalar2=None,
                               op0=ALU.mult),
               reads=[t["abc"]], writes=[t["anbc"]])

        ph1_work.append(b_fin)

    for nm, w_hbm, w_scr, nrow, ncol in ffn_mats:
        t = thr[nm]

        def c_item(r0, c0, w_hbm=w_hbm, w_scr=w_scr, t=t):
            buf = pb[pbi[0] % 2]
            pbi[0] += 1
            SC("sync", lambda e, b=buf, h=w_hbm, r0=r0, c0=c0:
               e.dma_start(out=b[:], in_=h[r0:r0 + 128, c0:c0 + 512]),
               writes=[buf], dma="ld")
            SC("vector", lambda e, b=buf, t=t:
               e.tensor_scalar(out=pbg[:], in0=b[:], scalar1=t["abc"][:], scalar2=2.0,
                               op0=ALU.is_gt, op1=ALU.mult),
               reads=[buf, t["abc"]], writes=[pbg])
            SC("vector", lambda e, b=buf, t=t:
               e.tensor_scalar(out=pb2[:], in0=b[:], scalar1=t["anbc"][:], scalar2=2.0,
                               op0=ALU.is_lt, op1=ALU.mult),
               reads=[buf, t["anbc"]], writes=[pb2])
            SC("vector", lambda e:
               e.tensor_tensor(out=pbt[:], in0=pbg[:], in1=pb2[:], op=ALU.subtract),
               reads=[pbg, pb2], writes=[pbt])
            if w_scr is None:
                SC("scalar", lambda e, r0=r0, c0=c0:
                   e.dma_start(out=ws_d[r0:r0 + 128, c0:c0 + 512], in_=pbt[:]),
                   reads=[pbt], writes=[ws_d], dma="st")
            else:
                SC("scalar", lambda e, gu=w_scr, ic=r0 // 256, rr=r0 % 256, c0=c0:
                   e.dma_start(out=ws_gu[ic, gu, rr:rr + 128, c0:c0 + 512], in_=pbt[:]),
                   reads=[pbt], writes=[ws_gu], dma="st")

        for r0, c0 in slice_items(nrow, ncol):
            ph1_work.append(lambda r0=r0, c0=c0, f=c_item: f(r0, c0))

    def drain_ph1(n):
        for _ in range(min(n, len(ph1_work))):
            ph1_work.pop(0)()

    drain_ph1(10 ** 9)

    for grp in range(NGRP):
        for cl in range(GRP):
            c = grp * GRP + cl
            SC("sync", lambda e, r0=c * 128:
               e.dma_start(out=xb0[:], in_=x_h[r0:r0 + 128, :]), writes=[xb0], dma="ld")
            SC("vector", lambda e: e.tensor_reduce(out=stat[0][:], in_=xb0[:], op=ALU.add,
                                                   axis=AX.X),
               reads=[xb0], writes=[stat[0]])
            SC("scalar", lambda e:
               e.activation(out=p1[:], in_=xb0[:], func=AF.Square, accum_out=stat[1][:]),
               reads=[xb0], writes=[p1, stat[1]])
            SC("vector", lambda e: e.tensor_scalar(out=stat[2][:], in0=stat[0][:],
                                                   scalar1=1.0 / D, scalar2=None, op0=ALU.mult),
               reads=[stat[0]], writes=[stat[2]])
            SC("vector", lambda e: e.tensor_scalar(out=stat[3][:], in0=stat[1][:],
                                                   scalar1=1.0 / D, scalar2=LN_EPS,
                                                   op0=ALU.mult, op1=ALU.add),
               reads=[stat[1]], writes=[stat[3]])
            SC("vector", lambda e: e.tensor_tensor(out=stat[4][:], in0=stat[2][:], in1=stat[2][:],
                                                   op=ALU.mult),
               reads=[stat[2]], writes=[stat[4]])
            SC("vector", lambda e: e.tensor_tensor(out=stat[3][:], in0=stat[3][:], in1=stat[4][:],
                                                   op=ALU.subtract),
               reads=[stat[3], stat[4]], writes=[stat[3]])
            SC("scalar", lambda e: e.activation(out=stat[4][:], in_=stat[3][:], func=AF.Sqrt),
               reads=[stat[3]], writes=[stat[4]])
            SC("vector", lambda e: e.reciprocal(out=stat[5][:], in_=stat[4][:]),
               reads=[stat[4]], writes=[stat[5]])
            SC("vector", lambda e: e.tensor_tensor(out=stat[4][:], in0=stat[5][:], in1=stat[5][:],
                                                   op=ALU.mult),
               reads=[stat[5]], writes=[stat[4]])
            SC("vector", lambda e: e.tensor_tensor(out=stat[4][:], in0=stat[4][:], in1=stat[3][:],
                                                   op=ALU.mult),
               reads=[stat[4], stat[3]], writes=[stat[4]])
            SC("vector", lambda e: e.tensor_scalar(out=stat[4][:], in0=stat[4][:], scalar1=-0.5,
                                                   scalar2=1.5, op0=ALU.mult, op1=ALU.add),
               reads=[stat[4]], writes=[stat[4]])
            SC("vector", lambda e: e.tensor_tensor(out=stat[4][:], in0=stat[4][:], in1=stat[5][:],
                                                   op=ALU.mult),
               reads=[stat[4], stat[5]], writes=[stat[4]])
            SC("vector", lambda e: e.tensor_tensor(out=stat[5][:], in0=stat[2][:], in1=stat[4][:],
                                                   op=ALU.mult),
               reads=[stat[2], stat[4]], writes=[stat[5]])
            SC("vector", lambda e: e.tensor_scalar(out=stat[5][:], in0=stat[5][:], scalar1=-1.0,
                                                   scalar2=None, op0=ALU.mult),
               reads=[stat[5]], writes=[stat[5]])
            xn = p0
            SC("scalar", lambda e:
               e.activation(out=xn[:], in_=xb0[:], func=AF.Identity, bias=stat[5][:],
                            scale=stat[4][:]),
               reads=[xb0, stat[4], stat[5]], writes=[xn])
            SC("vector", lambda e: e.tensor_copy(out=hsp_bf[:], in_=xn[:]),
               reads=[xn], writes=[hsp_bf])
            SC("vector", lambda e: e.tensor_tensor(out=p1[:], in0=xn[:], in1=hsp_bf[:],
                                                   op=ALU.subtract),
               reads=[xn, hsp_bf], writes=[p1])
            SC("vector", lambda e: e.tensor_copy(out=lsp_bf[:], in_=p1[:]),
               reads=[p1], writes=[lsp_bf])
            SC("scalar", lambda e: e.activation(out=p1[:], in_=xn[:], func=AF.Abs),
               reads=[xn], writes=[p1])
            SC("vector", lambda e: e.tensor_reduce(out=stat[0][:], in_=p1[:], op=ALU.max,
                                                   axis=AX.X),
               reads=[p1], writes=[stat[0]])
            SC("vector", lambda e: e.tensor_scalar(out=stat[1][:], in0=stat[0][:], scalar1=Q_EPS,
                                                   scalar2=None, op0=ALU.max),
               reads=[stat[0]], writes=[stat[1]])
            SC("vector", lambda e, c=c: e.tensor_scalar(out=sxinv_col[:, c:c + 1], in0=stat[1][:],
                                                        scalar1=1.0 / 127.0, scalar2=None,
                                                        op0=ALU.mult),
               reads=[stat[1]], writes=[sxinv_col])
            SC("vector", lambda e: e.reciprocal(out=stat[2][:], in_=stat[1][:]),
               reads=[stat[1]], writes=[stat[2]])
            SC("vector", lambda e: e.tensor_scalar(out=stat[2][:], in0=stat[2][:], scalar1=127.0,
                                                   scalar2=None, op0=ALU.mult),
               reads=[stat[2]], writes=[stat[2]])
            for sl4 in range(4):
                cs = slice(sl4 * 512, (sl4 + 1) * 512)
                SC("vector", lambda e, cs=cs: e.tensor_scalar(out=ep[0][:], in0=xn[:, cs],
                                                              scalar1=stat[2][:], scalar2=None,
                                                              op0=ALU.mult),
                   reads=[xn, stat[2]], writes=[ep[0]])
                SC("vector", lambda e: e.tensor_scalar(out=ep32[:], in0=ep[0][:], scalar1=MAGIC,
                                                       scalar2=MAGIC, op0=ALU.add,
                                                       op1=ALU.subtract),
                   reads=[ep[0]], writes=[ep32])
                SC("gpsimd", lambda e, cs=cs: e.tensor_copy(out=nxbf[:, cs], in_=ep32[:]),
                   reads=[ep32], writes=[nxbf])
            for kc in range(16):
                bank = ps_tr[kc % 2]
                SC("tensor", lambda e, b=bank, kc=kc:
                   e.transpose(b[:], nxbf[:, kc * 128:(kc + 1) * 128], idbf_sb[:]),
                   reads=[nxbf, idbf_sb], writes=[bank])
                SC("scalar", lambda e, b=bank, kc=kc, c=c:
                   e.copy(out=nxT[:, kc * S + c * 128: kc * S + c * 128 + 128], in_=b[:]),
                   reads=[bank], writes=[nxT])
            for (src, dstT) in ((hsp_bf, xnT[0]), (lsp_bf, xnT[1])):
                for kc in range(16):
                    bank = ps_tr[kc % 2]
                    SC("tensor", lambda e, b=bank, s=src, kc=kc:
                       e.transpose(b[:], s[:, kc * 128:(kc + 1) * 128], idbf_sb[:]),
                       reads=[src, idbf_sb], writes=[bank])
                    SC("vector", lambda e, b=bank, d=dstT, kc=kc, cl=cl:
                       e.tensor_copy(out=d[:, kc * GT + cl * 128: kc * GT + cl * 128 + 128],
                                     in_=b[:]),
                       reads=[bank], writes=[dstT])

        def rope_elu(bank, c, dst):
            SC("sync", lambda e, c=c: e.dma_start(out=cos_sb[:],
                                                  in_=cos_h[c * 128:(c + 1) * 128, :]),
               writes=[cos_sb], dma="ld")
            SC("sync", lambda e, c=c: e.dma_start(out=sin_sb[:],
                                                  in_=sin_h[c * 128:(c + 1) * 128, :]),
               writes=[sin_sb], dma="ld")
            qc, qs, ro = ep[0], ep[1], qro
            for hh in range(4):
                sl = slice(hh * 128, (hh + 1) * 128)
                SC("vector", lambda e, b=bank, sl=sl:
                   e.tensor_tensor(out=qc[:, sl], in0=b[:, sl], in1=cos_sb[:], op=ALU.mult),
                   reads=[bank, cos_sb], writes=[qc])
                SC("vector", lambda e, b=bank, sl=sl:
                   e.tensor_tensor(out=qs[:, sl], in0=b[:, sl], in1=sin_sb[:], op=ALU.mult),
                   reads=[bank, sin_sb], writes=[qs])
            for hh in range(4):
                b0 = hh * 128
                SC("vector", lambda e, b0=b0:
                   e.tensor_tensor(out=ro[:, b0:b0 + 64], in0=qc[:, b0:b0 + 64],
                                   in1=qs[:, b0 + 64:b0 + 128], op=ALU.subtract),
                   reads=[qc, qs], writes=[ro])
                SC("vector", lambda e, b0=b0:
                   e.tensor_tensor(out=ro[:, b0 + 64:b0 + 128], in0=qc[:, b0 + 64:b0 + 128],
                                   in1=qs[:, b0:b0 + 64], op=ALU.add),
                   reads=[qc, qs], writes=[ro])
            SC("vector", lambda e: e.tensor_scalar(out=ep[2][:], in0=ro[:], scalar1=0.0,
                                                   scalar2=None, op0=ALU.min),
               reads=[ro], writes=[ep[2]])
            SC("scalar", lambda e: e.activation(out=ep[3][:], in_=ep[2][:], func=AF.Exp),
               reads=[ep[2]], writes=[ep[3]])
            SC("vector", lambda e: e.scalar_tensor_tensor(out=dst[:], in0=ro[:], scalar=0.0,
                                                          in1=ep[3][:], op0=ALU.max, op1=ALU.add),
               reads=[ro, ep[3]], writes=[dst])

        def proj(w_scr, nblk, evac, tag):
            for ob in range(nblk):
                for kph in range(2):
                    for j in range(8):
                        kc = kph * 8 + j
                        SC("sync", lambda e, w=wth[j], s=w_scr, ob=ob, kc=kc:
                           e.dma_start_transpose(
                               w[:], s[ob].rearrange("a b c -> (a b) c")[
                                   :, kc * 128:(kc + 1) * 128]),
                           reads=[w_scr], writes=[wth[j]], dma="ld", tag=tag + "_wld")
                    for tb in range(GRP):
                        bank = ps_acc[tb]
                        for j in range(8):
                            kc = kph * 8 + j
                            xh_t = xnT[0][:, kc * GT + tb * 128: kc * GT + tb * 128 + 128]
                            xl_t = xnT[1][:, kc * GT + tb * 128: kc * GT + tb * 128 + 128]
                            SC("tensor", lambda e, b=bank, w=wth[j], xt=xh_t,
                               f=(kph == 0 and j == 0):
                               e.matmul(b[:], xt, w[:, 0:512], start=f, stop=False),
                               reads=[xnT[0], wth[j]], writes=[bank], tag=tag + "_mm")
                            SC("tensor", lambda e, b=bank, w=wth[j], xt=xh_t:
                               e.matmul(b[:], xt, w[:, 512:1024], start=False, stop=False),
                               reads=[xnT[0], wth[j]], writes=[bank])
                            SC("tensor", lambda e, b=bank, w=wth[j], xt=xl_t,
                               l=(kph == 1 and j == 7):
                               e.matmul(b[:], xt, w[:, 0:512], start=False, stop=l),
                               reads=[xnT[1], wth[j]], writes=[bank])
                for tb in range(GRP):
                    evac(tb, ob, ps_acc[tb])

        def q_evac(tb, ob, bank):
            c = grp * GRP + tb
            rope_elu(bank, c, qro)
            qh = nxbf
            SC("vector", lambda e: e.tensor_copy(out=qh[:, 0:512], in_=qro[:]),
               reads=[qro], writes=[qh])
            SC("vector", lambda e: e.tensor_tensor(out=ep[4][:], in0=qro[:], in1=qh[:, 0:512],
                                                   op=ALU.subtract),
               reads=[qro, qh], writes=[ep[4]])
            SC("vector", lambda e: e.tensor_copy(out=qh[:, 512:1024], in_=ep[4][:]),
               reads=[ep[4]], writes=[qh])
            for hh in range(4):
                hg = ob * 4 + hh
                for pi in range(2):
                    bank2 = ps_tr[(hh * 2 + pi) % 2]
                    SC("tensor", lambda e, b=bank2, pi=pi, hh=hh:
                       e.transpose(b[:], qh[:, pi * 512 + hh * 128: pi * 512 + hh * 128 + 128],
                                   idbf_sb[:]),
                       reads=[qh, idbf_sb], writes=[bank2])
                    SC("vector", lambda e, b=bank2, pi=pi, hg=hg, tb=tb:
                       e.tensor_copy(out=qfT[pi][:, hg * GT + tb * 128: hg * GT + tb * 128 + 128],
                                     in_=b[:]),
                       reads=[bank2], writes=[qfT[pi]])

        def k_evac(tb, ob, bank):
            c = grp * GRP + tb
            rope_elu(bank, c, qro)
            SC("vector", lambda e, tb=tb: e.tensor_copy(out=kftok[0][:, tb * 512:(tb + 1) * 512],
                                                        in_=qro[:]),
               reads=[qro], writes=[kftok[0]])
            SC("vector", lambda e, tb=tb:
               e.tensor_tensor(out=ep[4][:], in0=qro[:],
                               in1=kftok[0][:, tb * 512:(tb + 1) * 512], op=ALU.subtract),
               reads=[qro, kftok[0]], writes=[ep[4]])
            SC("vector", lambda e, tb=tb: e.tensor_copy(out=kftok[1][:, tb * 512:(tb + 1) * 512],
                                                        in_=ep[4][:]),
               reads=[ep[4]], writes=[kftok[1]])
            for hh in range(NKV):
                for pi in range(2):
                    bank2 = ps_tr[(hh * 2 + pi) % 2]
                    SC("tensor", lambda e, b=bank2, pi=pi, hh=hh, tb=tb:
                       e.transpose(b[:],
                                   kftok[pi][:, tb * 512 + hh * 128: tb * 512 + hh * 128 + 128],
                                   idbf_sb[:]),
                       reads=[kftok[pi], idbf_sb], writes=[bank2])
                    SC("vector", lambda e, b=bank2, pi=pi, hh=hh, tb=tb:
                       e.tensor_copy(out=kfT[pi][:, hh * GT + tb * 128: hh * GT + tb * 128 + 128],
                                     in_=b[:]),
                       reads=[bank2], writes=[kfT[pi]])

        def v_evac(tb, ob, bank):
            SC("vector", lambda e, b=bank: e.tensor_copy(out=qro[:], in_=b[:]),
               reads=[bank], writes=[qro])
            for g in range(NKV):
                slot = (tb * NKV + g) * 132
                SC("vector", lambda e, g=g, slot=slot:
                   e.tensor_copy(out=vaug[0][:, slot:slot + 128],
                                 in_=qro[:, g * 128:(g + 1) * 128]),
                   reads=[qro], writes=[vaug[0]])
                SC("vector", lambda e, g=g, slot=slot:
                   e.tensor_tensor(out=ep[4][:, 0:128], in0=qro[:, g * 128:(g + 1) * 128],
                                   in1=vaug[0][:, slot:slot + 128], op=ALU.subtract),
                   reads=[qro, vaug[0]], writes=[ep[4]])
                SC("vector", lambda e, slot=slot:
                   e.tensor_copy(out=vaug[1][:, slot:slot + 128], in_=ep[4][:, 0:128]),
                   reads=[ep[4]], writes=[vaug[1]])
                SC("gpsimd", lambda e, slot=slot: e.memset(vaug[0][:, slot + 128:slot + 129], 1.0),
                   writes=[vaug[0]])
                SC("gpsimd", lambda e, slot=slot: e.memset(vaug[1][:, slot + 128:slot + 129], 0.0),
                   writes=[vaug[1]])

        proj(ws_q, 4, q_evac, "qproj")
        proj(ws_k, 1, k_evac, "kproj")
        proj(ws_v, 1, v_evac, "vproj")
        if DEBUG_SINK:
            DEBUG_SINK(f"proj{grp}", nc, SC, dict(qfT=qfT, kfT=kfT, kftok=kftok,
                                                  vaug=vaug, xnT=xnT))

        for cl in range(GRP):
            c = grp * GRP + cl
            att_t = p0
            for h in range(NQH):
                g = h // 4
                pa = ps_af if h % 2 == 0 else ps_af2
                amt = amT[h % 2]
                a32 = am32[h % 2]
                nb = ps_acc[1] if h % 2 == 0 else ps_acc[3]
                ks = [kfT[p][:, g * GT + cl * 128: g * GT + cl * 128 + 128] for p in range(2)]
                qs_ = [qfT[p][:, h * GT + cl * 128: h * GT + cl * 128 + 128] for p in range(2)]
                SC("tensor", lambda e, pa=pa, a=ks[0], b=qs_[0]:
                   e.matmul(pa[:], a, b, start=True, stop=False),
                   reads=[kfT[0], qfT[0]], writes=[pa])
                SC("tensor", lambda e, pa=pa, a=ks[0], b=qs_[1]:
                   e.matmul(pa[:], a, b, start=False, stop=False),
                   reads=[kfT[0], qfT[1]], writes=[pa])
                SC("tensor", lambda e, pa=pa, a=ks[1], b=qs_[0]:
                   e.matmul(pa[:], a, b, start=False, stop=True),
                   reads=[kfT[1], qfT[0]], writes=[pa])
                SC("vector", lambda e, pa=pa, a32=a32:
                   e.tensor_tensor(out=a32[:], in0=pa[:], in1=mask_sb[:], op=ALU.mult),
                   reads=[pa, mask_sb], writes=[a32])
                SC("vector", lambda e, a32=a32, amt=amt: e.tensor_copy(out=amt[0][:], in_=a32[:]),
                   reads=[a32], writes=[amt[0]])
                SC("vector", lambda e, a32=a32, amt=amt:
                   e.tensor_tensor(out=ep[4][:, 0:128], in0=a32[:], in1=amt[0][:],
                                   op=ALU.subtract),
                   reads=[a32, amt[0]], writes=[ep[4]])
                SC("vector", lambda e, amt=amt: e.tensor_copy(out=amt[1][:], in_=ep[4][:, 0:128]),
                   reads=[ep[4]], writes=[amt[1]])
                vslot = (cl * NKV + g) * 132
                if c > 0:
                    SC("tensor", lambda e, nb=nb, q=qs_[0], g=g:
                       e.matmul(nb[:, 0:129], q, sth[g][:, 0:129], start=True, stop=False),
                       reads=[qfT[0], sth[g]], writes=[nb])
                    SC("tensor", lambda e, nb=nb, q=qs_[0], g=g:
                       e.matmul(nb[:, 0:129], q, stl[g][:, 0:129], start=False, stop=False),
                       reads=[qfT[0], stl[g]], writes=[nb])
                    SC("tensor", lambda e, nb=nb, q=qs_[1], g=g:
                       e.matmul(nb[:, 0:129], q, sth[g][:, 0:129], start=False, stop=False),
                       reads=[qfT[1], sth[g]], writes=[nb])
                    fi = False
                else:
                    fi = True
                SC("tensor", lambda e, nb=nb, amt=amt, vs=vslot, f=fi:
                   e.matmul(nb[:, 0:129], amt[0][:], vaug[0][:, vs:vs + 129], start=f, stop=False),
                   reads=[amt[0], vaug[0]], writes=[nb])
                SC("tensor", lambda e, nb=nb, amt=amt, vs=vslot:
                   e.matmul(nb[:, 0:129], amt[0][:], vaug[1][:, vs:vs + 129], start=False,
                            stop=False),
                   reads=[amt[0], vaug[1]], writes=[nb])
                SC("tensor", lambda e, nb=nb, amt=amt, vs=vslot:
                   e.matmul(nb[:, 0:129], amt[1][:], vaug[0][:, vs:vs + 129], start=False,
                            stop=True),
                   reads=[amt[1], vaug[0]], writes=[nb])
                st_d = stat[0] if h % 2 == 0 else stat[2]
                st_r = stat[1] if h % 2 == 0 else stat[3]
                SC("vector", lambda e, nb=nb, sd=st_d:
                   e.tensor_scalar(out=sd[:], in0=nb[:, 128:129],
                                   scalar1=ATT_EPS, scalar2=None, op0=ALU.add),
                   reads=[nb], writes=[st_d])
                SC("vector", lambda e, sd=st_d, sr=st_r: e.reciprocal(out=sr[:], in_=sd[:]),
                   reads=[st_d], writes=[st_r])
                SC("vector", lambda e, nb=nb, sr=st_r, h=h:
                   e.tensor_scalar(out=att_t[:, h * 128:(h + 1) * 128], in0=nb[:, 0:128],
                                   scalar1=sr[:], scalar2=None, op0=ALU.mult),
                   reads=[nb, st_r], writes=[att_t])
            for g in range(NKV):
                vslot = (cl * NKV + g) * 132
                kt = [kftok[p][:, cl * 512 + g * 128: cl * 512 + g * 128 + 128] for p in range(2)]
                db = ps_acc[0]
                SC("tensor", lambda e, k=kt[0], vs=vslot:
                   e.matmul(db[:, 0:129], k, vaug[0][:, vs:vs + 129], start=True, stop=False),
                   reads=[kftok[0], vaug[0]], writes=[db])
                SC("tensor", lambda e, k=kt[0], vs=vslot:
                   e.matmul(db[:, 0:129], k, vaug[1][:, vs:vs + 129], start=False, stop=False),
                   reads=[kftok[0], vaug[1]], writes=[db])
                SC("tensor", lambda e, k=kt[1], vs=vslot:
                   e.matmul(db[:, 0:129], k, vaug[0][:, vs:vs + 129], start=False, stop=True),
                   reads=[kftok[1], vaug[0]], writes=[db])
                SC("vector", lambda e, g=g:
                   e.tensor_tensor(out=st32[g][:, 0:129], in0=st32[g][:, 0:129],
                                   in1=db[:, 0:129], op=ALU.add),
                   reads=[st32[g], db], writes=[st32[g]])
                SC("vector", lambda e, g=g: e.tensor_copy(out=sth[g][:, 0:129],
                                                          in_=st32[g][:, 0:129]),
                   reads=[st32[g]], writes=[sth[g]])
                SC("vector", lambda e, g=g:
                   e.tensor_tensor(out=ep[2][:, 0:129], in0=st32[g][:, 0:129],
                                   in1=sth[g][:, 0:129], op=ALU.subtract),
                   reads=[st32[g], sth[g]], writes=[ep[2]])
                SC("vector", lambda e, g=g: e.tensor_copy(out=stl[g][:, 0:129],
                                                          in_=ep[2][:, 0:129]),
                   reads=[ep[2]], writes=[stl[g]])
            if DEBUG_SINK:
                DEBUG_SINK(f"att{c}", nc, SC, dict(att_t=att_t, st32=st32))
            SC("vector", lambda e: e.tensor_copy(out=hsp_bf[:], in_=att_t[:]),
               reads=[att_t], writes=[hsp_bf])
            SC("vector", lambda e: e.tensor_tensor(out=p1[:], in0=att_t[:], in1=hsp_bf[:],
                                                   op=ALU.subtract),
               reads=[att_t, hsp_bf], writes=[p1])
            SC("vector", lambda e: e.tensor_copy(out=lsp_bf[:], in_=p1[:]),
               reads=[p1], writes=[lsp_bf])
            for (src, dstT) in ((hsp_bf, attT[0]), (lsp_bf, attT[1])):
                for kc in range(16):
                    bank = ps_tr[kc % 2]
                    SC("tensor", lambda e, b=bank, s=src, kc=kc:
                       e.transpose(b[:], s[:, kc * 128:(kc + 1) * 128], idbf_sb[:]),
                       reads=[src, idbf_sb], writes=[bank])
                    SC("scalar", lambda e, b=bank, d=dstT, kc=kc, cl=cl:
                       e.copy(out=d[:, kc * GT + cl * 128: kc * GT + cl * 128 + 128], in_=b[:]),
                       reads=[bank], writes=[dstT])

        for ob in range(4):
            for kph in range(2):
                for j in range(8):
                    kc = kph * 8 + j
                    SC("sync", lambda e, w=wth[j], ob=ob, kc=kc:
                       e.dma_start_transpose(
                           w[:], ws_o[ob].rearrange("a b c -> (a b) c")[
                               :, kc * 128:(kc + 1) * 128]),
                       reads=[ws_o], writes=[wth[j]], dma="ld", tag="oproj_wld")
                for tb in range(GRP):
                    bank = ps_acc[tb]
                    for j in range(8):
                        kc = kph * 8 + j
                        ah_t = attT[0][:, kc * GT + tb * 128: kc * GT + tb * 128 + 128]
                        al_t = attT[1][:, kc * GT + tb * 128: kc * GT + tb * 128 + 128]
                        SC("tensor", lambda e, b=bank, w=wth[j], xt=ah_t,
                           f=(kph == 0 and j == 0):
                           e.matmul(b[:], xt, w[:, 0:512], start=f, stop=False),
                           reads=[attT[0], wth[j]], writes=[bank], tag="oproj_mm")
                        SC("tensor", lambda e, b=bank, w=wth[j], xt=ah_t:
                           e.matmul(b[:], xt, w[:, 512:1024], start=False, stop=False),
                           reads=[attT[0], wth[j]], writes=[bank])
                        SC("tensor", lambda e, b=bank, w=wth[j], xt=al_t,
                           l=(kph == 1 and j == 7):
                           e.matmul(b[:], xt, w[:, 0:512], start=False, stop=l),
                           reads=[attT[1], wth[j]], writes=[bank])
            for tb in range(GRP):
                bank = ps_acc[tb]
                SC("vector", lambda e, b=bank: e.tensor_copy(out=ep[0][:], in_=b[:]),
                   reads=[bank], writes=[ep[0]])
                SC("scalar", lambda e, ob=ob, tb=tb, grp=grp:
                   e.dma_start(out=att_h[grp * GT + tb * 128: grp * GT + tb * 128 + 128,
                                         ob * 512:(ob + 1) * 512], in_=ep[0][:]),
                   reads=[ep[0]], writes=[att_h], dma="st", tag="att_spill")

    for c in range(NCH):
        SC("scalar", lambda e, c=c: e.dma_start(out=rowscr_h[0, c * 128:(c + 1) * 128],
                                                in_=sxinv_col[:, c:c + 1]),
           reads=[sxinv_col], writes=[rowscr_h], dma="st")
    for bb in range(4):
        SC("sync", lambda e, bb=bb: e.dma_start(out=row0[:],
                                                in_=rowscr_h[0:1, bb * 512:(bb + 1) * 512]),
           reads=[rowscr_h], writes=[row0], dma="ld")
        SC("tensor", lambda e, bb=bb:
           e.matmul(ps_acc[3][:], ones_1x128_f[:], row0[0:1, 0:512],
                    start=True, stop=True),
           reads=[ones_1x128_f, row0], writes=[ps_acc[3]])
        SC("vector", lambda e, bb=bb: e.tensor_copy(out=Bsx[:, bb * 512:(bb + 1) * 512],
                                                    in_=ps_acc[3][:]),
           reads=[ps_acc[3]], writes=[Bsx])

    def emit_gu_epilogue(ic, ib, tb, bg, bu):
        ts0 = tb * 512
        ig = ic * 2 + ib
        SC("vector", lambda e, b=bg, ts0=ts0:
           e.scalar_tensor_tensor(out=ep[0][:], in0=b[:], scalar=thr["g"]["abc"][:],
                          in1=Bsx[:, ts0:ts0 + 512], op0=ALU.mult, op1=ALU.mult),
           reads=[bg, thr["g"]["abc"], Bsx], writes=[ep[0]])
        SC("scalar", lambda e: e.activation(out=ep[1][:], in_=ep[0][:], func=AF.Silu),
           reads=[ep[0]], writes=[ep[1]])
        SC("vector", lambda e, b=bu, ts0=ts0:
           e.scalar_tensor_tensor(out=ep[0][:], in0=b[:], scalar=thr["u"]["abc"][:],
                          in1=Bsx[:, ts0:ts0 + 512], op0=ALU.mult, op1=ALU.mult),
           reads=[bu, thr["u"]["abc"], Bsx], writes=[ep[0]])
        SC("vector", lambda e: e.tensor_tensor(out=ep[2][:], in0=ep[1][:], in1=ep[0][:],
                               op=ALU.mult),
           reads=[ep[1], ep[0]], writes=[ep[2]])
        if DEBUG_SINK and ic == 0 and ib == 0 and tb == 0:
            DEBUG_SINK("ffn0", nc, SC, dict(gt_silu=ep[1], ut=ep[0], Bsx=Bsx))
        SC("scalar", lambda e, ig=ig, ts0=ts0:
           e.dma_start(out=pt_h[ig * 128:(ig + 1) * 128, ts0:ts0 + 512], in_=ep[2][:]),
           reads=[ep[2]], writes=[pt_h], dma="st", tag="p_spill")

    def gu_tile(kc):
        return wth[kc // 2][:, (kc % 2) * 512:(kc % 2) * 512 + 512]

    for ic in range(32):
        for kc in range(16):
            SC("sync", lambda e, kc=kc, ic=ic:
               e.dma_start_transpose(
                   gu_tile(kc), ws_gu[ic].rearrange("a b c -> (a b) c")[
                       :, kc * 128:(kc + 1) * 128]),
               reads=[ws_gu], writes=[wth[kc // 2]], dma="ld", tag="wg_ld")
        for ib in range(2):
            ig = ic * 2 + ib
            for tb in range(4):
                par = (ib * 4 + tb) % 2
                bg, bu = ps_acc[0 + 2 * par], ps_acc[1 + 2 * par]
                for kc in range(16):
                    SC("tensor", lambda e, b=bg, kc=kc, ib=ib, tb=tb,
                       f=(kc == 0), l=(kc == 15):
                       e.matmul(b[:], gu_tile(kc)[:, ib * 128:(ib + 1) * 128],
                                nxT[:, kc * S + tb * 512: kc * S + (tb + 1) * 512],
                                start=f, stop=l),
                       reads=[wth[kc // 2], nxT], writes=[bg], tag="gmm")
                for kc in range(16):
                    SC("tensor", lambda e, b=bu, kc=kc, ib=ib, tb=tb,
                       f=(kc == 0), l=(kc == 15):
                       e.matmul(b[:], gu_tile(kc)[:, 256 + ib * 128: 256 + (ib + 1) * 128],
                                nxT[:, kc * S + tb * 512: kc * S + (tb + 1) * 512],
                                start=f, stop=l),
                       reads=[wth[kc // 2], nxT], writes=[bu], tag="umm")
                emit_gu_epilogue(ic, ib, tb, bg, bu)

    def emit_alpha(tsl):
        ts0 = tsl * 512
        SC("gpsimd", lambda e: e.memset(ssqsl[:], 0.0), writes=[ssqsl])
        SC("gpsimd", lambda e: e.memset(maxsl[:], 0.0), writes=[maxsl])
        for ig in range(64):
            SC("sync", lambda e, ig=ig, ts0=ts0:
               e.dma_start(out=ep[4][:], in_=pt_h[ig * 128:(ig + 1) * 128, ts0:ts0 + 512]),
               reads=[pt_h], writes=[ep[4]], dma="ld", tag="p_ld1")
            SC("scalar", lambda e: e.activation(out=ep[0][:], in_=ep[4][:], func=AF.Square),
               reads=[ep[4]], writes=[ep[0]])
            SC("vector", lambda e: e.tensor_tensor(out=ssqsl[:], in0=ssqsl[:], in1=ep[0][:],
                                                   op=ALU.add),
               reads=[ssqsl, ep[0]], writes=[ssqsl])
            SC("scalar", lambda e: e.activation(out=ep[0][:], in_=ep[4][:], func=AF.Abs),
               reads=[ep[4]], writes=[ep[0]])
            SC("vector", lambda e, ig=ig:
               e.scalar_tensor_tensor(out=maxsl[:], in0=ep[0][:], scalar=alnw_sb[:, ig:ig + 1],
                                      in1=maxsl[:], op0=ALU.mult, op1=ALU.max),
               reads=[ep[0], alnw_sb, maxsl], writes=[maxsl])

    def emit_stats(tsl):
        ts0 = tsl * 512
        for j in range(4):
            c = tsl * 4 + j
            SC("tensor", lambda e, j=j: e.transpose(ps_af[:], ssqsl[:, j * 128:(j + 1) * 128],
                                                    idf32_sb[:]),
               reads=[ssqsl, idf32_sb], writes=[ps_af])
            SC("vector", lambda e: e.tensor_copy(out=ep[0][:, 0:128], in_=ps_af[:]),
               reads=[ps_af], writes=[ep[0]])
            SC("vector", lambda e: e.tensor_reduce(out=stat[0][:], in_=ep[0][:, 0:128],
                                                   op=ALU.add, axis=AX.X),
               reads=[ep[0]], writes=[stat[0]])
            SC("tensor", lambda e, j=j: e.transpose(ps_af[:], maxsl[:, j * 128:(j + 1) * 128],
                                                    idf32_sb[:]),
               reads=[maxsl, idf32_sb], writes=[ps_af])
            SC("vector", lambda e: e.tensor_copy(out=ep[0][:, 0:128], in_=ps_af[:]),
               reads=[ps_af], writes=[ep[0]])
            SC("vector", lambda e: e.tensor_reduce(out=stat[1][:], in_=ep[0][:, 0:128],
                                                   op=ALU.max, axis=AX.X),
               reads=[ep[0]], writes=[stat[1]])
            SC("vector", lambda e: e.tensor_scalar(out=stat[2][:], in0=stat[0][:],
                                                   scalar1=1.0 / II, scalar2=RMS_EPS,
                                                   op0=ALU.mult, op1=ALU.add),
               reads=[stat[0]], writes=[stat[2]])
            SC("scalar", lambda e: e.activation(out=stat[3][:], in_=stat[2][:], func=AF.Sqrt),
               reads=[stat[2]], writes=[stat[3]])
            SC("vector", lambda e: e.reciprocal(out=stat[4][:], in_=stat[3][:]),
               reads=[stat[3]], writes=[stat[4]])
            SC("vector", lambda e: e.tensor_tensor(out=stat[3][:], in0=stat[4][:], in1=stat[4][:],
                                                   op=ALU.mult),
               reads=[stat[4]], writes=[stat[3]])
            SC("vector", lambda e: e.tensor_tensor(out=stat[3][:], in0=stat[3][:], in1=stat[2][:],
                                                   op=ALU.mult),
               reads=[stat[3], stat[2]], writes=[stat[3]])
            SC("vector", lambda e: e.tensor_scalar(out=stat[3][:], in0=stat[3][:], scalar1=-0.5,
                                                   scalar2=1.5, op0=ALU.mult, op1=ALU.add),
               reads=[stat[3]], writes=[stat[3]])
            SC("vector", lambda e: e.tensor_tensor(out=stat[3][:], in0=stat[3][:], in1=stat[4][:],
                                                   op=ALU.mult),
               reads=[stat[3], stat[4]], writes=[stat[3]])
            SC("vector", lambda e: e.tensor_tensor(out=stat[1][:], in0=stat[1][:], in1=stat[3][:],
                                                   op=ALU.mult),
               reads=[stat[1], stat[3]], writes=[stat[1]])
            SC("vector", lambda e: e.tensor_scalar(out=stat[1][:], in0=stat[1][:], scalar1=Q_EPS,
                                                   scalar2=None, op0=ALU.max),
               reads=[stat[1]], writes=[stat[1]])
            SC("vector", lambda e: e.reciprocal(out=stat[4][:], in_=stat[1][:]),
               reads=[stat[1]], writes=[stat[4]])
            SC("vector", lambda e: e.tensor_scalar(out=stat[4][:], in0=stat[4][:], scalar1=127.0,
                                                   scalar2=None, op0=ALU.mult),
               reads=[stat[4]], writes=[stat[4]])
            SC("vector", lambda e: e.tensor_tensor(out=stat[5][:], in0=stat[3][:], in1=stat[4][:],
                                                   op=ALU.mult),
               reads=[stat[3], stat[4]], writes=[stat[5]])
            SC("scalar", lambda e, c=c: e.dma_start(out=rowscr_h[3, c * 128:(c + 1) * 128],
                                                    in_=stat[5][:]),
               reads=[stat[5]], writes=[rowscr_h], dma="st")
            SC("vector", lambda e: e.tensor_scalar(out=stat[6][:], in0=stat[1][:],
                                                   scalar1=1.0 / 127.0, scalar2=None,
                                                   op0=ALU.mult),
               reads=[stat[1]], writes=[stat[6]])
            SC("vector", lambda e, c=c:
               e.tensor_tensor(out=sd_col[:, c:c + 1], in0=stat[6][:], in1=thr["d"]["abc"][:],
                               op=ALU.mult),
               reads=[stat[6], thr["d"]["abc"]], writes=[sd_col])
        SC("sync", lambda e, ts0=ts0: e.dma_start(out=row0[0:1, 0:512],
                                                  in_=rowscr_h[3, ts0:ts0 + 512]),
           reads=[rowscr_h], writes=[row0], dma="ld")
        SC("tensor", lambda e:
           e.matmul(ps_acc[3][:], ones_1x128_f[:], row0[0:1, 0:512], start=True, stop=True),
           reads=[ones_1x128_f, row0], writes=[ps_acc[3]])
        chs = ssqsl
        SC("vector", lambda e: e.tensor_copy(out=chs[:], in_=ps_acc[3][:]),
           reads=[ps_acc[3]], writes=[chs])

    def emit_beta(tsl):
        ts0 = tsl * 512
        chs = ssqsl
        for ig in range(64):
            SC("sync", lambda e, ig=ig, ts0=ts0:
               e.dma_start(out=ep[4][:], in_=pt_h[ig * 128:(ig + 1) * 128, ts0:ts0 + 512]),
               reads=[pt_h], writes=[ep[4]], dma="ld", tag="p_ld2")
            SC("vector", lambda e, ig=ig:
               e.scalar_tensor_tensor(out=ep[0][:], in0=ep[4][:], scalar=lnw_sb[:, ig:ig + 1],
                                      in1=chs[:], op0=ALU.mult, op1=ALU.mult),
               reads=[ep[4], lnw_sb, chs], writes=[ep[0]])
            SC("vector", lambda e: e.tensor_scalar(out=ep32[:], in0=ep[0][:], scalar1=MAGIC,
                                                   scalar2=MAGIC, op0=ALU.add, op1=ALU.subtract),
               reads=[ep[0]], writes=[ep32])
            SC("vector", lambda e, ig=ig: e.tensor_copy(out=nhT[:, ig * 512:(ig + 1) * 512],
                                                        in_=ep32[:]),
               reads=[ep32], writes=[nhT], tag="nh_cast")

    def emit_down(tsl):
        ts0 = tsl * 512
        for ob in range(4):
            for kc in range(64):
                wt_ = wth[kc % 8]
                q, qe = "ld", "sync"
                SC(qe, lambda e, w=wt_, ob=ob, kc=kc:
                   e.dma_start_transpose(w[:, 0:512], ws_d[ob * 512:(ob + 1) * 512,
                                                          kc * 128:(kc + 1) * 128]),
                   reads=[ws_d], writes=[wt_], dma=q, tag="wd_ld")
                for tb in range(4):
                    SC("tensor", lambda e, w=wt_, kc=kc, tb=tb, b=ps_acc[tb],
                       f=(kc == 0), l=(kc == 63):
                       e.matmul(b[:], nhT[:, kc * 512 + tb * 128: kc * 512 + tb * 128 + 128],
                                w[:, 0:512], start=f, stop=l),
                       reads=[nhT, wt_], writes=[ps_acc[tb]], tag="dmm")
            for tb in range(4):
                bank = ps_acc[tb]
                c = tsl * 4 + tb
                SC("sync", lambda e, c=c, ob=ob:
                   e.dma_start(out=ep[1][:], in_=x_h[c * 128:(c + 1) * 128,
                                                     ob * 512:(ob + 1) * 512]),
                   writes=[ep[1]], dma="ld", tag="x_reload")
                SC("sync", lambda e, c=c, ob=ob:
                   e.dma_start(out=ep[2][:], in_=att_h[c * 128:(c + 1) * 128,
                                                       ob * 512:(ob + 1) * 512]),
                   reads=[att_h], writes=[ep[2]], dma="ld", tag="att_reload")
                SC("vector", lambda e, b=bank, c=c:
                   e.tensor_scalar(out=ep[3][:], in0=b[:], scalar1=sd_col[:, c:c + 1],
                                   scalar2=None, op0=ALU.mult),
                   reads=[bank, sd_col], writes=[ep[3]])
                SC("vector", lambda e: e.tensor_tensor(out=ep[2][:], in0=ep[1][:], in1=ep[2][:],
                                                       op=ALU.add),
                   reads=[ep[1], ep[2]], writes=[ep[2]])
                SC("vector", lambda e: e.tensor_tensor(out=ep[2][:], in0=ep[3][:], in1=ep[2][:],
                                                       op=ALU.add),
                   reads=[ep[3], ep[2]], writes=[ep[2]])
                SC("scalar", lambda e, c=c, ob=ob:
                   e.dma_start(out=out_h[c * 128:(c + 1) * 128, ob * 512:(ob + 1) * 512],
                               in_=ep[2][:]),
                   reads=[ep[2]], dma="st", tag="out_st")


    for tsl in range(4):
        emit_alpha(tsl)
        emit_stats(tsl)
        if tsl > 0:
            emit_down(tsl - 1)
        emit_beta(tsl)
    emit_down(3)

    _probe = []
    import os as _os
    _pk = int(_os.environ.get("SBUF_PROBE_KB", "0"))
    if _pk:
        _probe.append(sb("probe", [128, _pk * 256], F32))
    last_store = sch.steps[-1]
    term = Step("gpsimd", lambda e: e.engine_nop(), [], [], None, len(sch.steps), "term")
    term.waits.append(last_store)
    sch.steps.append(term)

    sch.resolve_and_emit(nc)
    stack.close()
    return nc


def _host_constants():
    j = np.arange(0, HD, 2, dtype=np.float64)
    inv_freq = 1.0 / (10000.0 ** (j / HD))
    t = np.arange(S, dtype=np.float64)
    freqs = np.outer(t, inv_freq)
    emb = np.concatenate([freqs, freqs], axis=-1)
    cos = np.cos(emb).astype(np.float32)
    sin = np.sin(emb).astype(np.float32)
    ii = np.arange(128)
    mask = (ii[:, None] <= ii[None, :]).astype(ml_dtypes.bfloat16)
    idbf = np.eye(128, dtype=ml_dtypes.bfloat16)
    idf32 = np.eye(128, dtype=np.float32)
    return cos, sin, mask, idbf, idf32


_NC_CACHE = {}


def kernel(L0_attn_x, L0_attn_Wq, L0_attn_Wk, L0_attn_Wv, L0_attn_Wo,
           L0_ffn_w_gate, L0_ffn_w_up, L0_ffn_w_down, L0_ffn_ln_weight):
    x = np.ascontiguousarray(np.asarray(L0_attn_x, dtype=np.float32))
    B = x.shape[0]
    cos, sin, mask, idbf, idf32 = _host_constants()
    common = {
        "wq": np.ascontiguousarray(np.asarray(L0_attn_Wq, np.float32)),
        "wk": np.ascontiguousarray(np.asarray(L0_attn_Wk, np.float32)),
        "wv": np.ascontiguousarray(np.asarray(L0_attn_Wv, np.float32)),
        "wo": np.ascontiguousarray(np.asarray(L0_attn_Wo, np.float32)),
        "wg": np.ascontiguousarray(np.asarray(L0_ffn_w_gate, np.float32)),
        "wu": np.ascontiguousarray(np.asarray(L0_ffn_w_up, np.float32)),
        "wd": np.ascontiguousarray(np.asarray(L0_ffn_w_down, np.float32)),
        "lnw": np.ascontiguousarray(np.asarray(L0_ffn_ln_weight, np.float32)),
        "costab": cos, "sintab": sin, "masktab": mask,
        "idbf": idbf, "idf32": idf32,
    }
    if "nc" not in _NC_CACHE:
        _NC_CACHE["nc"] = build_program()
    nc = _NC_CACHE["nc"]
    in_maps = [dict(common, x=np.ascontiguousarray(x[b])) for b in range(B)]
    res = run_bass_kernel_spmd(nc, in_maps, core_ids=list(range(B)))
    return np.stack([res.results[b]["out"] for b in range(B)], axis=0)


if __name__ == "__main__":
    import time
    t0 = time.time()
    nc = build_program()
    print(f"built ok in {time.time() - t0:.1f}s, "
          f"instructions={sum(1 for _ in nc.all_instructions())}")


# version 21
# speedup vs baseline: 1.3096x; 1.2992x over previous
import numpy as np
import ml_dtypes
import concourse.bass as bass
import concourse.mybir as mybir
from concourse.bass_utils import run_bass_kernel_spmd

F32 = mybir.dt.float32
BF16 = mybir.dt.bfloat16
I32 = mybir.dt.int32
AF = mybir.ActivationFunctionType
ALU = mybir.AluOpType
AX = mybir.AxisListType

MAGIC = 12582912.0

S = 2048
D = 2048
II = 8192
HD = 128
NQH = 16
NKV = 4
NCH = S // 128
GRP = 2
NGRP = NCH // GRP
GT = GRP * 128
LN_EPS = 1e-5
RMS_EPS = 1e-6
ATT_EPS = 1e-6
Q_EPS = 1e-5
W_EPS = 1e-5

ENGINES = ("sync", "vector", "scalar", "tensor", "gpsimd")

DEBUG_SINK = None


class Step:
    __slots__ = ("eng", "emit", "reads", "writes", "dma", "idx",
                 "needs_inc", "count", "waits", "tag", "slot", "gen")

    def __init__(self, eng, emit, reads, writes, dma, idx, tag):
        self.eng = eng
        self.emit = emit
        self.reads = reads
        self.writes = writes
        self.dma = dma
        self.idx = idx
        self.needs_inc = False
        self.count = None
        self.waits = []
        self.tag = tag


class Sched:

    def __init__(self):
        self.steps = []
        self.last_writer = {}
        self.readers = {}

    def do(self, eng, emit, reads=(), writes=(), dma=None, tag=""):
        st = Step(eng, emit, list(reads), list(writes), dma, len(self.steps), tag)
        deps = []
        for b in st.reads:
            w = self.last_writer.get(id(b))
            if w is not None:
                deps.append(w)
        for b in st.writes:
            w = self.last_writer.get(id(b))
            if w is not None:
                deps.append(w)
            deps.extend(self.readers.get(id(b), ()))
        seen = set()
        for d in deps:
            if d.idx in seen or d is st:
                continue
            seen.add(d.idx)
            if (d.dma is None and st.dma is None and d.eng == st.eng
                    and d.eng == "tensor"):
                continue
            st.waits.append(d)
        for b in st.reads:
            self.readers.setdefault(id(b), []).append(st)
        for b in st.writes:
            self.last_writer[id(b)] = st
            self.readers[id(b)] = []
        self.steps.append(st)
        return st

    DMA_SLOTS = 16

    def _semkey(self, st):
        if st.dma is not None:
            return ("dma", st.dma, st.slot)
        return ("eng", st.eng)

    def resolve_and_emit(self, nc):
        P = self.DMA_SLOTS
        qidx = {}
        for st in self.steps:
            if st.dma is not None:
                i = qidx.get(st.dma, 0)
                qidx[st.dma] = i + 1
                st.slot = i % P
                st.gen = i // P
                st.needs_inc = True
                st.count = 16 * (st.gen + 1)
            for d in st.waits:
                d.needs_inc = True
        cnt = {}
        for st in self.steps:
            if st.needs_inc and st.dma is None:
                key = self._semkey(st)
                cnt[key] = cnt.get(key, 0) + 1
                st.count = cnt[key]
        from contextlib import ExitStack
        self._semstack = ExitStack()
        sems = {}
        for st in self.steps:
            if st.needs_inc:
                key = self._semkey(st)
                if key not in sems:
                    sems[key] = self._semstack.enter_context(
                        nc.semaphore("sem_" + "_".join(str(k) for k in key[1:])))
        hw = {e: {} for e in ENGINES}
        per_eng = {e: [] for e in ENGINES}
        for st in self.steps:
            per_eng[st.eng].append(st)

        with nc.Block() as block:
            def make_body(eng_name):
                def body(eng):
                    marks = hw[eng_name]
                    for st in per_eng[eng_name]:
                        if st.dma is not None and st.gen > 0:
                            key = self._semkey(st)
                            v = 16 * st.gen
                            if marks.get(key, 0) < v:
                                eng.wait_ge(sems[key], v)
                                marks[key] = v
                        for d in st.waits:
                            key = self._semkey(d)
                            v = d.count
                            if marks.get(key, 0) >= v:
                                continue
                            eng.wait_ge(sems[key], v)
                            marks[key] = v
                        inst = st.emit(eng)
                        if st.needs_inc:
                            key = self._semkey(st)
                            inst.then_inc(sems[key], 16 if st.dma else 1)
                return body
            block.sync(make_body("sync"))
            block.vector(make_body("vector"))
            block.scalar(make_body("scalar"))
            block.tensor(make_body("tensor"))
            block.gpsimd(make_body("gpsimd"))
        return sems


def build_program():
    nc = bass.Bass("TRN2", target_bir_lowering=False, debug=False)
    sch = Sched()
    from contextlib import ExitStack
    stack = ExitStack()

    def sb(name, shape, dtype):
        return stack.enter_context(nc.sbuf_tensor(name, shape, dtype))

    def psu(name, shape, dtype=F32):
        return stack.enter_context(nc.psum_tensor(name, shape, dtype))

    x_h = nc.dram_tensor("x", [S, D], F32, kind="ExternalInput").ap()
    wq_h = nc.dram_tensor("wq", [D, D], F32, kind="ExternalInput").ap()
    wk_h = nc.dram_tensor("wk", [512, D], F32, kind="ExternalInput").ap()
    wv_h = nc.dram_tensor("wv", [512, D], F32, kind="ExternalInput").ap()
    wo_h = nc.dram_tensor("wo", [D, D], F32, kind="ExternalInput").ap()
    wg_h = nc.dram_tensor("wg", [II, D], F32, kind="ExternalInput").ap()
    wu_h = nc.dram_tensor("wu", [II, D], F32, kind="ExternalInput").ap()
    wd_h = nc.dram_tensor("wd", [D, II], F32, kind="ExternalInput").ap()
    lnw_h = nc.dram_tensor("lnw", [II], F32, kind="ExternalInput").ap()
    cos_h = nc.dram_tensor("costab", [S, HD], F32, kind="ExternalInput").ap()
    sin_h = nc.dram_tensor("sintab", [S, HD], F32, kind="ExternalInput").ap()
    mask_h = nc.dram_tensor("masktab", [128, 128], BF16, kind="ExternalInput").ap()
    idbf_h = nc.dram_tensor("idbf", [128, 128], BF16, kind="ExternalInput").ap()
    idf32_h = nc.dram_tensor("idf32", [128, 128], F32, kind="ExternalInput").ap()
    out_h = nc.dram_tensor("out", [S, D], F32, kind="ExternalOutput").ap()

    ws_q = nc.dram_tensor("ws_q", [4, 2, 512, D], BF16, kind="Internal").ap()
    ws_k = nc.dram_tensor("ws_k", [1, 2, 512, D], BF16, kind="Internal").ap()
    ws_v = nc.dram_tensor("ws_v", [1, 2, 512, D], BF16, kind="Internal").ap()
    ws_o = nc.dram_tensor("ws_o", [4, 2, 512, D], BF16, kind="Internal").ap()
    ws_gu = nc.dram_tensor("ws_gu", [32, 2, 256, D], BF16, kind="Internal").ap()
    ws_d = nc.dram_tensor("ws_d", [D, II], BF16, kind="Internal").ap()
    pt_h = nc.dram_tensor("pt", [II, S], F32, kind="Internal").ap()
    att_h = nc.dram_tensor("att_sp", [S, D], F32, kind="Internal").ap()
    rowscr_h = nc.dram_tensor("rowscr", [4, S], F32, kind="Internal").ap()

    mask_sb = sb("mask_sb", [128, 128], BF16)
    idbf_sb = sb("idbf_sb", [128, 128], BF16)
    idf32_sb = sb("idf32_sb", [128, 128], F32)
    lnw_tmp = sb("lnw_tmp", [64, 128], F32)
    lnw_sb = sb("lnw_sb", [128, 64], F32)
    alnw_sb = sb("alnw_sb", [128, 64], F32)
    ones_1x128_f = sb("ones_1x128_f", [1, 128], F32)
    nxT = sb("nxT", [128, NCH * S], BF16)
    nhT = nxT
    sxinv_col = sb("sxinv_col", [128, NCH], F32)
    Bsx = sb("Bsx", [128, S], F32)
    sd_col = sb("sd_col", [128, NCH], F32)
    thr = {}
    for nm in ("g", "u", "d"):
        thr[nm] = dict(
            racc=sb(f"racc_{nm}", [128, 1], F32),
            a=sb(f"a_{nm}", [1, 1], F32),
            abc=sb(f"abc_{nm}", [128, 1], F32),
            anbc=sb(f"anbc_{nm}", [128, 1], F32),
        )
    st32 = [sb(f"st32_{g}", [128, 132], F32) for g in range(NKV)]
    sth = [sb(f"sth_{g}", [128, 132], BF16) for g in range(NKV)]
    stl = [sb(f"stl_{g}", [128, 132], BF16) for g in range(NKV)]

    p0 = sb("p0", [128, D], F32)
    p1 = sb("p1", [128, D], F32)
    xb0 = sb("xb0", [128, D], F32)
    fwbuf = [xb0, p0, p1]
    nxbf = sb("nxbf", [128, D], BF16)
    ternbf = nxbf
    hsp_bf = sb("hsp_bf", [128, D], BF16)
    lsp_bf = sb("lsp_bf", [128, D], BF16)
    stat = [sb(f"stat{i}", [128, 1], F32) for i in range(8)]
    row0 = sb("row0", [1, 512], F32)
    ep = [sb(f"ep{i}", [128, 512], F32) for i in range(5)]
    epp = [sb(f"epp{i}", [128, 512], F32) for i in range(2)]
    ep32 = sb("epi32", [128, 512], I32)
    qro = sb("qro", [128, 512], F32)
    ssqsl = sb("ssqsl", [128, 512], F32)
    maxsl = sb("maxsl", [128, 512], F32)
    cos_sb = sb("cos_sb", [128, 128], F32)
    sin_sb = sb("sin_sb", [128, 128], F32)

    xnT = [sb(f"xnT_{p}", [128, NCH * GT], BF16) for p in "hl"]
    attT = xnT
    qfT = [sb(f"qfT_{p}", [128, NQH * GT], BF16) for p in "hl"]
    kfT = [sb(f"kfT_{p}", [128, NKV * GT], BF16) for p in "hl"]
    kftok = [sb(f"kftok_{p}", [128, GRP * 512], BF16) for p in "hl"]
    vaug = [sb(f"vaug_{p}", [128, GRP * NKV * 132], BF16) for p in "hl"]
    amT = [[sb(f"amT_{p}{i}", [128, 128], BF16) for p in "hl"] for i in range(2)]
    am32 = [sb(f"am32_{i}", [128, 128], F32) for i in range(2)]

    wth = [sb(f"wth{i}", [128, 1024], BF16) for i in range(8)]

    ps_acc = [psu(f"ps_acc{i}", [128, 512], F32) for i in range(4)]
    ps_tr = [psu(f"ps_tr{i}", [128, 128], BF16) for i in range(2)]
    ps_af = psu("ps_af", [128, 128], F32)
    ps_af2 = psu("ps_af2", [128, 128], F32)

    SC = sch.do

    SC("sync", lambda e: e.dma_start(out=mask_sb[:], in_=mask_h), writes=[mask_sb], dma="ld")
    SC("sync", lambda e: e.dma_start(out=idbf_sb[:], in_=idbf_h), writes=[idbf_sb], dma="ld")
    SC("sync", lambda e: e.dma_start(out=idf32_sb[:], in_=idf32_h), writes=[idf32_sb], dma="ld")
    SC("sync", lambda e: e.dma_start(out=lnw_tmp[:], in_=lnw_h.rearrange("(a b) -> a b", b=128)),
       writes=[lnw_tmp], dma="ld")
    SC("gpsimd", lambda e: e.memset(ones_1x128_f[:], 1.0), writes=[ones_1x128_f])
    for g in range(NKV):
        SC("gpsimd", lambda e, g=g: e.memset(st32[g][:], 0.0), writes=[st32[g]])
    SC("tensor", lambda e: e.transpose(ps_af[:, 0:64], lnw_tmp[:], idf32_sb[0:64, 0:64]),
       reads=[lnw_tmp, idf32_sb], writes=[ps_af])
    SC("vector", lambda e: e.tensor_copy(out=lnw_sb[:], in_=ps_af[:, 0:64]),
       reads=[ps_af], writes=[lnw_sb])
    SC("scalar", lambda e: e.activation(out=alnw_sb[:], in_=lnw_sb[:], func=AF.Abs),
       reads=[lnw_sb], writes=[alnw_sb])

    for (w_hbm, w_scr, nrow) in ((wq_h, ws_q, D), (wk_h, ws_k, 512),
                                 (wv_h, ws_v, 512), (wo_h, ws_o, D)):
        for t in range(nrow // 128):
            buf = fwbuf[t % 3]
            r0 = t * 128
            ob, rr = r0 // 512, r0 % 512
            SC("sync", lambda e, b=buf, h=w_hbm, r0=r0:
               e.dma_start(out=b[:], in_=h[r0:r0 + 128, :]), writes=[buf], dma="ld")
            SC("vector", lambda e, b=buf: e.tensor_copy(out=hsp_bf[:], in_=b[:]),
               reads=[buf], writes=[hsp_bf])
            SC("vector", lambda e, b=buf:
               e.tensor_tensor(out=b[:], in0=b[:], in1=hsp_bf[:], op=ALU.subtract),
               reads=[buf, hsp_bf], writes=[buf])
            SC("vector", lambda e, b=buf: e.tensor_copy(out=lsp_bf[:], in_=b[:]),
               reads=[buf], writes=[lsp_bf])
            SC("scalar", lambda e, s=w_scr, ob=ob, rr=rr:
               e.dma_start(out=s[ob, 0, rr:rr + 128, :], in_=hsp_bf[:]), reads=[hsp_bf],
               writes=[w_scr], dma="st")
            SC("scalar", lambda e, s=w_scr, ob=ob, rr=rr:
               e.dma_start(out=s[ob, 1, rr:rr + 128, :], in_=lsp_bf[:]), reads=[lsp_bf],
               writes=[w_scr], dma="st")

    def ffn_subtiles(nrow, ncol):
        for r in range(nrow // 128):
            for cs in range(ncol // D):
                yield r * 128, cs * D

    ffn_mats = (("g", wg_h, 0, II, D), ("u", wu_h, 1, II, D), ("d", wd_h, None, D, II))
    for nm, w_hbm, w_scr, nrow, ncol in ffn_mats:
        t = thr[nm]
        SC("gpsimd", lambda e, t=t: e.memset(t["racc"][:], 0.0), writes=[t["racc"]])
        ti = 0
        for r0, c0 in ffn_subtiles(nrow, ncol):
            buf = fwbuf[ti % 3]
            ti += 1
            SC("sync", lambda e, b=buf, h=w_hbm, r0=r0, c0=c0:
               e.dma_start(out=b[:], in_=h[r0:r0 + 128, c0:c0 + D]), writes=[buf], dma="ld")
            SC("scalar", lambda e, b=buf:
               e.activation(out=b[:], in_=b[:], func=AF.Abs, accum_out=stat[0][:]),
               reads=[buf], writes=[buf, stat[0]])
            SC("vector", lambda e, t=t:
               e.tensor_tensor(out=t["racc"][:], in0=t["racc"][:], in1=stat[0][:], op=ALU.add),
               reads=[stat[0], t["racc"]], writes=[t["racc"]])
        SC("tensor", lambda e, t=t:
           e.transpose(ps_af[0:1, 0:128], t["racc"][:], idf32_sb[:]),
           reads=[t["racc"], idf32_sb], writes=[ps_af], tag="meanred")
        SC("vector", lambda e: e.tensor_copy(out=row0[0:1, 0:128], in_=ps_af[0:1, 0:128]),
           reads=[ps_af], writes=[row0])
        nelem = float(nrow * ncol)
        SC("vector", lambda e, t=t:
           e.tensor_reduce(out=t["a"][:], in_=row0[0:1, 0:128], op=ALU.add, axis=AX.X),
           reads=[row0], writes=[t["a"]])
        SC("vector", lambda e, t=t, n=nelem:
           e.tensor_scalar(out=t["a"][:], in0=t["a"][:], scalar1=1.0 / n,
                           scalar2=W_EPS, op0=ALU.mult, op1=ALU.max),
           reads=[t["a"]], writes=[t["a"]])
        SC("vector", lambda e, t=t:
           e.tensor_scalar(out=t["a"][:], in0=t["a"][:], scalar1=0.5, scalar2=None, op0=ALU.mult),
           reads=[t["a"]], writes=[t["a"]])
        SC("tensor", lambda e, t=t:
           e.matmul(ps_acc[3][:, 0:1], ones_1x128_f[:], t["a"][:], start=True, stop=True),
           reads=[ones_1x128_f, t["a"]], writes=[ps_acc[3]])
        SC("vector", lambda e, t=t: e.tensor_copy(out=t["abc"][:], in_=ps_acc[3][:, 0:1]),
           reads=[ps_acc[3]], writes=[t["abc"]])
        SC("vector", lambda e, t=t:
           e.tensor_scalar(out=t["anbc"][:], in0=t["abc"][:], scalar1=-1.0, scalar2=None,
                           op0=ALU.mult),
           reads=[t["abc"]], writes=[t["anbc"]])

    for nm, w_hbm, w_scr, nrow, ncol in ffn_mats:
        t = thr[nm]
        ti = 0
        for r0, c0 in ffn_subtiles(nrow, ncol):
            buf = fwbuf[ti % 3]
            ti += 1
            SC("sync", lambda e, b=buf, h=w_hbm, r0=r0, c0=c0:
               e.dma_start(out=b[:], in_=h[r0:r0 + 128, c0:c0 + D]), writes=[buf], dma="ld")
            SC("scalar", lambda e, b=buf, t=t:
               e.activation(out=hsp_bf[:], in_=b[:], func=AF.Sign, bias=t["anbc"][:]),
               reads=[buf, t["anbc"]], writes=[hsp_bf])
            SC("scalar", lambda e, b=buf, t=t:
               e.activation(out=lsp_bf[:], in_=b[:], func=AF.Sign, bias=t["abc"][:]),
               reads=[buf, t["abc"]], writes=[lsp_bf])
            SC("vector", lambda e:
               e.tensor_tensor(out=ternbf[:], in0=hsp_bf[:], in1=lsp_bf[:], op=ALU.add),
               reads=[hsp_bf, lsp_bf], writes=[ternbf])
            if w_scr is None:
                SC("scalar", lambda e, r0=r0, c0=c0:
                   e.dma_start(out=ws_d[r0:r0 + 128, c0:c0 + D], in_=ternbf[:]),
                   reads=[ternbf], writes=[ws_d], dma="st")
            else:
                SC("scalar", lambda e, gu=w_scr, ic=r0 // 256, rr=r0 % 256, c0=c0:
                   e.dma_start(out=ws_gu[ic, gu, rr:rr + 128, c0:c0 + D], in_=ternbf[:]),
                   reads=[ternbf], writes=[ws_gu], dma="st")

    for grp in range(NGRP):
        for cl in range(GRP):
            c = grp * GRP + cl
            SC("sync", lambda e, r0=c * 128:
               e.dma_start(out=xb0[:], in_=x_h[r0:r0 + 128, :]), writes=[xb0], dma="ld")
            SC("vector", lambda e: e.tensor_reduce(out=stat[0][:], in_=xb0[:], op=ALU.add,
                                                   axis=AX.X),
               reads=[xb0], writes=[stat[0]])
            SC("scalar", lambda e:
               e.activation(out=p1[:], in_=xb0[:], func=AF.Square, accum_out=stat[1][:]),
               reads=[xb0], writes=[p1, stat[1]])
            SC("vector", lambda e: e.tensor_scalar(out=stat[2][:], in0=stat[0][:],
                                                   scalar1=1.0 / D, scalar2=None, op0=ALU.mult),
               reads=[stat[0]], writes=[stat[2]])
            SC("vector", lambda e: e.tensor_scalar(out=stat[3][:], in0=stat[1][:],
                                                   scalar1=1.0 / D, scalar2=LN_EPS,
                                                   op0=ALU.mult, op1=ALU.add),
               reads=[stat[1]], writes=[stat[3]])
            SC("vector", lambda e: e.tensor_tensor(out=stat[4][:], in0=stat[2][:], in1=stat[2][:],
                                                   op=ALU.mult),
               reads=[stat[2]], writes=[stat[4]])
            SC("vector", lambda e: e.tensor_tensor(out=stat[3][:], in0=stat[3][:], in1=stat[4][:],
                                                   op=ALU.subtract),
               reads=[stat[3], stat[4]], writes=[stat[3]])
            SC("scalar", lambda e: e.activation(out=stat[4][:], in_=stat[3][:], func=AF.Sqrt),
               reads=[stat[3]], writes=[stat[4]])
            SC("vector", lambda e: e.reciprocal(out=stat[5][:], in_=stat[4][:]),
               reads=[stat[4]], writes=[stat[5]])
            SC("vector", lambda e: e.tensor_tensor(out=stat[4][:], in0=stat[5][:], in1=stat[5][:],
                                                   op=ALU.mult),
               reads=[stat[5]], writes=[stat[4]])
            SC("vector", lambda e: e.tensor_tensor(out=stat[4][:], in0=stat[4][:], in1=stat[3][:],
                                                   op=ALU.mult),
               reads=[stat[4], stat[3]], writes=[stat[4]])
            SC("vector", lambda e: e.tensor_scalar(out=stat[4][:], in0=stat[4][:], scalar1=-0.5,
                                                   scalar2=1.5, op0=ALU.mult, op1=ALU.add),
               reads=[stat[4]], writes=[stat[4]])
            SC("vector", lambda e: e.tensor_tensor(out=stat[4][:], in0=stat[4][:], in1=stat[5][:],
                                                   op=ALU.mult),
               reads=[stat[4], stat[5]], writes=[stat[4]])
            SC("vector", lambda e: e.tensor_tensor(out=stat[5][:], in0=stat[2][:], in1=stat[4][:],
                                                   op=ALU.mult),
               reads=[stat[2], stat[4]], writes=[stat[5]])
            SC("vector", lambda e: e.tensor_scalar(out=stat[5][:], in0=stat[5][:], scalar1=-1.0,
                                                   scalar2=None, op0=ALU.mult),
               reads=[stat[5]], writes=[stat[5]])
            xn = p0
            SC("scalar", lambda e:
               e.activation(out=xn[:], in_=xb0[:], func=AF.Identity, bias=stat[5][:],
                            scale=stat[4][:]),
               reads=[xb0, stat[4], stat[5]], writes=[xn])
            SC("vector", lambda e: e.tensor_copy(out=hsp_bf[:], in_=xn[:]),
               reads=[xn], writes=[hsp_bf])
            SC("vector", lambda e: e.tensor_tensor(out=p1[:], in0=xn[:], in1=hsp_bf[:],
                                                   op=ALU.subtract),
               reads=[xn, hsp_bf], writes=[p1])
            SC("vector", lambda e: e.tensor_copy(out=lsp_bf[:], in_=p1[:]),
               reads=[p1], writes=[lsp_bf])
            SC("scalar", lambda e: e.activation(out=p1[:], in_=xn[:], func=AF.Abs),
               reads=[xn], writes=[p1])
            SC("vector", lambda e: e.tensor_reduce(out=stat[0][:], in_=p1[:], op=ALU.max,
                                                   axis=AX.X),
               reads=[p1], writes=[stat[0]])
            SC("vector", lambda e: e.tensor_scalar(out=stat[1][:], in0=stat[0][:], scalar1=Q_EPS,
                                                   scalar2=None, op0=ALU.max),
               reads=[stat[0]], writes=[stat[1]])
            SC("vector", lambda e, c=c: e.tensor_scalar(out=sxinv_col[:, c:c + 1], in0=stat[1][:],
                                                        scalar1=1.0 / 127.0, scalar2=None,
                                                        op0=ALU.mult),
               reads=[stat[1]], writes=[sxinv_col])
            SC("vector", lambda e: e.reciprocal(out=stat[2][:], in_=stat[1][:]),
               reads=[stat[1]], writes=[stat[2]])
            SC("vector", lambda e: e.tensor_scalar(out=stat[2][:], in0=stat[2][:], scalar1=127.0,
                                                   scalar2=None, op0=ALU.mult),
               reads=[stat[2]], writes=[stat[2]])
            for sl4 in range(4):
                cs = slice(sl4 * 512, (sl4 + 1) * 512)
                SC("vector", lambda e, cs=cs: e.tensor_scalar(out=ep[0][:], in0=xn[:, cs],
                                                              scalar1=stat[2][:], scalar2=None,
                                                              op0=ALU.mult),
                   reads=[xn, stat[2]], writes=[ep[0]])
                SC("vector", lambda e: e.tensor_scalar(out=ep32[:], in0=ep[0][:], scalar1=MAGIC,
                                                       scalar2=MAGIC, op0=ALU.add,
                                                       op1=ALU.subtract),
                   reads=[ep[0]], writes=[ep32])
                SC("gpsimd", lambda e, cs=cs: e.tensor_copy(out=nxbf[:, cs], in_=ep32[:]),
                   reads=[ep32], writes=[nxbf])
            for kc in range(16):
                bank = ps_tr[kc % 2]
                SC("tensor", lambda e, b=bank, kc=kc:
                   e.transpose(b[:], nxbf[:, kc * 128:(kc + 1) * 128], idbf_sb[:]),
                   reads=[nxbf, idbf_sb], writes=[bank])
                SC("scalar", lambda e, b=bank, kc=kc, c=c:
                   e.copy(out=nxT[:, kc * S + c * 128: kc * S + c * 128 + 128], in_=b[:]),
                   reads=[bank], writes=[nxT])
            for (src, dstT) in ((hsp_bf, xnT[0]), (lsp_bf, xnT[1])):
                for kc in range(16):
                    bank = ps_tr[kc % 2]
                    SC("tensor", lambda e, b=bank, s=src, kc=kc:
                       e.transpose(b[:], s[:, kc * 128:(kc + 1) * 128], idbf_sb[:]),
                       reads=[src, idbf_sb], writes=[bank])
                    SC("vector", lambda e, b=bank, d=dstT, kc=kc, cl=cl:
                       e.tensor_copy(out=d[:, kc * GT + cl * 128: kc * GT + cl * 128 + 128],
                                     in_=b[:]),
                       reads=[bank], writes=[dstT])

        def rope_elu(bank, c, dst):
            SC("sync", lambda e, c=c: e.dma_start(out=cos_sb[:],
                                                  in_=cos_h[c * 128:(c + 1) * 128, :]),
               writes=[cos_sb], dma="ld")
            SC("sync", lambda e, c=c: e.dma_start(out=sin_sb[:],
                                                  in_=sin_h[c * 128:(c + 1) * 128, :]),
               writes=[sin_sb], dma="ld")
            qc, qs, ro = ep[0], ep[1], qro
            for hh in range(4):
                sl = slice(hh * 128, (hh + 1) * 128)
                SC("vector", lambda e, b=bank, sl=sl:
                   e.tensor_tensor(out=qc[:, sl], in0=b[:, sl], in1=cos_sb[:], op=ALU.mult),
                   reads=[bank, cos_sb], writes=[qc])
                SC("vector", lambda e, b=bank, sl=sl:
                   e.tensor_tensor(out=qs[:, sl], in0=b[:, sl], in1=sin_sb[:], op=ALU.mult),
                   reads=[bank, sin_sb], writes=[qs])
            for hh in range(4):
                b0 = hh * 128
                SC("vector", lambda e, b0=b0:
                   e.tensor_tensor(out=ro[:, b0:b0 + 64], in0=qc[:, b0:b0 + 64],
                                   in1=qs[:, b0 + 64:b0 + 128], op=ALU.subtract),
                   reads=[qc, qs], writes=[ro])
                SC("vector", lambda e, b0=b0:
                   e.tensor_tensor(out=ro[:, b0 + 64:b0 + 128], in0=qc[:, b0 + 64:b0 + 128],
                                   in1=qs[:, b0:b0 + 64], op=ALU.add),
                   reads=[qc, qs], writes=[ro])
            SC("vector", lambda e: e.tensor_scalar(out=ep[2][:], in0=ro[:], scalar1=0.0,
                                                   scalar2=None, op0=ALU.min),
               reads=[ro], writes=[ep[2]])
            SC("scalar", lambda e: e.activation(out=ep[3][:], in_=ep[2][:], func=AF.Exp),
               reads=[ep[2]], writes=[ep[3]])
            SC("vector", lambda e: e.scalar_tensor_tensor(out=dst[:], in0=ro[:], scalar=0.0,
                                                          in1=ep[3][:], op0=ALU.max, op1=ALU.add),
               reads=[ro, ep[3]], writes=[dst])

        def proj(w_scr, nblk, evac, tag):
            for ob in range(nblk):
                for kph in range(2):
                    for j in range(8):
                        kc = kph * 8 + j
                        SC("sync", lambda e, w=wth[j], s=w_scr, ob=ob, kc=kc:
                           e.dma_start_transpose(
                               w[:], s[ob].rearrange("a b c -> (a b) c")[
                                   :, kc * 128:(kc + 1) * 128]),
                           reads=[w_scr], writes=[wth[j]], dma="ld", tag=tag + "_wld")
                    for tb in range(GRP):
                        bank = ps_acc[tb]
                        for j in range(8):
                            kc = kph * 8 + j
                            xh_t = xnT[0][:, kc * GT + tb * 128: kc * GT + tb * 128 + 128]
                            xl_t = xnT[1][:, kc * GT + tb * 128: kc * GT + tb * 128 + 128]
                            SC("tensor", lambda e, b=bank, w=wth[j], xt=xh_t,
                               f=(kph == 0 and j == 0):
                               e.matmul(b[:], xt, w[:, 0:512], start=f, stop=False),
                               reads=[xnT[0], wth[j]], writes=[bank], tag=tag + "_mm")
                            SC("tensor", lambda e, b=bank, w=wth[j], xt=xh_t:
                               e.matmul(b[:], xt, w[:, 512:1024], start=False, stop=False),
                               reads=[xnT[0], wth[j]], writes=[bank])
                            SC("tensor", lambda e, b=bank, w=wth[j], xt=xl_t,
                               l=(kph == 1 and j == 7):
                               e.matmul(b[:], xt, w[:, 0:512], start=False, stop=l),
                               reads=[xnT[1], wth[j]], writes=[bank])
                for tb in range(GRP):
                    evac(tb, ob, ps_acc[tb])

        def q_evac(tb, ob, bank):
            c = grp * GRP + tb
            rope_elu(bank, c, qro)
            qh = nxbf
            SC("vector", lambda e: e.tensor_copy(out=qh[:, 0:512], in_=qro[:]),
               reads=[qro], writes=[qh])
            SC("vector", lambda e: e.tensor_tensor(out=ep[4][:], in0=qro[:], in1=qh[:, 0:512],
                                                   op=ALU.subtract),
               reads=[qro, qh], writes=[ep[4]])
            SC("vector", lambda e: e.tensor_copy(out=qh[:, 512:1024], in_=ep[4][:]),
               reads=[ep[4]], writes=[qh])
            for hh in range(4):
                hg = ob * 4 + hh
                for pi in range(2):
                    bank2 = ps_tr[(hh * 2 + pi) % 2]
                    SC("tensor", lambda e, b=bank2, pi=pi, hh=hh:
                       e.transpose(b[:], qh[:, pi * 512 + hh * 128: pi * 512 + hh * 128 + 128],
                                   idbf_sb[:]),
                       reads=[qh, idbf_sb], writes=[bank2])
                    SC("vector", lambda e, b=bank2, pi=pi, hg=hg, tb=tb:
                       e.tensor_copy(out=qfT[pi][:, hg * GT + tb * 128: hg * GT + tb * 128 + 128],
                                     in_=b[:]),
                       reads=[bank2], writes=[qfT[pi]])

        def k_evac(tb, ob, bank):
            c = grp * GRP + tb
            rope_elu(bank, c, qro)
            SC("vector", lambda e, tb=tb: e.tensor_copy(out=kftok[0][:, tb * 512:(tb + 1) * 512],
                                                        in_=qro[:]),
               reads=[qro], writes=[kftok[0]])
            SC("vector", lambda e, tb=tb:
               e.tensor_tensor(out=ep[4][:], in0=qro[:],
                               in1=kftok[0][:, tb * 512:(tb + 1) * 512], op=ALU.subtract),
               reads=[qro, kftok[0]], writes=[ep[4]])
            SC("vector", lambda e, tb=tb: e.tensor_copy(out=kftok[1][:, tb * 512:(tb + 1) * 512],
                                                        in_=ep[4][:]),
               reads=[ep[4]], writes=[kftok[1]])
            for hh in range(NKV):
                for pi in range(2):
                    bank2 = ps_tr[(hh * 2 + pi) % 2]
                    SC("tensor", lambda e, b=bank2, pi=pi, hh=hh, tb=tb:
                       e.transpose(b[:],
                                   kftok[pi][:, tb * 512 + hh * 128: tb * 512 + hh * 128 + 128],
                                   idbf_sb[:]),
                       reads=[kftok[pi], idbf_sb], writes=[bank2])
                    SC("vector", lambda e, b=bank2, pi=pi, hh=hh, tb=tb:
                       e.tensor_copy(out=kfT[pi][:, hh * GT + tb * 128: hh * GT + tb * 128 + 128],
                                     in_=b[:]),
                       reads=[bank2], writes=[kfT[pi]])

        def v_evac(tb, ob, bank):
            SC("vector", lambda e, b=bank: e.tensor_copy(out=qro[:], in_=b[:]),
               reads=[bank], writes=[qro])
            for g in range(NKV):
                slot = (tb * NKV + g) * 132
                SC("vector", lambda e, g=g, slot=slot:
                   e.tensor_copy(out=vaug[0][:, slot:slot + 128],
                                 in_=qro[:, g * 128:(g + 1) * 128]),
                   reads=[qro], writes=[vaug[0]])
                SC("vector", lambda e, g=g, slot=slot:
                   e.tensor_tensor(out=ep[4][:, 0:128], in0=qro[:, g * 128:(g + 1) * 128],
                                   in1=vaug[0][:, slot:slot + 128], op=ALU.subtract),
                   reads=[qro, vaug[0]], writes=[ep[4]])
                SC("vector", lambda e, slot=slot:
                   e.tensor_copy(out=vaug[1][:, slot:slot + 128], in_=ep[4][:, 0:128]),
                   reads=[ep[4]], writes=[vaug[1]])
                SC("gpsimd", lambda e, slot=slot: e.memset(vaug[0][:, slot + 128:slot + 129], 1.0),
                   writes=[vaug[0]])
                SC("gpsimd", lambda e, slot=slot: e.memset(vaug[1][:, slot + 128:slot + 129], 0.0),
                   writes=[vaug[1]])

        proj(ws_q, 4, q_evac, "qproj")
        proj(ws_k, 1, k_evac, "kproj")
        proj(ws_v, 1, v_evac, "vproj")
        if DEBUG_SINK:
            DEBUG_SINK(f"proj{grp}", nc, SC, dict(qfT=qfT, kfT=kfT, kftok=kftok,
                                                  vaug=vaug, xnT=xnT))

        for cl in range(GRP):
            c = grp * GRP + cl
            att_t = p0
            for h in range(NQH):
                g = h // 4
                pa = ps_af if h % 2 == 0 else ps_af2
                amt = amT[h % 2]
                a32 = am32[h % 2]
                nb = ps_acc[1] if h % 2 == 0 else ps_acc[3]
                ks = [kfT[p][:, g * GT + cl * 128: g * GT + cl * 128 + 128] for p in range(2)]
                qs_ = [qfT[p][:, h * GT + cl * 128: h * GT + cl * 128 + 128] for p in range(2)]
                SC("tensor", lambda e, pa=pa, a=ks[0], b=qs_[0]:
                   e.matmul(pa[:], a, b, start=True, stop=False),
                   reads=[kfT[0], qfT[0]], writes=[pa])
                SC("tensor", lambda e, pa=pa, a=ks[0], b=qs_[1]:
                   e.matmul(pa[:], a, b, start=False, stop=False),
                   reads=[kfT[0], qfT[1]], writes=[pa])
                SC("tensor", lambda e, pa=pa, a=ks[1], b=qs_[0]:
                   e.matmul(pa[:], a, b, start=False, stop=True),
                   reads=[kfT[1], qfT[0]], writes=[pa])
                SC("vector", lambda e, pa=pa, a32=a32:
                   e.tensor_tensor(out=a32[:], in0=pa[:], in1=mask_sb[:], op=ALU.mult),
                   reads=[pa, mask_sb], writes=[a32])
                SC("vector", lambda e, a32=a32, amt=amt: e.tensor_copy(out=amt[0][:], in_=a32[:]),
                   reads=[a32], writes=[amt[0]])
                SC("vector", lambda e, a32=a32, amt=amt:
                   e.tensor_tensor(out=ep[4][:, 0:128], in0=a32[:], in1=amt[0][:],
                                   op=ALU.subtract),
                   reads=[a32, amt[0]], writes=[ep[4]])
                SC("vector", lambda e, amt=amt: e.tensor_copy(out=amt[1][:], in_=ep[4][:, 0:128]),
                   reads=[ep[4]], writes=[amt[1]])
                vslot = (cl * NKV + g) * 132
                if c > 0:
                    SC("tensor", lambda e, nb=nb, q=qs_[0], g=g:
                       e.matmul(nb[:, 0:129], q, sth[g][:, 0:129], start=True, stop=False),
                       reads=[qfT[0], sth[g]], writes=[nb])
                    SC("tensor", lambda e, nb=nb, q=qs_[0], g=g:
                       e.matmul(nb[:, 0:129], q, stl[g][:, 0:129], start=False, stop=False),
                       reads=[qfT[0], stl[g]], writes=[nb])
                    SC("tensor", lambda e, nb=nb, q=qs_[1], g=g:
                       e.matmul(nb[:, 0:129], q, sth[g][:, 0:129], start=False, stop=False),
                       reads=[qfT[1], sth[g]], writes=[nb])
                    fi = False
                else:
                    fi = True
                SC("tensor", lambda e, nb=nb, amt=amt, vs=vslot, f=fi:
                   e.matmul(nb[:, 0:129], amt[0][:], vaug[0][:, vs:vs + 129], start=f, stop=False),
                   reads=[amt[0], vaug[0]], writes=[nb])
                SC("tensor", lambda e, nb=nb, amt=amt, vs=vslot:
                   e.matmul(nb[:, 0:129], amt[0][:], vaug[1][:, vs:vs + 129], start=False,
                            stop=False),
                   reads=[amt[0], vaug[1]], writes=[nb])
                SC("tensor", lambda e, nb=nb, amt=amt, vs=vslot:
                   e.matmul(nb[:, 0:129], amt[1][:], vaug[0][:, vs:vs + 129], start=False,
                            stop=True),
                   reads=[amt[1], vaug[0]], writes=[nb])
                st_d = stat[0] if h % 2 == 0 else stat[2]
                st_r = stat[1] if h % 2 == 0 else stat[3]
                SC("vector", lambda e, nb=nb, sd=st_d:
                   e.tensor_scalar(out=sd[:], in0=nb[:, 128:129],
                                   scalar1=ATT_EPS, scalar2=None, op0=ALU.add),
                   reads=[nb], writes=[st_d])
                SC("vector", lambda e, sd=st_d, sr=st_r: e.reciprocal(out=sr[:], in_=sd[:]),
                   reads=[st_d], writes=[st_r])
                SC("vector", lambda e, nb=nb, sr=st_r, h=h:
                   e.tensor_scalar(out=att_t[:, h * 128:(h + 1) * 128], in0=nb[:, 0:128],
                                   scalar1=sr[:], scalar2=None, op0=ALU.mult),
                   reads=[nb, st_r], writes=[att_t])
            for g in range(NKV):
                vslot = (cl * NKV + g) * 132
                kt = [kftok[p][:, cl * 512 + g * 128: cl * 512 + g * 128 + 128] for p in range(2)]
                db = ps_acc[0]
                SC("tensor", lambda e, k=kt[0], vs=vslot:
                   e.matmul(db[:, 0:129], k, vaug[0][:, vs:vs + 129], start=True, stop=False),
                   reads=[kftok[0], vaug[0]], writes=[db])
                SC("tensor", lambda e, k=kt[0], vs=vslot:
                   e.matmul(db[:, 0:129], k, vaug[1][:, vs:vs + 129], start=False, stop=False),
                   reads=[kftok[0], vaug[1]], writes=[db])
                SC("tensor", lambda e, k=kt[1], vs=vslot:
                   e.matmul(db[:, 0:129], k, vaug[0][:, vs:vs + 129], start=False, stop=True),
                   reads=[kftok[1], vaug[0]], writes=[db])
                SC("vector", lambda e, g=g:
                   e.tensor_tensor(out=st32[g][:, 0:129], in0=st32[g][:, 0:129],
                                   in1=db[:, 0:129], op=ALU.add),
                   reads=[st32[g], db], writes=[st32[g]])
                SC("vector", lambda e, g=g: e.tensor_copy(out=sth[g][:, 0:129],
                                                          in_=st32[g][:, 0:129]),
                   reads=[st32[g]], writes=[sth[g]])
                SC("vector", lambda e, g=g:
                   e.tensor_tensor(out=ep[2][:, 0:129], in0=st32[g][:, 0:129],
                                   in1=sth[g][:, 0:129], op=ALU.subtract),
                   reads=[st32[g], sth[g]], writes=[ep[2]])
                SC("vector", lambda e, g=g: e.tensor_copy(out=stl[g][:, 0:129],
                                                          in_=ep[2][:, 0:129]),
                   reads=[ep[2]], writes=[stl[g]])
            if DEBUG_SINK:
                DEBUG_SINK(f"att{c}", nc, SC, dict(att_t=att_t, st32=st32))
            SC("vector", lambda e: e.tensor_copy(out=hsp_bf[:], in_=att_t[:]),
               reads=[att_t], writes=[hsp_bf])
            SC("vector", lambda e: e.tensor_tensor(out=p1[:], in0=att_t[:], in1=hsp_bf[:],
                                                   op=ALU.subtract),
               reads=[att_t, hsp_bf], writes=[p1])
            SC("vector", lambda e: e.tensor_copy(out=lsp_bf[:], in_=p1[:]),
               reads=[p1], writes=[lsp_bf])
            for (src, dstT) in ((hsp_bf, attT[0]), (lsp_bf, attT[1])):
                for kc in range(16):
                    bank = ps_tr[kc % 2]
                    SC("tensor", lambda e, b=bank, s=src, kc=kc:
                       e.transpose(b[:], s[:, kc * 128:(kc + 1) * 128], idbf_sb[:]),
                       reads=[src, idbf_sb], writes=[bank])
                    SC("scalar", lambda e, b=bank, d=dstT, kc=kc, cl=cl:
                       e.copy(out=d[:, kc * GT + cl * 128: kc * GT + cl * 128 + 128], in_=b[:]),
                       reads=[bank], writes=[dstT])

        for ob in range(4):
            for kph in range(2):
                for j in range(8):
                    kc = kph * 8 + j
                    SC("sync", lambda e, w=wth[j], ob=ob, kc=kc:
                       e.dma_start_transpose(
                           w[:], ws_o[ob].rearrange("a b c -> (a b) c")[
                               :, kc * 128:(kc + 1) * 128]),
                       reads=[ws_o], writes=[wth[j]], dma="ld", tag="oproj_wld")
                for tb in range(GRP):
                    bank = ps_acc[tb]
                    for j in range(8):
                        kc = kph * 8 + j
                        ah_t = attT[0][:, kc * GT + tb * 128: kc * GT + tb * 128 + 128]
                        al_t = attT[1][:, kc * GT + tb * 128: kc * GT + tb * 128 + 128]
                        SC("tensor", lambda e, b=bank, w=wth[j], xt=ah_t,
                           f=(kph == 0 and j == 0):
                           e.matmul(b[:], xt, w[:, 0:512], start=f, stop=False),
                           reads=[attT[0], wth[j]], writes=[bank], tag="oproj_mm")
                        SC("tensor", lambda e, b=bank, w=wth[j], xt=ah_t:
                           e.matmul(b[:], xt, w[:, 512:1024], start=False, stop=False),
                           reads=[attT[0], wth[j]], writes=[bank])
                        SC("tensor", lambda e, b=bank, w=wth[j], xt=al_t,
                           l=(kph == 1 and j == 7):
                           e.matmul(b[:], xt, w[:, 0:512], start=False, stop=l),
                           reads=[attT[1], wth[j]], writes=[bank])
            for tb in range(GRP):
                bank = ps_acc[tb]
                SC("vector", lambda e, b=bank: e.tensor_copy(out=ep[0][:], in_=b[:]),
                   reads=[bank], writes=[ep[0]])
                SC("scalar", lambda e, ob=ob, tb=tb, grp=grp:
                   e.dma_start(out=att_h[grp * GT + tb * 128: grp * GT + tb * 128 + 128,
                                         ob * 512:(ob + 1) * 512], in_=ep[0][:]),
                   reads=[ep[0]], writes=[att_h], dma="st", tag="att_spill")

    for c in range(NCH):
        SC("scalar", lambda e, c=c: e.dma_start(out=rowscr_h[0, c * 128:(c + 1) * 128],
                                                in_=sxinv_col[:, c:c + 1]),
           reads=[sxinv_col], writes=[rowscr_h], dma="st")
    for bb in range(4):
        SC("sync", lambda e, bb=bb: e.dma_start(out=row0[:],
                                                in_=rowscr_h[0:1, bb * 512:(bb + 1) * 512]),
           reads=[rowscr_h], writes=[row0], dma="ld")
        SC("tensor", lambda e, bb=bb:
           e.matmul(ps_acc[3][:], ones_1x128_f[:], row0[0:1, 0:512],
                    start=True, stop=True),
           reads=[ones_1x128_f, row0], writes=[ps_acc[3]])
        SC("vector", lambda e, bb=bb: e.tensor_copy(out=Bsx[:, bb * 512:(bb + 1) * 512],
                                                    in_=ps_acc[3][:]),
           reads=[ps_acc[3]], writes=[Bsx])

    def emit_gu_epilogue(ic, ib, tb, bg, bu):
        ts0 = tb * 512
        ig = ic * 2 + ib
        SC("vector", lambda e, b=bg, ts0=ts0:
           e.scalar_tensor_tensor(out=ep[0][:], in0=b[:], scalar=thr["g"]["abc"][:],
                          in1=Bsx[:, ts0:ts0 + 512], op0=ALU.mult, op1=ALU.mult),
           reads=[bg, thr["g"]["abc"], Bsx], writes=[ep[0]])
        SC("scalar", lambda e: e.activation(out=ep[1][:], in_=ep[0][:], func=AF.Silu),
           reads=[ep[0]], writes=[ep[1]])
        SC("vector", lambda e, b=bu, ts0=ts0:
           e.scalar_tensor_tensor(out=ep[0][:], in0=b[:], scalar=thr["u"]["abc"][:],
                          in1=Bsx[:, ts0:ts0 + 512], op0=ALU.mult, op1=ALU.mult),
           reads=[bu, thr["u"]["abc"], Bsx], writes=[ep[0]])
        SC("vector", lambda e: e.tensor_tensor(out=ep[2][:], in0=ep[1][:], in1=ep[0][:],
                               op=ALU.mult),
           reads=[ep[1], ep[0]], writes=[ep[2]])
        if DEBUG_SINK and ic == 0 and ib == 0 and tb == 0:
            DEBUG_SINK("ffn0", nc, SC, dict(gt_silu=ep[1], ut=ep[0], Bsx=Bsx))
        SC("scalar", lambda e, ig=ig, ts0=ts0:
           e.dma_start(out=pt_h[ig * 128:(ig + 1) * 128, ts0:ts0 + 512], in_=ep[2][:]),
           reads=[ep[2]], writes=[pt_h], dma="st", tag="p_spill")

    def gu_tile(kc):
        return wth[kc // 2][:, (kc % 2) * 512:(kc % 2) * 512 + 512]

    for ic in range(32):
        for kc in range(16):
            SC("sync", lambda e, kc=kc, ic=ic:
               e.dma_start_transpose(
                   gu_tile(kc), ws_gu[ic].rearrange("a b c -> (a b) c")[
                       :, kc * 128:(kc + 1) * 128]),
               reads=[ws_gu], writes=[wth[kc // 2]], dma="ld", tag="wg_ld")
        for ib in range(2):
            ig = ic * 2 + ib
            for tb in range(4):
                par = (ib * 4 + tb) % 2
                bg, bu = ps_acc[0 + 2 * par], ps_acc[1 + 2 * par]
                for kc in range(16):
                    SC("tensor", lambda e, b=bg, kc=kc, ib=ib, tb=tb,
                       f=(kc == 0), l=(kc == 15):
                       e.matmul(b[:], gu_tile(kc)[:, ib * 128:(ib + 1) * 128],
                                nxT[:, kc * S + tb * 512: kc * S + (tb + 1) * 512],
                                start=f, stop=l),
                       reads=[wth[kc // 2], nxT], writes=[bg], tag="gmm")
                for kc in range(16):
                    SC("tensor", lambda e, b=bu, kc=kc, ib=ib, tb=tb,
                       f=(kc == 0), l=(kc == 15):
                       e.matmul(b[:], gu_tile(kc)[:, 256 + ib * 128: 256 + (ib + 1) * 128],
                                nxT[:, kc * S + tb * 512: kc * S + (tb + 1) * 512],
                                start=f, stop=l),
                       reads=[wth[kc // 2], nxT], writes=[bu], tag="umm")
                emit_gu_epilogue(ic, ib, tb, bg, bu)

    def emit_alpha(tsl):
        ts0 = tsl * 512
        SC("gpsimd", lambda e: e.memset(ssqsl[:], 0.0), writes=[ssqsl])
        SC("gpsimd", lambda e: e.memset(maxsl[:], 0.0), writes=[maxsl])
        for ig in range(64):
            pl = epp[ig % 2]
            sq = ep[0] if ig % 2 == 0 else ep[4]
            SC("sync", lambda e, b=pl, ig=ig, ts0=ts0:
               e.dma_start(out=b[:], in_=pt_h[ig * 128:(ig + 1) * 128, ts0:ts0 + 512]),
               reads=[pt_h], writes=[pl], dma="ld", tag="p_ld1")
            SC("scalar", lambda e, b=pl, o=sq: e.activation(out=o[:], in_=b[:], func=AF.Square),
               reads=[pl], writes=[sq])
            SC("vector", lambda e, o=sq: e.tensor_tensor(out=ssqsl[:], in0=ssqsl[:], in1=o[:],
                                                         op=ALU.add),
               reads=[ssqsl, sq], writes=[ssqsl])
            SC("scalar", lambda e, b=pl, o=sq: e.activation(out=o[:], in_=b[:], func=AF.Abs),
               reads=[pl], writes=[sq])
            SC("vector", lambda e, o=sq, ig=ig:
               e.scalar_tensor_tensor(out=maxsl[:], in0=o[:], scalar=alnw_sb[:, ig:ig + 1],
                                      in1=maxsl[:], op0=ALU.mult, op1=ALU.max),
               reads=[sq, alnw_sb, maxsl], writes=[maxsl])

    def emit_stats(tsl):
        ts0 = tsl * 512
        for j in range(4):
            c = tsl * 4 + j
            SC("tensor", lambda e, j=j: e.transpose(ps_af[:], ssqsl[:, j * 128:(j + 1) * 128],
                                                    idf32_sb[:]),
               reads=[ssqsl, idf32_sb], writes=[ps_af])
            SC("vector", lambda e: e.tensor_copy(out=ep[0][:, 0:128], in_=ps_af[:]),
               reads=[ps_af], writes=[ep[0]])
            SC("vector", lambda e: e.tensor_reduce(out=stat[0][:], in_=ep[0][:, 0:128],
                                                   op=ALU.add, axis=AX.X),
               reads=[ep[0]], writes=[stat[0]])
            SC("tensor", lambda e, j=j: e.transpose(ps_af[:], maxsl[:, j * 128:(j + 1) * 128],
                                                    idf32_sb[:]),
               reads=[maxsl, idf32_sb], writes=[ps_af])
            SC("vector", lambda e: e.tensor_copy(out=ep[0][:, 0:128], in_=ps_af[:]),
               reads=[ps_af], writes=[ep[0]])
            SC("vector", lambda e: e.tensor_reduce(out=stat[1][:], in_=ep[0][:, 0:128],
                                                   op=ALU.max, axis=AX.X),
               reads=[ep[0]], writes=[stat[1]])
            SC("vector", lambda e: e.tensor_scalar(out=stat[2][:], in0=stat[0][:],
                                                   scalar1=1.0 / II, scalar2=RMS_EPS,
                                                   op0=ALU.mult, op1=ALU.add),
               reads=[stat[0]], writes=[stat[2]])
            SC("scalar", lambda e: e.activation(out=stat[3][:], in_=stat[2][:], func=AF.Sqrt),
               reads=[stat[2]], writes=[stat[3]])
            SC("vector", lambda e: e.reciprocal(out=stat[4][:], in_=stat[3][:]),
               reads=[stat[3]], writes=[stat[4]])
            SC("vector", lambda e: e.tensor_tensor(out=stat[3][:], in0=stat[4][:], in1=stat[4][:],
                                                   op=ALU.mult),
               reads=[stat[4]], writes=[stat[3]])
            SC("vector", lambda e: e.tensor_tensor(out=stat[3][:], in0=stat[3][:], in1=stat[2][:],
                                                   op=ALU.mult),
               reads=[stat[3], stat[2]], writes=[stat[3]])
            SC("vector", lambda e: e.tensor_scalar(out=stat[3][:], in0=stat[3][:], scalar1=-0.5,
                                                   scalar2=1.5, op0=ALU.mult, op1=ALU.add),
               reads=[stat[3]], writes=[stat[3]])
            SC("vector", lambda e: e.tensor_tensor(out=stat[3][:], in0=stat[3][:], in1=stat[4][:],
                                                   op=ALU.mult),
               reads=[stat[3], stat[4]], writes=[stat[3]])
            SC("vector", lambda e: e.tensor_tensor(out=stat[1][:], in0=stat[1][:], in1=stat[3][:],
                                                   op=ALU.mult),
               reads=[stat[1], stat[3]], writes=[stat[1]])
            SC("vector", lambda e: e.tensor_scalar(out=stat[1][:], in0=stat[1][:], scalar1=Q_EPS,
                                                   scalar2=None, op0=ALU.max),
               reads=[stat[1]], writes=[stat[1]])
            SC("vector", lambda e: e.reciprocal(out=stat[4][:], in_=stat[1][:]),
               reads=[stat[1]], writes=[stat[4]])
            SC("vector", lambda e: e.tensor_scalar(out=stat[4][:], in0=stat[4][:], scalar1=127.0,
                                                   scalar2=None, op0=ALU.mult),
               reads=[stat[4]], writes=[stat[4]])
            SC("vector", lambda e: e.tensor_tensor(out=stat[5][:], in0=stat[3][:], in1=stat[4][:],
                                                   op=ALU.mult),
               reads=[stat[3], stat[4]], writes=[stat[5]])
            SC("scalar", lambda e, c=c: e.dma_start(out=rowscr_h[3, c * 128:(c + 1) * 128],
                                                    in_=stat[5][:]),
               reads=[stat[5]], writes=[rowscr_h], dma="st")
            SC("vector", lambda e: e.tensor_scalar(out=stat[6][:], in0=stat[1][:],
                                                   scalar1=1.0 / 127.0, scalar2=None,
                                                   op0=ALU.mult),
               reads=[stat[1]], writes=[stat[6]])
            SC("vector", lambda e, c=c:
               e.tensor_tensor(out=sd_col[:, c:c + 1], in0=stat[6][:], in1=thr["d"]["abc"][:],
                               op=ALU.mult),
               reads=[stat[6], thr["d"]["abc"]], writes=[sd_col])
        SC("sync", lambda e, ts0=ts0: e.dma_start(out=row0[0:1, 0:512],
                                                  in_=rowscr_h[3, ts0:ts0 + 512]),
           reads=[rowscr_h], writes=[row0], dma="ld")
        SC("tensor", lambda e:
           e.matmul(ps_acc[3][:], ones_1x128_f[:], row0[0:1, 0:512], start=True, stop=True),
           reads=[ones_1x128_f, row0], writes=[ps_acc[3]])
        chs = ssqsl
        SC("vector", lambda e: e.tensor_copy(out=chs[:], in_=ps_acc[3][:]),
           reads=[ps_acc[3]], writes=[chs])

    def emit_beta(tsl):
        ts0 = tsl * 512
        chs = ssqsl
        for ig in range(64):
            pl = epp[ig % 2]
            yb = ep[0] if ig % 2 == 0 else ep[4]
            SC("sync", lambda e, b=pl, ig=ig, ts0=ts0:
               e.dma_start(out=b[:], in_=pt_h[ig * 128:(ig + 1) * 128, ts0:ts0 + 512]),
               reads=[pt_h], writes=[pl], dma="ld", tag="p_ld2")
            SC("vector", lambda e, b=pl, o=yb, ig=ig:
               e.scalar_tensor_tensor(out=o[:], in0=b[:], scalar=lnw_sb[:, ig:ig + 1],
                                      in1=chs[:], op0=ALU.mult, op1=ALU.mult),
               reads=[pl, lnw_sb, chs], writes=[yb])
            SC("vector", lambda e, o=yb: e.tensor_scalar(out=ep32[:], in0=o[:], scalar1=MAGIC,
                                                         scalar2=MAGIC, op0=ALU.add,
                                                         op1=ALU.subtract),
               reads=[yb], writes=[ep32])
            SC("vector", lambda e, ig=ig: e.tensor_copy(out=nhT[:, ig * 512:(ig + 1) * 512],
                                                        in_=ep32[:]),
               reads=[ep32], writes=[nhT], tag="nh_cast")

    def emit_down(tsl):
        ts0 = tsl * 512
        for ob in range(4):
            for kc in range(64):
                wt_ = wth[kc % 8]
                q, qe = "ld", "sync"
                SC(qe, lambda e, w=wt_, ob=ob, kc=kc:
                   e.dma_start_transpose(w[:, 0:512], ws_d[ob * 512:(ob + 1) * 512,
                                                          kc * 128:(kc + 1) * 128]),
                   reads=[ws_d], writes=[wt_], dma=q, tag="wd_ld")
                for tb in range(4):
                    SC("tensor", lambda e, w=wt_, kc=kc, tb=tb, b=ps_acc[tb],
                       f=(kc == 0), l=(kc == 63):
                       e.matmul(b[:], nhT[:, kc * 512 + tb * 128: kc * 512 + tb * 128 + 128],
                                w[:, 0:512], start=f, stop=l),
                       reads=[nhT, wt_], writes=[ps_acc[tb]], tag="dmm")
            for tb in range(4):
                bank = ps_acc[tb]
                c = tsl * 4 + tb
                SC("sync", lambda e, c=c, ob=ob:
                   e.dma_start(out=ep[1][:], in_=x_h[c * 128:(c + 1) * 128,
                                                     ob * 512:(ob + 1) * 512]),
                   writes=[ep[1]], dma="ld", tag="x_reload")
                SC("sync", lambda e, c=c, ob=ob:
                   e.dma_start(out=ep[2][:], in_=att_h[c * 128:(c + 1) * 128,
                                                       ob * 512:(ob + 1) * 512]),
                   reads=[att_h], writes=[ep[2]], dma="ld", tag="att_reload")
                SC("vector", lambda e, b=bank, c=c:
                   e.tensor_scalar(out=ep[3][:], in0=b[:], scalar1=sd_col[:, c:c + 1],
                                   scalar2=None, op0=ALU.mult),
                   reads=[bank, sd_col], writes=[ep[3]])
                SC("vector", lambda e: e.tensor_tensor(out=ep[2][:], in0=ep[1][:], in1=ep[2][:],
                                                       op=ALU.add),
                   reads=[ep[1], ep[2]], writes=[ep[2]])
                SC("vector", lambda e: e.tensor_tensor(out=ep[2][:], in0=ep[3][:], in1=ep[2][:],
                                                       op=ALU.add),
                   reads=[ep[3], ep[2]], writes=[ep[2]])
                SC("scalar", lambda e, c=c, ob=ob:
                   e.dma_start(out=out_h[c * 128:(c + 1) * 128, ob * 512:(ob + 1) * 512],
                               in_=ep[2][:]),
                   reads=[ep[2]], dma="st", tag="out_st")


    for tsl in range(4):
        emit_alpha(tsl)
        emit_stats(tsl)
        if tsl > 0:
            emit_down(tsl - 1)
        emit_beta(tsl)
    emit_down(3)

    _probe = []
    import os as _os
    _pk = int(_os.environ.get("SBUF_PROBE_KB", "0"))
    if _pk:
        _probe.append(sb("probe", [128, _pk * 256], F32))
    last_store = sch.steps[-1]
    term = Step("gpsimd", lambda e: e.engine_nop(), [], [], None, len(sch.steps), "term")
    term.waits.append(last_store)
    sch.steps.append(term)

    sch.resolve_and_emit(nc)
    stack.close()
    return nc


def _host_constants():
    j = np.arange(0, HD, 2, dtype=np.float64)
    inv_freq = 1.0 / (10000.0 ** (j / HD))
    t = np.arange(S, dtype=np.float64)
    freqs = np.outer(t, inv_freq)
    emb = np.concatenate([freqs, freqs], axis=-1)
    cos = np.cos(emb).astype(np.float32)
    sin = np.sin(emb).astype(np.float32)
    ii = np.arange(128)
    mask = (ii[:, None] <= ii[None, :]).astype(ml_dtypes.bfloat16)
    idbf = np.eye(128, dtype=ml_dtypes.bfloat16)
    idf32 = np.eye(128, dtype=np.float32)
    return cos, sin, mask, idbf, idf32


_NC_CACHE = {}


def kernel(L0_attn_x, L0_attn_Wq, L0_attn_Wk, L0_attn_Wv, L0_attn_Wo,
           L0_ffn_w_gate, L0_ffn_w_up, L0_ffn_w_down, L0_ffn_ln_weight):
    x = np.ascontiguousarray(np.asarray(L0_attn_x, dtype=np.float32))
    B = x.shape[0]
    cos, sin, mask, idbf, idf32 = _host_constants()
    common = {
        "wq": np.ascontiguousarray(np.asarray(L0_attn_Wq, np.float32)),
        "wk": np.ascontiguousarray(np.asarray(L0_attn_Wk, np.float32)),
        "wv": np.ascontiguousarray(np.asarray(L0_attn_Wv, np.float32)),
        "wo": np.ascontiguousarray(np.asarray(L0_attn_Wo, np.float32)),
        "wg": np.ascontiguousarray(np.asarray(L0_ffn_w_gate, np.float32)),
        "wu": np.ascontiguousarray(np.asarray(L0_ffn_w_up, np.float32)),
        "wd": np.ascontiguousarray(np.asarray(L0_ffn_w_down, np.float32)),
        "lnw": np.ascontiguousarray(np.asarray(L0_ffn_ln_weight, np.float32)),
        "costab": cos, "sintab": sin, "masktab": mask,
        "idbf": idbf, "idf32": idf32,
    }
    if "nc" not in _NC_CACHE:
        _NC_CACHE["nc"] = build_program()
    nc = _NC_CACHE["nc"]
    in_maps = [dict(common, x=np.ascontiguousarray(x[b])) for b in range(B)]
    res = run_bass_kernel_spmd(nc, in_maps, core_ids=list(range(B)))
    return np.stack([res.results[b]["out"] for b in range(B)], axis=0)


if __name__ == "__main__":
    import time
    t0 = time.time()
    nc = build_program()
    print(f"built ok in {time.time() - t0:.1f}s, "
          f"instructions={sum(1 for _ in nc.all_instructions())}")


# version 22 (final)
# speedup vs baseline: 1.3848x; 1.0574x over previous
import numpy as np
import ml_dtypes
import concourse.bass as bass
import concourse.mybir as mybir
from concourse.bass_utils import run_bass_kernel_spmd

F32 = mybir.dt.float32
BF16 = mybir.dt.bfloat16
I32 = mybir.dt.int32
AF = mybir.ActivationFunctionType
ALU = mybir.AluOpType
AX = mybir.AxisListType

MAGIC = 12582912.0

S = 2048
D = 2048
II = 8192
HD = 128
NQH = 16
NKV = 4
NCH = S // 128
GRP = 2
NGRP = NCH // GRP
GT = GRP * 128
LN_EPS = 1e-5
RMS_EPS = 1e-6
ATT_EPS = 1e-6
Q_EPS = 1e-5
W_EPS = 1e-5

ENGINES = ("sync", "vector", "scalar", "tensor", "gpsimd")

DEBUG_SINK = None


class Step:
    __slots__ = ("eng", "emit", "reads", "writes", "dma", "idx",
                 "needs_inc", "count", "waits", "tag", "slot", "gen")

    def __init__(self, eng, emit, reads, writes, dma, idx, tag):
        self.eng = eng
        self.emit = emit
        self.reads = reads
        self.writes = writes
        self.dma = dma
        self.idx = idx
        self.needs_inc = False
        self.count = None
        self.waits = []
        self.tag = tag


class Sched:

    def __init__(self):
        self.steps = []
        self.last_writer = {}
        self.readers = {}

    def do(self, eng, emit, reads=(), writes=(), dma=None, tag=""):
        st = Step(eng, emit, list(reads), list(writes), dma, len(self.steps), tag)
        deps = []
        for b in st.reads:
            w = self.last_writer.get(id(b))
            if w is not None:
                deps.append(w)
        for b in st.writes:
            w = self.last_writer.get(id(b))
            if w is not None:
                deps.append(w)
            deps.extend(self.readers.get(id(b), ()))
        seen = set()
        for d in deps:
            if d.idx in seen or d is st:
                continue
            seen.add(d.idx)
            if (d.dma is None and st.dma is None and d.eng == st.eng
                    and d.eng == "tensor"):
                continue
            st.waits.append(d)
        for b in st.reads:
            self.readers.setdefault(id(b), []).append(st)
        for b in st.writes:
            self.last_writer[id(b)] = st
            self.readers[id(b)] = []
        self.steps.append(st)
        return st

    DMA_SLOTS = 16

    def _semkey(self, st):
        if st.dma is not None:
            return ("dma", st.dma, st.slot)
        return ("eng", st.eng)

    def resolve_and_emit(self, nc):
        P = self.DMA_SLOTS
        qidx = {}
        for st in self.steps:
            if st.dma is not None:
                i = qidx.get(st.dma, 0)
                qidx[st.dma] = i + 1
                st.slot = i % P
                st.gen = i // P
                st.needs_inc = True
                st.count = 16 * (st.gen + 1)
            for d in st.waits:
                d.needs_inc = True
        cnt = {}
        for st in self.steps:
            if st.needs_inc and st.dma is None:
                key = self._semkey(st)
                cnt[key] = cnt.get(key, 0) + 1
                st.count = cnt[key]
        from contextlib import ExitStack
        self._semstack = ExitStack()
        sems = {}
        for st in self.steps:
            if st.needs_inc:
                key = self._semkey(st)
                if key not in sems:
                    sems[key] = self._semstack.enter_context(
                        nc.semaphore("sem_" + "_".join(str(k) for k in key[1:])))
        hw = {e: {} for e in ENGINES}
        per_eng = {e: [] for e in ENGINES}
        for st in self.steps:
            per_eng[st.eng].append(st)

        with nc.Block() as block:
            def make_body(eng_name):
                def body(eng):
                    marks = hw[eng_name]
                    for st in per_eng[eng_name]:
                        if st.dma is not None and st.gen > 0:
                            key = self._semkey(st)
                            v = 16 * st.gen
                            if marks.get(key, 0) < v:
                                eng.wait_ge(sems[key], v)
                                marks[key] = v
                        for d in st.waits:
                            key = self._semkey(d)
                            v = d.count
                            if marks.get(key, 0) >= v:
                                continue
                            eng.wait_ge(sems[key], v)
                            marks[key] = v
                        inst = st.emit(eng)
                        if st.needs_inc:
                            key = self._semkey(st)
                            inst.then_inc(sems[key], 16 if st.dma else 1)
                return body
            block.sync(make_body("sync"))
            block.vector(make_body("vector"))
            block.scalar(make_body("scalar"))
            block.tensor(make_body("tensor"))
            block.gpsimd(make_body("gpsimd"))
        return sems


def build_program():
    nc = bass.Bass("TRN2", target_bir_lowering=False, debug=False)
    sch = Sched()
    from contextlib import ExitStack
    stack = ExitStack()

    def sb(name, shape, dtype):
        return stack.enter_context(nc.sbuf_tensor(name, shape, dtype))

    def psu(name, shape, dtype=F32):
        return stack.enter_context(nc.psum_tensor(name, shape, dtype))

    x_h = nc.dram_tensor("x", [S, D], F32, kind="ExternalInput").ap()
    wq_h = nc.dram_tensor("wq", [D, D], F32, kind="ExternalInput").ap()
    wk_h = nc.dram_tensor("wk", [512, D], F32, kind="ExternalInput").ap()
    wv_h = nc.dram_tensor("wv", [512, D], F32, kind="ExternalInput").ap()
    wo_h = nc.dram_tensor("wo", [D, D], F32, kind="ExternalInput").ap()
    wg_h = nc.dram_tensor("wg", [II, D], F32, kind="ExternalInput").ap()
    wu_h = nc.dram_tensor("wu", [II, D], F32, kind="ExternalInput").ap()
    wd_h = nc.dram_tensor("wd", [D, II], F32, kind="ExternalInput").ap()
    lnw_h = nc.dram_tensor("lnw", [II], F32, kind="ExternalInput").ap()
    cos_h = nc.dram_tensor("costab", [S, HD], F32, kind="ExternalInput").ap()
    sin_h = nc.dram_tensor("sintab", [S, HD], F32, kind="ExternalInput").ap()
    mask_h = nc.dram_tensor("masktab", [128, 128], BF16, kind="ExternalInput").ap()
    idbf_h = nc.dram_tensor("idbf", [128, 128], BF16, kind="ExternalInput").ap()
    idf32_h = nc.dram_tensor("idf32", [128, 128], F32, kind="ExternalInput").ap()
    out_h = nc.dram_tensor("out", [S, D], F32, kind="ExternalOutput").ap()

    ws_q = nc.dram_tensor("ws_q", [4, 2, 512, D], BF16, kind="Internal").ap()
    ws_k = nc.dram_tensor("ws_k", [1, 2, 512, D], BF16, kind="Internal").ap()
    ws_v = nc.dram_tensor("ws_v", [1, 2, 512, D], BF16, kind="Internal").ap()
    ws_o = nc.dram_tensor("ws_o", [4, 2, 512, D], BF16, kind="Internal").ap()
    ws_gu = nc.dram_tensor("ws_gu", [32, 2, 256, D], BF16, kind="Internal").ap()
    ws_d = nc.dram_tensor("ws_d", [D, II], BF16, kind="Internal").ap()
    pt_h = nc.dram_tensor("pt", [II, S], F32, kind="Internal").ap()
    att_h = nc.dram_tensor("att_sp", [S, D], F32, kind="Internal").ap()
    rowscr_h = nc.dram_tensor("rowscr", [4, S], F32, kind="Internal").ap()

    mask_sb = sb("mask_sb", [128, 128], BF16)
    idbf_sb = sb("idbf_sb", [128, 128], BF16)
    idf32_sb = sb("idf32_sb", [128, 128], F32)
    lnw_tmp = sb("lnw_tmp", [64, 128], F32)
    lnw_sb = sb("lnw_sb", [128, 64], F32)
    alnw_sb = sb("alnw_sb", [128, 64], F32)
    ones_1x128_f = sb("ones_1x128_f", [1, 128], F32)
    nxT = sb("nxT", [128, NCH * S], BF16)
    nhT = nxT
    sxinv_col = sb("sxinv_col", [128, NCH], F32)
    Bsx = sb("Bsx", [128, S], F32)
    sd_col = sb("sd_col", [128, NCH], F32)
    thr = {}
    for nm in ("g", "u", "d"):
        thr[nm] = dict(
            racc=sb(f"racc_{nm}", [128, 1], F32),
            a=sb(f"a_{nm}", [1, 1], F32),
            abc=sb(f"abc_{nm}", [128, 1], F32),
            anbc=sb(f"anbc_{nm}", [128, 1], F32),
        )
    st32 = [sb(f"st32_{g}", [128, 132], F32) for g in range(NKV)]
    sth = [sb(f"sth_{g}", [128, 132], BF16) for g in range(NKV)]
    stl = [sb(f"stl_{g}", [128, 132], BF16) for g in range(NKV)]

    p0 = sb("p0", [128, D], F32)
    p1 = sb("p1", [128, D], F32)
    xb0 = sb("xb0", [128, D], F32)
    fwbuf = [xb0, p0, p1]
    nxbf = sb("nxbf", [128, D], BF16)
    ternbf = nxbf
    hsp_bf = sb("hsp_bf", [128, D], BF16)
    lsp_bf = sb("lsp_bf", [128, D], BF16)
    stat = [sb(f"stat{i}", [128, 1], F32) for i in range(8)]
    row0 = sb("row0", [1, 512], F32)
    ep = [sb(f"ep{i}", [128, 512], F32) for i in range(5)]
    epp = [sb(f"epp{i}", [128, 512], F32) for i in range(2)]
    ep32 = sb("epi32", [128, 512], I32)
    qro = sb("qro", [128, 512], F32)
    ssqsl = sb("ssqsl", [128, 512], F32)
    maxsl = sb("maxsl", [128, 512], F32)
    cos_sb = sb("cos_sb", [128, 128], F32)
    sin_sb = sb("sin_sb", [128, 128], F32)

    xnT = [sb(f"xnT_{p}", [128, NCH * GT], BF16) for p in "hl"]
    attT = xnT
    qfT = [sb(f"qfT_{p}", [128, NQH * GT], BF16) for p in "hl"]
    kfT = [sb(f"kfT_{p}", [128, NKV * GT], BF16) for p in "hl"]
    kftok = [sb(f"kftok_{p}", [128, GRP * 512], BF16) for p in "hl"]
    vaug = [sb(f"vaug_{p}", [128, GRP * NKV * 132], BF16) for p in "hl"]
    amT = [[sb(f"amT_{p}{i}", [128, 128], BF16) for p in "hl"] for i in range(2)]
    am32 = [sb(f"am32_{i}", [128, 128], F32) for i in range(2)]

    wth = [sb(f"wth{i}", [128, 1024], BF16) for i in range(8)]

    ps_acc = [psu(f"ps_acc{i}", [128, 512], F32) for i in range(4)]
    ps_tr = [psu(f"ps_tr{i}", [128, 128], BF16) for i in range(2)]
    ps_af = psu("ps_af", [128, 128], F32)
    ps_af2 = psu("ps_af2", [128, 128], F32)

    SC = sch.do

    SC("sync", lambda e: e.dma_start(out=mask_sb[:], in_=mask_h), writes=[mask_sb], dma="ld")
    SC("sync", lambda e: e.dma_start(out=idbf_sb[:], in_=idbf_h), writes=[idbf_sb], dma="ld")
    SC("sync", lambda e: e.dma_start(out=idf32_sb[:], in_=idf32_h), writes=[idf32_sb], dma="ld")
    SC("sync", lambda e: e.dma_start(out=lnw_tmp[:], in_=lnw_h.rearrange("(a b) -> a b", b=128)),
       writes=[lnw_tmp], dma="ld")
    SC("gpsimd", lambda e: e.memset(ones_1x128_f[:], 1.0), writes=[ones_1x128_f])
    for g in range(NKV):
        SC("gpsimd", lambda e, g=g: e.memset(st32[g][:], 0.0), writes=[st32[g]])
    SC("tensor", lambda e: e.transpose(ps_af[:, 0:64], lnw_tmp[:], idf32_sb[0:64, 0:64]),
       reads=[lnw_tmp, idf32_sb], writes=[ps_af])
    SC("vector", lambda e: e.tensor_copy(out=lnw_sb[:], in_=ps_af[:, 0:64]),
       reads=[ps_af], writes=[lnw_sb])
    SC("scalar", lambda e: e.activation(out=alnw_sb[:], in_=lnw_sb[:], func=AF.Abs),
       reads=[lnw_sb], writes=[alnw_sb])

    for (w_hbm, w_scr, nrow) in ((wq_h, ws_q, D), (wk_h, ws_k, 512),
                                 (wv_h, ws_v, 512), (wo_h, ws_o, D)):
        for t in range(nrow // 128):
            buf = fwbuf[t % 3]
            r0 = t * 128
            ob, rr = r0 // 512, r0 % 512
            SC("sync", lambda e, b=buf, h=w_hbm, r0=r0:
               e.dma_start(out=b[:], in_=h[r0:r0 + 128, :]), writes=[buf], dma="ld")
            SC("vector", lambda e, b=buf: e.tensor_copy(out=hsp_bf[:], in_=b[:]),
               reads=[buf], writes=[hsp_bf])
            SC("vector", lambda e, b=buf:
               e.tensor_tensor(out=b[:], in0=b[:], in1=hsp_bf[:], op=ALU.subtract),
               reads=[buf, hsp_bf], writes=[buf])
            SC("vector", lambda e, b=buf: e.tensor_copy(out=lsp_bf[:], in_=b[:]),
               reads=[buf], writes=[lsp_bf])
            SC("scalar", lambda e, s=w_scr, ob=ob, rr=rr:
               e.dma_start(out=s[ob, 0, rr:rr + 128, :], in_=hsp_bf[:]), reads=[hsp_bf],
               writes=[w_scr], dma="st")
            SC("scalar", lambda e, s=w_scr, ob=ob, rr=rr:
               e.dma_start(out=s[ob, 1, rr:rr + 128, :], in_=lsp_bf[:]), reads=[lsp_bf],
               writes=[w_scr], dma="st")

    def ffn_subtiles(nrow, ncol):
        for r in range(nrow // 128):
            for cs in range(ncol // D):
                yield r * 128, cs * D

    ffn_mats = (("g", wg_h, 0, II, D), ("u", wu_h, 1, II, D), ("d", wd_h, None, D, II))
    for nm, w_hbm, w_scr, nrow, ncol in ffn_mats:
        t = thr[nm]
        SC("gpsimd", lambda e, t=t: e.memset(t["racc"][:], 0.0), writes=[t["racc"]])
        ti = 0
        for r0, c0 in ffn_subtiles(nrow, ncol):
            buf = fwbuf[ti % 3]
            ti += 1
            SC("sync", lambda e, b=buf, h=w_hbm, r0=r0, c0=c0:
               e.dma_start(out=b[:], in_=h[r0:r0 + 128, c0:c0 + D]), writes=[buf], dma="ld")
            SC("scalar", lambda e, b=buf:
               e.activation(out=b[:], in_=b[:], func=AF.Abs, accum_out=stat[0][:]),
               reads=[buf], writes=[buf, stat[0]])
            SC("vector", lambda e, t=t:
               e.tensor_tensor(out=t["racc"][:], in0=t["racc"][:], in1=stat[0][:], op=ALU.add),
               reads=[stat[0], t["racc"]], writes=[t["racc"]])
        SC("tensor", lambda e, t=t:
           e.transpose(ps_af[0:1, 0:128], t["racc"][:], idf32_sb[:]),
           reads=[t["racc"], idf32_sb], writes=[ps_af], tag="meanred")
        SC("vector", lambda e: e.tensor_copy(out=row0[0:1, 0:128], in_=ps_af[0:1, 0:128]),
           reads=[ps_af], writes=[row0])
        nelem = float(nrow * ncol)
        SC("vector", lambda e, t=t:
           e.tensor_reduce(out=t["a"][:], in_=row0[0:1, 0:128], op=ALU.add, axis=AX.X),
           reads=[row0], writes=[t["a"]])
        SC("vector", lambda e, t=t, n=nelem:
           e.tensor_scalar(out=t["a"][:], in0=t["a"][:], scalar1=1.0 / n,
                           scalar2=W_EPS, op0=ALU.mult, op1=ALU.max),
           reads=[t["a"]], writes=[t["a"]])
        SC("vector", lambda e, t=t:
           e.tensor_scalar(out=t["a"][:], in0=t["a"][:], scalar1=0.5, scalar2=None, op0=ALU.mult),
           reads=[t["a"]], writes=[t["a"]])
        SC("tensor", lambda e, t=t:
           e.matmul(ps_acc[3][:, 0:1], ones_1x128_f[:], t["a"][:], start=True, stop=True),
           reads=[ones_1x128_f, t["a"]], writes=[ps_acc[3]])
        SC("vector", lambda e, t=t: e.tensor_copy(out=t["abc"][:], in_=ps_acc[3][:, 0:1]),
           reads=[ps_acc[3]], writes=[t["abc"]])
        SC("vector", lambda e, t=t:
           e.tensor_scalar(out=t["anbc"][:], in0=t["abc"][:], scalar1=-1.0, scalar2=None,
                           op0=ALU.mult),
           reads=[t["abc"]], writes=[t["anbc"]])

    for nm, w_hbm, w_scr, nrow, ncol in ffn_mats:
        t = thr[nm]
        ti = 0
        for r0, c0 in ffn_subtiles(nrow, ncol):
            buf = fwbuf[ti % 3]
            ti += 1
            SC("sync", lambda e, b=buf, h=w_hbm, r0=r0, c0=c0:
               e.dma_start(out=b[:], in_=h[r0:r0 + 128, c0:c0 + D]), writes=[buf], dma="ld")
            SC("scalar", lambda e, b=buf, t=t:
               e.activation(out=hsp_bf[:], in_=b[:], func=AF.Sign, bias=t["anbc"][:]),
               reads=[buf, t["anbc"]], writes=[hsp_bf])
            SC("scalar", lambda e, b=buf, t=t:
               e.activation(out=lsp_bf[:], in_=b[:], func=AF.Sign, bias=t["abc"][:]),
               reads=[buf, t["abc"]], writes=[lsp_bf])
            SC("vector", lambda e:
               e.tensor_tensor(out=ternbf[:], in0=hsp_bf[:], in1=lsp_bf[:], op=ALU.add),
               reads=[hsp_bf, lsp_bf], writes=[ternbf])
            if w_scr is None:
                SC("scalar", lambda e, r0=r0, c0=c0:
                   e.dma_start(out=ws_d[r0:r0 + 128, c0:c0 + D], in_=ternbf[:]),
                   reads=[ternbf], writes=[ws_d], dma="st")
            else:
                SC("scalar", lambda e, gu=w_scr, ic=r0 // 256, rr=r0 % 256, c0=c0:
                   e.dma_start(out=ws_gu[ic, gu, rr:rr + 128, c0:c0 + D], in_=ternbf[:]),
                   reads=[ternbf], writes=[ws_gu], dma="st")

    for grp in range(NGRP):
        for cl in range(GRP):
            c = grp * GRP + cl
            SC("sync", lambda e, r0=c * 128:
               e.dma_start(out=xb0[:], in_=x_h[r0:r0 + 128, :]), writes=[xb0], dma="ld")
            SC("vector", lambda e: e.tensor_reduce(out=stat[0][:], in_=xb0[:], op=ALU.add,
                                                   axis=AX.X),
               reads=[xb0], writes=[stat[0]])
            SC("scalar", lambda e:
               e.activation(out=p1[:], in_=xb0[:], func=AF.Square, accum_out=stat[1][:]),
               reads=[xb0], writes=[p1, stat[1]])
            SC("vector", lambda e: e.tensor_scalar(out=stat[2][:], in0=stat[0][:],
                                                   scalar1=1.0 / D, scalar2=None, op0=ALU.mult),
               reads=[stat[0]], writes=[stat[2]])
            SC("vector", lambda e: e.tensor_scalar(out=stat[3][:], in0=stat[1][:],
                                                   scalar1=1.0 / D, scalar2=LN_EPS,
                                                   op0=ALU.mult, op1=ALU.add),
               reads=[stat[1]], writes=[stat[3]])
            SC("vector", lambda e: e.tensor_tensor(out=stat[4][:], in0=stat[2][:], in1=stat[2][:],
                                                   op=ALU.mult),
               reads=[stat[2]], writes=[stat[4]])
            SC("vector", lambda e: e.tensor_tensor(out=stat[3][:], in0=stat[3][:], in1=stat[4][:],
                                                   op=ALU.subtract),
               reads=[stat[3], stat[4]], writes=[stat[3]])
            SC("scalar", lambda e: e.activation(out=stat[4][:], in_=stat[3][:], func=AF.Sqrt),
               reads=[stat[3]], writes=[stat[4]])
            SC("vector", lambda e: e.reciprocal(out=stat[5][:], in_=stat[4][:]),
               reads=[stat[4]], writes=[stat[5]])
            SC("vector", lambda e: e.tensor_tensor(out=stat[4][:], in0=stat[5][:], in1=stat[5][:],
                                                   op=ALU.mult),
               reads=[stat[5]], writes=[stat[4]])
            SC("vector", lambda e: e.tensor_tensor(out=stat[4][:], in0=stat[4][:], in1=stat[3][:],
                                                   op=ALU.mult),
               reads=[stat[4], stat[3]], writes=[stat[4]])
            SC("vector", lambda e: e.tensor_scalar(out=stat[4][:], in0=stat[4][:], scalar1=-0.5,
                                                   scalar2=1.5, op0=ALU.mult, op1=ALU.add),
               reads=[stat[4]], writes=[stat[4]])
            SC("vector", lambda e: e.tensor_tensor(out=stat[4][:], in0=stat[4][:], in1=stat[5][:],
                                                   op=ALU.mult),
               reads=[stat[4], stat[5]], writes=[stat[4]])
            SC("vector", lambda e: e.tensor_tensor(out=stat[5][:], in0=stat[2][:], in1=stat[4][:],
                                                   op=ALU.mult),
               reads=[stat[2], stat[4]], writes=[stat[5]])
            SC("vector", lambda e: e.tensor_scalar(out=stat[5][:], in0=stat[5][:], scalar1=-1.0,
                                                   scalar2=None, op0=ALU.mult),
               reads=[stat[5]], writes=[stat[5]])
            xn = p0
            SC("scalar", lambda e:
               e.activation(out=xn[:], in_=xb0[:], func=AF.Identity, bias=stat[5][:],
                            scale=stat[4][:]),
               reads=[xb0, stat[4], stat[5]], writes=[xn])
            SC("vector", lambda e: e.tensor_copy(out=hsp_bf[:], in_=xn[:]),
               reads=[xn], writes=[hsp_bf])
            SC("vector", lambda e: e.tensor_tensor(out=p1[:], in0=xn[:], in1=hsp_bf[:],
                                                   op=ALU.subtract),
               reads=[xn, hsp_bf], writes=[p1])
            SC("vector", lambda e: e.tensor_copy(out=lsp_bf[:], in_=p1[:]),
               reads=[p1], writes=[lsp_bf])
            SC("scalar", lambda e: e.activation(out=p1[:], in_=xn[:], func=AF.Abs),
               reads=[xn], writes=[p1])
            SC("vector", lambda e: e.tensor_reduce(out=stat[0][:], in_=p1[:], op=ALU.max,
                                                   axis=AX.X),
               reads=[p1], writes=[stat[0]])
            SC("vector", lambda e: e.tensor_scalar(out=stat[1][:], in0=stat[0][:], scalar1=Q_EPS,
                                                   scalar2=None, op0=ALU.max),
               reads=[stat[0]], writes=[stat[1]])
            SC("vector", lambda e, c=c: e.tensor_scalar(out=sxinv_col[:, c:c + 1], in0=stat[1][:],
                                                        scalar1=1.0 / 127.0, scalar2=None,
                                                        op0=ALU.mult),
               reads=[stat[1]], writes=[sxinv_col])
            SC("vector", lambda e: e.reciprocal(out=stat[2][:], in_=stat[1][:]),
               reads=[stat[1]], writes=[stat[2]])
            SC("vector", lambda e: e.tensor_scalar(out=stat[2][:], in0=stat[2][:], scalar1=127.0,
                                                   scalar2=None, op0=ALU.mult),
               reads=[stat[2]], writes=[stat[2]])
            for sl4 in range(4):
                cs = slice(sl4 * 512, (sl4 + 1) * 512)
                SC("vector", lambda e, cs=cs: e.tensor_scalar(out=ep[0][:], in0=xn[:, cs],
                                                              scalar1=stat[2][:], scalar2=None,
                                                              op0=ALU.mult),
                   reads=[xn, stat[2]], writes=[ep[0]])
                SC("vector", lambda e: e.tensor_scalar(out=ep32[:], in0=ep[0][:], scalar1=MAGIC,
                                                       scalar2=MAGIC, op0=ALU.add,
                                                       op1=ALU.subtract),
                   reads=[ep[0]], writes=[ep32])
                SC("gpsimd", lambda e, cs=cs: e.tensor_copy(out=nxbf[:, cs], in_=ep32[:]),
                   reads=[ep32], writes=[nxbf])
            for kc in range(16):
                bank = ps_tr[kc % 2]
                SC("tensor", lambda e, b=bank, kc=kc:
                   e.transpose(b[:], nxbf[:, kc * 128:(kc + 1) * 128], idbf_sb[:]),
                   reads=[nxbf, idbf_sb], writes=[bank])
                SC("scalar", lambda e, b=bank, kc=kc, c=c:
                   e.copy(out=nxT[:, kc * S + c * 128: kc * S + c * 128 + 128], in_=b[:]),
                   reads=[bank], writes=[nxT])
            for (src, dstT) in ((hsp_bf, xnT[0]), (lsp_bf, xnT[1])):
                for kc in range(16):
                    bank = ps_tr[kc % 2]
                    SC("tensor", lambda e, b=bank, s=src, kc=kc:
                       e.transpose(b[:], s[:, kc * 128:(kc + 1) * 128], idbf_sb[:]),
                       reads=[src, idbf_sb], writes=[bank])
                    SC("vector", lambda e, b=bank, d=dstT, kc=kc, cl=cl:
                       e.tensor_copy(out=d[:, kc * GT + cl * 128: kc * GT + cl * 128 + 128],
                                     in_=b[:]),
                       reads=[bank], writes=[dstT])

        def rope_elu(bank, c, dst):
            SC("sync", lambda e, c=c: e.dma_start(out=cos_sb[:],
                                                  in_=cos_h[c * 128:(c + 1) * 128, :]),
               writes=[cos_sb], dma="ld")
            SC("sync", lambda e, c=c: e.dma_start(out=sin_sb[:],
                                                  in_=sin_h[c * 128:(c + 1) * 128, :]),
               writes=[sin_sb], dma="ld")
            qc, qs, ro = ep[0], ep[1], qro
            for hh in range(4):
                sl = slice(hh * 128, (hh + 1) * 128)
                SC("vector", lambda e, b=bank, sl=sl:
                   e.tensor_tensor(out=qc[:, sl], in0=b[:, sl], in1=cos_sb[:], op=ALU.mult),
                   reads=[bank, cos_sb], writes=[qc])
                SC("vector", lambda e, b=bank, sl=sl:
                   e.tensor_tensor(out=qs[:, sl], in0=b[:, sl], in1=sin_sb[:], op=ALU.mult),
                   reads=[bank, sin_sb], writes=[qs])
            for hh in range(4):
                b0 = hh * 128
                SC("vector", lambda e, b0=b0:
                   e.tensor_tensor(out=ro[:, b0:b0 + 64], in0=qc[:, b0:b0 + 64],
                                   in1=qs[:, b0 + 64:b0 + 128], op=ALU.subtract),
                   reads=[qc, qs], writes=[ro])
                SC("vector", lambda e, b0=b0:
                   e.tensor_tensor(out=ro[:, b0 + 64:b0 + 128], in0=qc[:, b0 + 64:b0 + 128],
                                   in1=qs[:, b0:b0 + 64], op=ALU.add),
                   reads=[qc, qs], writes=[ro])
            SC("vector", lambda e: e.tensor_scalar(out=ep[2][:], in0=ro[:], scalar1=0.0,
                                                   scalar2=None, op0=ALU.min),
               reads=[ro], writes=[ep[2]])
            SC("scalar", lambda e: e.activation(out=ep[3][:], in_=ep[2][:], func=AF.Exp),
               reads=[ep[2]], writes=[ep[3]])
            SC("vector", lambda e: e.scalar_tensor_tensor(out=dst[:], in0=ro[:], scalar=0.0,
                                                          in1=ep[3][:], op0=ALU.max, op1=ALU.add),
               reads=[ro, ep[3]], writes=[dst])

        def proj(w_scr, nblk, evac, tag):
            for ob in range(nblk):
                for kph in range(4):
                    for j in range(4):
                        kc = kph * 4 + j
                        SC("sync", lambda e, w=wth[kc % 8], s=w_scr, ob=ob, kc=kc:
                           e.dma_start_transpose(
                               w[:], s[ob].rearrange("a b c -> (a b) c")[
                                   :, kc * 128:(kc + 1) * 128]),
                           reads=[w_scr], writes=[wth[kc % 8]], dma="ld", tag=tag + "_wld")
                    for tb in range(GRP):
                        bank = ps_acc[tb]
                        for j in range(4):
                            kc = kph * 4 + j
                            xh_t = xnT[0][:, kc * GT + tb * 128: kc * GT + tb * 128 + 128]
                            xl_t = xnT[1][:, kc * GT + tb * 128: kc * GT + tb * 128 + 128]
                            SC("tensor", lambda e, b=bank, w=wth[kc % 8], xt=xh_t,
                               f=(kph == 0 and j == 0):
                               e.matmul(b[:], xt, w[:, 0:512], start=f, stop=False),
                               reads=[xnT[0], wth[kc % 8]], writes=[bank], tag=tag + "_mm")
                            SC("tensor", lambda e, b=bank, w=wth[kc % 8], xt=xh_t:
                               e.matmul(b[:], xt, w[:, 512:1024], start=False, stop=False),
                               reads=[xnT[0], wth[kc % 8]], writes=[bank])
                            SC("tensor", lambda e, b=bank, w=wth[kc % 8], xt=xl_t,
                               l=(kph == 3 and j == 3):
                               e.matmul(b[:], xt, w[:, 0:512], start=False, stop=l),
                               reads=[xnT[1], wth[kc % 8]], writes=[bank])
                for tb in range(GRP):
                    evac(tb, ob, ps_acc[tb])

        def q_evac(tb, ob, bank):
            c = grp * GRP + tb
            rope_elu(bank, c, qro)
            qh = nxbf
            SC("vector", lambda e: e.tensor_copy(out=qh[:, 0:512], in_=qro[:]),
               reads=[qro], writes=[qh])
            SC("vector", lambda e: e.tensor_tensor(out=ep[4][:], in0=qro[:], in1=qh[:, 0:512],
                                                   op=ALU.subtract),
               reads=[qro, qh], writes=[ep[4]])
            SC("vector", lambda e: e.tensor_copy(out=qh[:, 512:1024], in_=ep[4][:]),
               reads=[ep[4]], writes=[qh])
            for hh in range(4):
                hg = ob * 4 + hh
                for pi in range(2):
                    bank2 = ps_tr[(hh * 2 + pi) % 2]
                    SC("tensor", lambda e, b=bank2, pi=pi, hh=hh:
                       e.transpose(b[:], qh[:, pi * 512 + hh * 128: pi * 512 + hh * 128 + 128],
                                   idbf_sb[:]),
                       reads=[qh, idbf_sb], writes=[bank2])
                    SC("vector", lambda e, b=bank2, pi=pi, hg=hg, tb=tb:
                       e.tensor_copy(out=qfT[pi][:, hg * GT + tb * 128: hg * GT + tb * 128 + 128],
                                     in_=b[:]),
                       reads=[bank2], writes=[qfT[pi]])

        def k_evac(tb, ob, bank):
            c = grp * GRP + tb
            rope_elu(bank, c, qro)
            SC("vector", lambda e, tb=tb: e.tensor_copy(out=kftok[0][:, tb * 512:(tb + 1) * 512],
                                                        in_=qro[:]),
               reads=[qro], writes=[kftok[0]])
            SC("vector", lambda e, tb=tb:
               e.tensor_tensor(out=ep[4][:], in0=qro[:],
                               in1=kftok[0][:, tb * 512:(tb + 1) * 512], op=ALU.subtract),
               reads=[qro, kftok[0]], writes=[ep[4]])
            SC("vector", lambda e, tb=tb: e.tensor_copy(out=kftok[1][:, tb * 512:(tb + 1) * 512],
                                                        in_=ep[4][:]),
               reads=[ep[4]], writes=[kftok[1]])
            for hh in range(NKV):
                for pi in range(2):
                    bank2 = ps_tr[(hh * 2 + pi) % 2]
                    SC("tensor", lambda e, b=bank2, pi=pi, hh=hh, tb=tb:
                       e.transpose(b[:],
                                   kftok[pi][:, tb * 512 + hh * 128: tb * 512 + hh * 128 + 128],
                                   idbf_sb[:]),
                       reads=[kftok[pi], idbf_sb], writes=[bank2])
                    SC("vector", lambda e, b=bank2, pi=pi, hh=hh, tb=tb:
                       e.tensor_copy(out=kfT[pi][:, hh * GT + tb * 128: hh * GT + tb * 128 + 128],
                                     in_=b[:]),
                       reads=[bank2], writes=[kfT[pi]])

        def v_evac(tb, ob, bank):
            SC("vector", lambda e, b=bank: e.tensor_copy(out=qro[:], in_=b[:]),
               reads=[bank], writes=[qro])
            for g in range(NKV):
                slot = (tb * NKV + g) * 132
                SC("vector", lambda e, g=g, slot=slot:
                   e.tensor_copy(out=vaug[0][:, slot:slot + 128],
                                 in_=qro[:, g * 128:(g + 1) * 128]),
                   reads=[qro], writes=[vaug[0]])
                SC("vector", lambda e, g=g, slot=slot:
                   e.tensor_tensor(out=ep[4][:, 0:128], in0=qro[:, g * 128:(g + 1) * 128],
                                   in1=vaug[0][:, slot:slot + 128], op=ALU.subtract),
                   reads=[qro, vaug[0]], writes=[ep[4]])
                SC("vector", lambda e, slot=slot:
                   e.tensor_copy(out=vaug[1][:, slot:slot + 128], in_=ep[4][:, 0:128]),
                   reads=[ep[4]], writes=[vaug[1]])
                SC("gpsimd", lambda e, slot=slot: e.memset(vaug[0][:, slot + 128:slot + 129], 1.0),
                   writes=[vaug[0]])
                SC("gpsimd", lambda e, slot=slot: e.memset(vaug[1][:, slot + 128:slot + 129], 0.0),
                   writes=[vaug[1]])

        proj(ws_q, 4, q_evac, "qproj")
        proj(ws_k, 1, k_evac, "kproj")
        proj(ws_v, 1, v_evac, "vproj")
        if DEBUG_SINK:
            DEBUG_SINK(f"proj{grp}", nc, SC, dict(qfT=qfT, kfT=kfT, kftok=kftok,
                                                  vaug=vaug, xnT=xnT))

        for cl in range(GRP):
            c = grp * GRP + cl
            att_t = p0
            for h in range(NQH):
                g = h // 4
                pa = ps_af if h % 2 == 0 else ps_af2
                amt = amT[h % 2]
                a32 = am32[h % 2]
                nb = ps_acc[1] if h % 2 == 0 else ps_acc[3]
                ks = [kfT[p][:, g * GT + cl * 128: g * GT + cl * 128 + 128] for p in range(2)]
                qs_ = [qfT[p][:, h * GT + cl * 128: h * GT + cl * 128 + 128] for p in range(2)]
                SC("tensor", lambda e, pa=pa, a=ks[0], b=qs_[0]:
                   e.matmul(pa[:], a, b, start=True, stop=False),
                   reads=[kfT[0], qfT[0]], writes=[pa])
                SC("tensor", lambda e, pa=pa, a=ks[0], b=qs_[1]:
                   e.matmul(pa[:], a, b, start=False, stop=False),
                   reads=[kfT[0], qfT[1]], writes=[pa])
                SC("tensor", lambda e, pa=pa, a=ks[1], b=qs_[0]:
                   e.matmul(pa[:], a, b, start=False, stop=True),
                   reads=[kfT[1], qfT[0]], writes=[pa])
                SC("vector", lambda e, pa=pa, a32=a32:
                   e.tensor_tensor(out=a32[:], in0=pa[:], in1=mask_sb[:], op=ALU.mult),
                   reads=[pa, mask_sb], writes=[a32])
                SC("vector", lambda e, a32=a32, amt=amt: e.tensor_copy(out=amt[0][:], in_=a32[:]),
                   reads=[a32], writes=[amt[0]])
                SC("vector", lambda e, a32=a32, amt=amt:
                   e.tensor_tensor(out=ep[4][:, 0:128], in0=a32[:], in1=amt[0][:],
                                   op=ALU.subtract),
                   reads=[a32, amt[0]], writes=[ep[4]])
                SC("vector", lambda e, amt=amt: e.tensor_copy(out=amt[1][:], in_=ep[4][:, 0:128]),
                   reads=[ep[4]], writes=[amt[1]])
                vslot = (cl * NKV + g) * 132
                if c > 0:
                    SC("tensor", lambda e, nb=nb, q=qs_[0], g=g:
                       e.matmul(nb[:, 0:129], q, sth[g][:, 0:129], start=True, stop=False),
                       reads=[qfT[0], sth[g]], writes=[nb])
                    SC("tensor", lambda e, nb=nb, q=qs_[0], g=g:
                       e.matmul(nb[:, 0:129], q, stl[g][:, 0:129], start=False, stop=False),
                       reads=[qfT[0], stl[g]], writes=[nb])
                    SC("tensor", lambda e, nb=nb, q=qs_[1], g=g:
                       e.matmul(nb[:, 0:129], q, sth[g][:, 0:129], start=False, stop=False),
                       reads=[qfT[1], sth[g]], writes=[nb])
                    fi = False
                else:
                    fi = True
                SC("tensor", lambda e, nb=nb, amt=amt, vs=vslot, f=fi:
                   e.matmul(nb[:, 0:129], amt[0][:], vaug[0][:, vs:vs + 129], start=f, stop=False),
                   reads=[amt[0], vaug[0]], writes=[nb])
                SC("tensor", lambda e, nb=nb, amt=amt, vs=vslot:
                   e.matmul(nb[:, 0:129], amt[0][:], vaug[1][:, vs:vs + 129], start=False,
                            stop=False),
                   reads=[amt[0], vaug[1]], writes=[nb])
                SC("tensor", lambda e, nb=nb, amt=amt, vs=vslot:
                   e.matmul(nb[:, 0:129], amt[1][:], vaug[0][:, vs:vs + 129], start=False,
                            stop=True),
                   reads=[amt[1], vaug[0]], writes=[nb])
                st_d = stat[0] if h % 2 == 0 else stat[2]
                st_r = stat[1] if h % 2 == 0 else stat[3]
                SC("vector", lambda e, nb=nb, sd=st_d:
                   e.tensor_scalar(out=sd[:], in0=nb[:, 128:129],
                                   scalar1=ATT_EPS, scalar2=None, op0=ALU.add),
                   reads=[nb], writes=[st_d])
                SC("vector", lambda e, sd=st_d, sr=st_r: e.reciprocal(out=sr[:], in_=sd[:]),
                   reads=[st_d], writes=[st_r])
                SC("vector", lambda e, nb=nb, sr=st_r, h=h:
                   e.tensor_scalar(out=att_t[:, h * 128:(h + 1) * 128], in0=nb[:, 0:128],
                                   scalar1=sr[:], scalar2=None, op0=ALU.mult),
                   reads=[nb, st_r], writes=[att_t])
            for g in range(NKV):
                vslot = (cl * NKV + g) * 132
                kt = [kftok[p][:, cl * 512 + g * 128: cl * 512 + g * 128 + 128] for p in range(2)]
                db = ps_acc[0]
                SC("tensor", lambda e, k=kt[0], vs=vslot:
                   e.matmul(db[:, 0:129], k, vaug[0][:, vs:vs + 129], start=True, stop=False),
                   reads=[kftok[0], vaug[0]], writes=[db])
                SC("tensor", lambda e, k=kt[0], vs=vslot:
                   e.matmul(db[:, 0:129], k, vaug[1][:, vs:vs + 129], start=False, stop=False),
                   reads=[kftok[0], vaug[1]], writes=[db])
                SC("tensor", lambda e, k=kt[1], vs=vslot:
                   e.matmul(db[:, 0:129], k, vaug[0][:, vs:vs + 129], start=False, stop=True),
                   reads=[kftok[1], vaug[0]], writes=[db])
                SC("vector", lambda e, g=g:
                   e.tensor_tensor(out=st32[g][:, 0:129], in0=st32[g][:, 0:129],
                                   in1=db[:, 0:129], op=ALU.add),
                   reads=[st32[g], db], writes=[st32[g]])
                SC("vector", lambda e, g=g: e.tensor_copy(out=sth[g][:, 0:129],
                                                          in_=st32[g][:, 0:129]),
                   reads=[st32[g]], writes=[sth[g]])
                SC("vector", lambda e, g=g:
                   e.tensor_tensor(out=ep[2][:, 0:129], in0=st32[g][:, 0:129],
                                   in1=sth[g][:, 0:129], op=ALU.subtract),
                   reads=[st32[g], sth[g]], writes=[ep[2]])
                SC("vector", lambda e, g=g: e.tensor_copy(out=stl[g][:, 0:129],
                                                          in_=ep[2][:, 0:129]),
                   reads=[ep[2]], writes=[stl[g]])
            if DEBUG_SINK:
                DEBUG_SINK(f"att{c}", nc, SC, dict(att_t=att_t, st32=st32))
            SC("vector", lambda e: e.tensor_copy(out=hsp_bf[:], in_=att_t[:]),
               reads=[att_t], writes=[hsp_bf])
            SC("vector", lambda e: e.tensor_tensor(out=p1[:], in0=att_t[:], in1=hsp_bf[:],
                                                   op=ALU.subtract),
               reads=[att_t, hsp_bf], writes=[p1])
            SC("vector", lambda e: e.tensor_copy(out=lsp_bf[:], in_=p1[:]),
               reads=[p1], writes=[lsp_bf])
            for (src, dstT) in ((hsp_bf, attT[0]), (lsp_bf, attT[1])):
                for kc in range(16):
                    bank = ps_tr[kc % 2]
                    SC("tensor", lambda e, b=bank, s=src, kc=kc:
                       e.transpose(b[:], s[:, kc * 128:(kc + 1) * 128], idbf_sb[:]),
                       reads=[src, idbf_sb], writes=[bank])
                    SC("scalar", lambda e, b=bank, d=dstT, kc=kc, cl=cl:
                       e.copy(out=d[:, kc * GT + cl * 128: kc * GT + cl * 128 + 128], in_=b[:]),
                       reads=[bank], writes=[dstT])

        for ob in range(4):
            for kph in range(4):
                for j in range(4):
                    kc = kph * 4 + j
                    SC("sync", lambda e, w=wth[kc % 8], ob=ob, kc=kc:
                       e.dma_start_transpose(
                           w[:], ws_o[ob].rearrange("a b c -> (a b) c")[
                               :, kc * 128:(kc + 1) * 128]),
                       reads=[ws_o], writes=[wth[kc % 8]], dma="ld", tag="oproj_wld")
                for tb in range(GRP):
                    bank = ps_acc[tb]
                    for j in range(4):
                        kc = kph * 4 + j
                        ah_t = attT[0][:, kc * GT + tb * 128: kc * GT + tb * 128 + 128]
                        al_t = attT[1][:, kc * GT + tb * 128: kc * GT + tb * 128 + 128]
                        SC("tensor", lambda e, b=bank, w=wth[kc % 8], xt=ah_t,
                           f=(kph == 0 and j == 0):
                           e.matmul(b[:], xt, w[:, 0:512], start=f, stop=False),
                           reads=[attT[0], wth[kc % 8]], writes=[bank], tag="oproj_mm")
                        SC("tensor", lambda e, b=bank, w=wth[kc % 8], xt=ah_t:
                           e.matmul(b[:], xt, w[:, 512:1024], start=False, stop=False),
                           reads=[attT[0], wth[kc % 8]], writes=[bank])
                        SC("tensor", lambda e, b=bank, w=wth[kc % 8], xt=al_t,
                           l=(kph == 3 and j == 3):
                           e.matmul(b[:], xt, w[:, 0:512], start=False, stop=l),
                           reads=[attT[1], wth[kc % 8]], writes=[bank])
            for tb in range(GRP):
                bank = ps_acc[tb]
                SC("vector", lambda e, b=bank: e.tensor_copy(out=ep[0][:], in_=b[:]),
                   reads=[bank], writes=[ep[0]])
                SC("scalar", lambda e, ob=ob, tb=tb, grp=grp:
                   e.dma_start(out=att_h[grp * GT + tb * 128: grp * GT + tb * 128 + 128,
                                         ob * 512:(ob + 1) * 512], in_=ep[0][:]),
                   reads=[ep[0]], writes=[att_h], dma="st", tag="att_spill")

    for c in range(NCH):
        SC("scalar", lambda e, c=c: e.dma_start(out=rowscr_h[0, c * 128:(c + 1) * 128],
                                                in_=sxinv_col[:, c:c + 1]),
           reads=[sxinv_col], writes=[rowscr_h], dma="st")
    for bb in range(4):
        SC("sync", lambda e, bb=bb: e.dma_start(out=row0[:],
                                                in_=rowscr_h[0:1, bb * 512:(bb + 1) * 512]),
           reads=[rowscr_h], writes=[row0], dma="ld")
        SC("tensor", lambda e, bb=bb:
           e.matmul(ps_acc[3][:], ones_1x128_f[:], row0[0:1, 0:512],
                    start=True, stop=True),
           reads=[ones_1x128_f, row0], writes=[ps_acc[3]])
        SC("vector", lambda e, bb=bb: e.tensor_copy(out=Bsx[:, bb * 512:(bb + 1) * 512],
                                                    in_=ps_acc[3][:]),
           reads=[ps_acc[3]], writes=[Bsx])

    def emit_gu_epilogue(ic, ib, tb, bg, bu):
        ts0 = tb * 512
        ig = ic * 2 + ib
        SC("vector", lambda e, b=bg, ts0=ts0:
           e.scalar_tensor_tensor(out=ep[0][:], in0=b[:], scalar=thr["g"]["abc"][:],
                          in1=Bsx[:, ts0:ts0 + 512], op0=ALU.mult, op1=ALU.mult),
           reads=[bg, thr["g"]["abc"], Bsx], writes=[ep[0]])
        SC("scalar", lambda e: e.activation(out=ep[1][:], in_=ep[0][:], func=AF.Silu),
           reads=[ep[0]], writes=[ep[1]])
        SC("vector", lambda e, b=bu, ts0=ts0:
           e.scalar_tensor_tensor(out=ep[0][:], in0=b[:], scalar=thr["u"]["abc"][:],
                          in1=Bsx[:, ts0:ts0 + 512], op0=ALU.mult, op1=ALU.mult),
           reads=[bu, thr["u"]["abc"], Bsx], writes=[ep[0]])
        SC("vector", lambda e: e.tensor_tensor(out=ep[2][:], in0=ep[1][:], in1=ep[0][:],
                               op=ALU.mult),
           reads=[ep[1], ep[0]], writes=[ep[2]])
        if DEBUG_SINK and ic == 0 and ib == 0 and tb == 0:
            DEBUG_SINK("ffn0", nc, SC, dict(gt_silu=ep[1], ut=ep[0], Bsx=Bsx))
        SC("scalar", lambda e, ig=ig, ts0=ts0:
           e.dma_start(out=pt_h[ig * 128:(ig + 1) * 128, ts0:ts0 + 512], in_=ep[2][:]),
           reads=[ep[2]], writes=[pt_h], dma="st", tag="p_spill")

    def gu_tile(kc):
        return wth[kc // 2][:, (kc % 2) * 512:(kc % 2) * 512 + 512]

    for ic in range(32):
        for kc in range(16):
            SC("sync", lambda e, kc=kc, ic=ic:
               e.dma_start_transpose(
                   gu_tile(kc), ws_gu[ic].rearrange("a b c -> (a b) c")[
                       :, kc * 128:(kc + 1) * 128]),
               reads=[ws_gu], writes=[wth[kc // 2]], dma="ld", tag="wg_ld")
        for ib in range(2):
            ig = ic * 2 + ib
            for tb in range(4):
                par = (ib * 4 + tb) % 2
                bg, bu = ps_acc[0 + 2 * par], ps_acc[1 + 2 * par]
                for kc in range(16):
                    SC("tensor", lambda e, b=bg, kc=kc, ib=ib, tb=tb,
                       f=(kc == 0), l=(kc == 15):
                       e.matmul(b[:], gu_tile(kc)[:, ib * 128:(ib + 1) * 128],
                                nxT[:, kc * S + tb * 512: kc * S + (tb + 1) * 512],
                                start=f, stop=l),
                       reads=[wth[kc // 2], nxT], writes=[bg], tag="gmm")
                for kc in range(16):
                    SC("tensor", lambda e, b=bu, kc=kc, ib=ib, tb=tb,
                       f=(kc == 0), l=(kc == 15):
                       e.matmul(b[:], gu_tile(kc)[:, 256 + ib * 128: 256 + (ib + 1) * 128],
                                nxT[:, kc * S + tb * 512: kc * S + (tb + 1) * 512],
                                start=f, stop=l),
                       reads=[wth[kc // 2], nxT], writes=[bu], tag="umm")
                emit_gu_epilogue(ic, ib, tb, bg, bu)

    def emit_alpha(tsl):
        ts0 = tsl * 512
        SC("gpsimd", lambda e: e.memset(ssqsl[:], 0.0), writes=[ssqsl])
        SC("gpsimd", lambda e: e.memset(maxsl[:], 0.0), writes=[maxsl])
        for ig in range(64):
            pl = epp[ig % 2]
            sq = ep[0] if ig % 2 == 0 else ep[4]
            SC("sync", lambda e, b=pl, ig=ig, ts0=ts0:
               e.dma_start(out=b[:], in_=pt_h[ig * 128:(ig + 1) * 128, ts0:ts0 + 512]),
               reads=[pt_h], writes=[pl], dma="ld", tag="p_ld1")
            SC("scalar", lambda e, b=pl, o=sq: e.activation(out=o[:], in_=b[:], func=AF.Square),
               reads=[pl], writes=[sq])
            SC("vector", lambda e, o=sq: e.tensor_tensor(out=ssqsl[:], in0=ssqsl[:], in1=o[:],
                                                         op=ALU.add),
               reads=[ssqsl, sq], writes=[ssqsl])
            SC("scalar", lambda e, b=pl, o=sq: e.activation(out=o[:], in_=b[:], func=AF.Abs),
               reads=[pl], writes=[sq])
            SC("vector", lambda e, o=sq, ig=ig:
               e.scalar_tensor_tensor(out=maxsl[:], in0=o[:], scalar=alnw_sb[:, ig:ig + 1],
                                      in1=maxsl[:], op0=ALU.mult, op1=ALU.max),
               reads=[sq, alnw_sb, maxsl], writes=[maxsl])

    def emit_stats(tsl):
        ts0 = tsl * 512
        for j in range(4):
            c = tsl * 4 + j
            SC("tensor", lambda e, j=j: e.transpose(ps_af[:], ssqsl[:, j * 128:(j + 1) * 128],
                                                    idf32_sb[:]),
               reads=[ssqsl, idf32_sb], writes=[ps_af])
            SC("vector", lambda e: e.tensor_copy(out=ep[0][:, 0:128], in_=ps_af[:]),
               reads=[ps_af], writes=[ep[0]])
            SC("vector", lambda e: e.tensor_reduce(out=stat[0][:], in_=ep[0][:, 0:128],
                                                   op=ALU.add, axis=AX.X),
               reads=[ep[0]], writes=[stat[0]])
            SC("tensor", lambda e, j=j: e.transpose(ps_af[:], maxsl[:, j * 128:(j + 1) * 128],
                                                    idf32_sb[:]),
               reads=[maxsl, idf32_sb], writes=[ps_af])
            SC("vector", lambda e: e.tensor_copy(out=ep[0][:, 0:128], in_=ps_af[:]),
               reads=[ps_af], writes=[ep[0]])
            SC("vector", lambda e: e.tensor_reduce(out=stat[1][:], in_=ep[0][:, 0:128],
                                                   op=ALU.max, axis=AX.X),
               reads=[ep[0]], writes=[stat[1]])
            SC("vector", lambda e: e.tensor_scalar(out=stat[2][:], in0=stat[0][:],
                                                   scalar1=1.0 / II, scalar2=RMS_EPS,
                                                   op0=ALU.mult, op1=ALU.add),
               reads=[stat[0]], writes=[stat[2]])
            SC("scalar", lambda e: e.activation(out=stat[3][:], in_=stat[2][:], func=AF.Sqrt),
               reads=[stat[2]], writes=[stat[3]])
            SC("vector", lambda e: e.reciprocal(out=stat[4][:], in_=stat[3][:]),
               reads=[stat[3]], writes=[stat[4]])
            SC("vector", lambda e: e.tensor_tensor(out=stat[3][:], in0=stat[4][:], in1=stat[4][:],
                                                   op=ALU.mult),
               reads=[stat[4]], writes=[stat[3]])
            SC("vector", lambda e: e.tensor_tensor(out=stat[3][:], in0=stat[3][:], in1=stat[2][:],
                                                   op=ALU.mult),
               reads=[stat[3], stat[2]], writes=[stat[3]])
            SC("vector", lambda e: e.tensor_scalar(out=stat[3][:], in0=stat[3][:], scalar1=-0.5,
                                                   scalar2=1.5, op0=ALU.mult, op1=ALU.add),
               reads=[stat[3]], writes=[stat[3]])
            SC("vector", lambda e: e.tensor_tensor(out=stat[3][:], in0=stat[3][:], in1=stat[4][:],
                                                   op=ALU.mult),
               reads=[stat[3], stat[4]], writes=[stat[3]])
            SC("vector", lambda e: e.tensor_tensor(out=stat[1][:], in0=stat[1][:], in1=stat[3][:],
                                                   op=ALU.mult),
               reads=[stat[1], stat[3]], writes=[stat[1]])
            SC("vector", lambda e: e.tensor_scalar(out=stat[1][:], in0=stat[1][:], scalar1=Q_EPS,
                                                   scalar2=None, op0=ALU.max),
               reads=[stat[1]], writes=[stat[1]])
            SC("vector", lambda e: e.reciprocal(out=stat[4][:], in_=stat[1][:]),
               reads=[stat[1]], writes=[stat[4]])
            SC("vector", lambda e: e.tensor_scalar(out=stat[4][:], in0=stat[4][:], scalar1=127.0,
                                                   scalar2=None, op0=ALU.mult),
               reads=[stat[4]], writes=[stat[4]])
            SC("vector", lambda e: e.tensor_tensor(out=stat[5][:], in0=stat[3][:], in1=stat[4][:],
                                                   op=ALU.mult),
               reads=[stat[3], stat[4]], writes=[stat[5]])
            SC("scalar", lambda e, c=c: e.dma_start(out=rowscr_h[3, c * 128:(c + 1) * 128],
                                                    in_=stat[5][:]),
               reads=[stat[5]], writes=[rowscr_h], dma="st")
            SC("vector", lambda e: e.tensor_scalar(out=stat[6][:], in0=stat[1][:],
                                                   scalar1=1.0 / 127.0, scalar2=None,
                                                   op0=ALU.mult),
               reads=[stat[1]], writes=[stat[6]])
            SC("vector", lambda e, c=c:
               e.tensor_tensor(out=sd_col[:, c:c + 1], in0=stat[6][:], in1=thr["d"]["abc"][:],
                               op=ALU.mult),
               reads=[stat[6], thr["d"]["abc"]], writes=[sd_col])
        SC("sync", lambda e, ts0=ts0: e.dma_start(out=row0[0:1, 0:512],
                                                  in_=rowscr_h[3, ts0:ts0 + 512]),
           reads=[rowscr_h], writes=[row0], dma="ld")
        SC("tensor", lambda e:
           e.matmul(ps_acc[3][:], ones_1x128_f[:], row0[0:1, 0:512], start=True, stop=True),
           reads=[ones_1x128_f, row0], writes=[ps_acc[3]])
        chs = ssqsl
        SC("vector", lambda e: e.tensor_copy(out=chs[:], in_=ps_acc[3][:]),
           reads=[ps_acc[3]], writes=[chs])

    def emit_beta(tsl):
        ts0 = tsl * 512
        chs = ssqsl
        for ig in range(64):
            pl = epp[ig % 2]
            yb = ep[0] if ig % 2 == 0 else ep[4]
            SC("sync", lambda e, b=pl, ig=ig, ts0=ts0:
               e.dma_start(out=b[:], in_=pt_h[ig * 128:(ig + 1) * 128, ts0:ts0 + 512]),
               reads=[pt_h], writes=[pl], dma="ld", tag="p_ld2")
            SC("vector", lambda e, b=pl, o=yb, ig=ig:
               e.scalar_tensor_tensor(out=o[:], in0=b[:], scalar=lnw_sb[:, ig:ig + 1],
                                      in1=chs[:], op0=ALU.mult, op1=ALU.mult),
               reads=[pl, lnw_sb, chs], writes=[yb])
            SC("vector", lambda e, o=yb: e.tensor_scalar(out=ep32[:], in0=o[:], scalar1=MAGIC,
                                                         scalar2=MAGIC, op0=ALU.add,
                                                         op1=ALU.subtract),
               reads=[yb], writes=[ep32])
            SC("vector", lambda e, ig=ig: e.tensor_copy(out=nhT[:, ig * 512:(ig + 1) * 512],
                                                        in_=ep32[:]),
               reads=[ep32], writes=[nhT], tag="nh_cast")

    def emit_down(tsl):
        ts0 = tsl * 512
        for ob in range(4):
            for kc in range(64):
                wt_ = wth[kc % 8]
                q, qe = "ld", "sync"
                SC(qe, lambda e, w=wt_, ob=ob, kc=kc:
                   e.dma_start_transpose(w[:, 0:512], ws_d[ob * 512:(ob + 1) * 512,
                                                          kc * 128:(kc + 1) * 128]),
                   reads=[ws_d], writes=[wt_], dma=q, tag="wd_ld")
                for tb in range(4):
                    SC("tensor", lambda e, w=wt_, kc=kc, tb=tb, b=ps_acc[tb],
                       f=(kc == 0), l=(kc == 63):
                       e.matmul(b[:], nhT[:, kc * 512 + tb * 128: kc * 512 + tb * 128 + 128],
                                w[:, 0:512], start=f, stop=l),
                       reads=[nhT, wt_], writes=[ps_acc[tb]], tag="dmm")
            for tb in range(4):
                bank = ps_acc[tb]
                c = tsl * 4 + tb
                SC("sync", lambda e, c=c, ob=ob:
                   e.dma_start(out=ep[1][:], in_=x_h[c * 128:(c + 1) * 128,
                                                     ob * 512:(ob + 1) * 512]),
                   writes=[ep[1]], dma="ld", tag="x_reload")
                SC("sync", lambda e, c=c, ob=ob:
                   e.dma_start(out=ep[2][:], in_=att_h[c * 128:(c + 1) * 128,
                                                       ob * 512:(ob + 1) * 512]),
                   reads=[att_h], writes=[ep[2]], dma="ld", tag="att_reload")
                SC("vector", lambda e, b=bank, c=c:
                   e.tensor_scalar(out=ep[3][:], in0=b[:], scalar1=sd_col[:, c:c + 1],
                                   scalar2=None, op0=ALU.mult),
                   reads=[bank, sd_col], writes=[ep[3]])
                SC("vector", lambda e: e.tensor_tensor(out=ep[2][:], in0=ep[1][:], in1=ep[2][:],
                                                       op=ALU.add),
                   reads=[ep[1], ep[2]], writes=[ep[2]])
                SC("vector", lambda e: e.tensor_tensor(out=ep[2][:], in0=ep[3][:], in1=ep[2][:],
                                                       op=ALU.add),
                   reads=[ep[3], ep[2]], writes=[ep[2]])
                SC("scalar", lambda e, c=c, ob=ob:
                   e.dma_start(out=out_h[c * 128:(c + 1) * 128, ob * 512:(ob + 1) * 512],
                               in_=ep[2][:]),
                   reads=[ep[2]], dma="st", tag="out_st")


    for tsl in range(4):
        emit_alpha(tsl)
        emit_stats(tsl)
        if tsl > 0:
            emit_down(tsl - 1)
        emit_beta(tsl)
    emit_down(3)

    _probe = []
    import os as _os
    _pk = int(_os.environ.get("SBUF_PROBE_KB", "0"))
    if _pk:
        _probe.append(sb("probe", [128, _pk * 256], F32))
    last_store = sch.steps[-1]
    term = Step("gpsimd", lambda e: e.engine_nop(), [], [], None, len(sch.steps), "term")
    term.waits.append(last_store)
    sch.steps.append(term)

    sch.resolve_and_emit(nc)
    stack.close()
    return nc


def _host_constants():
    j = np.arange(0, HD, 2, dtype=np.float64)
    inv_freq = 1.0 / (10000.0 ** (j / HD))
    t = np.arange(S, dtype=np.float64)
    freqs = np.outer(t, inv_freq)
    emb = np.concatenate([freqs, freqs], axis=-1)
    cos = np.cos(emb).astype(np.float32)
    sin = np.sin(emb).astype(np.float32)
    ii = np.arange(128)
    mask = (ii[:, None] <= ii[None, :]).astype(ml_dtypes.bfloat16)
    idbf = np.eye(128, dtype=ml_dtypes.bfloat16)
    idf32 = np.eye(128, dtype=np.float32)
    return cos, sin, mask, idbf, idf32


_NC_CACHE = {}


def kernel(L0_attn_x, L0_attn_Wq, L0_attn_Wk, L0_attn_Wv, L0_attn_Wo,
           L0_ffn_w_gate, L0_ffn_w_up, L0_ffn_w_down, L0_ffn_ln_weight):
    x = np.ascontiguousarray(np.asarray(L0_attn_x, dtype=np.float32))
    B = x.shape[0]
    cos, sin, mask, idbf, idf32 = _host_constants()
    common = {
        "wq": np.ascontiguousarray(np.asarray(L0_attn_Wq, np.float32)),
        "wk": np.ascontiguousarray(np.asarray(L0_attn_Wk, np.float32)),
        "wv": np.ascontiguousarray(np.asarray(L0_attn_Wv, np.float32)),
        "wo": np.ascontiguousarray(np.asarray(L0_attn_Wo, np.float32)),
        "wg": np.ascontiguousarray(np.asarray(L0_ffn_w_gate, np.float32)),
        "wu": np.ascontiguousarray(np.asarray(L0_ffn_w_up, np.float32)),
        "wd": np.ascontiguousarray(np.asarray(L0_ffn_w_down, np.float32)),
        "lnw": np.ascontiguousarray(np.asarray(L0_ffn_ln_weight, np.float32)),
        "costab": cos, "sintab": sin, "masktab": mask,
        "idbf": idbf, "idf32": idf32,
    }
    if "nc" not in _NC_CACHE:
        _NC_CACHE["nc"] = build_program()
    nc = _NC_CACHE["nc"]
    in_maps = [dict(common, x=np.ascontiguousarray(x[b])) for b in range(B)]
    res = run_bass_kernel_spmd(nc, in_maps, core_ids=list(range(B)))
    return np.stack([res.results[b]["out"] for b in range(B)], axis=0)


if __name__ == "__main__":
    import time
    t0 = time.time()
    nc = build_program()
    print(f"built ok in {time.time() - t0:.1f}s, "
          f"instructions={sum(1 for _ in nc.all_instructions())}")
